# version 09
# baseline (speedup 1.0000x reference)
import contextlib
import numpy as np

import concourse.bass as bass
import concourse.mybir as mybir
from concourse.bass_utils import run_bass_kernel_spmd

F32 = mybir.dt.float32
BF = mybir.dt.bfloat16
AF = mybir.ActivationFunctionType
ALU = mybir.AluOpType

NCORES = 8
DM = 2048
LTOK = 2048
FFNH = 8192
INTER = 4096
HEADS = 64
PD = 64
PROJ = 8384
CONVD = 4224
LN_EPS = 1e-5
RMS_EPS = 1e-6

TFF = 512
NFB = LTOK // TFF
GH = 256
NGH = FFNH // GH
TM = 256
NMT = LTOK // TM
QS = 128
NQ = TM // QS

ENGS = ("sync", "scalar", "tensor", "vector", "gpsimd")


NDMASEM = 4


class _Op:
    __slots__ = ("eng", "fn", "dma", "deps", "sem_val", "need_inc", "idx", "slot")

    def __init__(self, eng, fn, dma):
        self.eng = eng
        self.fn = fn
        self.dma = dma
        self.deps = []
        self.sem_val = None
        self.need_inc = dma
        self.idx = -1
        self.slot = 0


class Sched:

    def __init__(self, nc, stack, tag=""):
        self.nc = nc
        self.ops = []
        self.last_writer = {}
        self.readers = {}
        self.guards = []
        S = stack.enter_context
        self.eng_sem = {e: S(nc.semaphore(f"{tag}e_{e}")) for e in ENGS}
        self.dma_sem = {e: [S(nc.semaphore(f"{tag}d_{e}{i}")) for i in range(NDMASEM)]
                        for e in ("sync", "scalar", "gpsimd")}
        self.dma_rr = {e: 0 for e in self.dma_sem}

    def op(self, eng, fn, reads=(), writes=(), dma=False, after=()):
        o = _Op(eng, fn, dma)
        o.idx = len(self.ops)
        if dma:
            o.slot = self.dma_rr[eng] % NDMASEM
            self.dma_rr[eng] += 1
        deps = set(self.guards)
        for r in reads:
            w = self.last_writer.get(r)
            if w is not None:
                deps.add(w)
        for w in writes:
            lw = self.last_writer.get(w)
            if lw is not None:
                deps.add(lw)
            deps.update(self.readers.get(w, ()))
        for d in after:
            if d is not None:
                deps.add(d)
        deps.discard(o)
        o.deps = sorted(deps, key=lambda d: d.idx)
        self.ops.append(o)
        for r in reads:
            self.readers.setdefault(r, []).append(o)
        for w in writes:
            self.last_writer[w] = o
            self.readers[w] = []
        return o

    def barrier(self):
        last = {}
        for o in self.ops:
            last[(o.eng, o.dma)] = o
        g = self.op("sync", lambda e: e.nop(), after=list(last.values()))
        self.guards = [g]
        return g

    def emit(self, final_engine="sync"):
        for o in self.ops:
            for d in o.deps:
                d.need_inc = True
        last_per_eng = {}
        for o in self.ops:
            last_per_eng[o.eng] = o
        for o in last_per_eng.values():
            o.need_inc = True
        ctr = {e: 0 for e in ENGS}
        dctr = {(e, i): 0 for e in self.dma_sem for i in range(NDMASEM)}
        for o in self.ops:
            if o.dma:
                dctr[(o.eng, o.slot)] += 16
                o.sem_val = dctr[(o.eng, o.slot)]
            elif o.need_inc:
                ctr[o.eng] += 1
                o.sem_val = ctr[o.eng]
        for e, c in ctr.items():
            if c >= 60000:
                raise RuntimeError(f"semaphore range risk on {e}: {c}")
        for k, c in dctr.items():
            if c >= 60000:
                raise RuntimeError(f"semaphore range risk on {k}: {c}")
        finals = []
        for (e, i), c in dctr.items():
            if c:
                finals.append((self.dma_sem[e][i], c))
        for e, c in ctr.items():
            if c and e != final_engine:
                finals.append((self.eng_sem[e], c))
        nc = self.nc
        with nc.Block() as block:
            for eng_name in ENGS:
                items = [o for o in self.ops if o.eng == eng_name]
                if not items and eng_name != final_engine:
                    continue

                def make(items, eng_name=eng_name):
                    def f(eng):
                        hw = {}
                        for o in items:
                            for d in o.deps:
                                if d.dma:
                                    sem = self.dma_sem[d.eng][d.slot]
                                else:
                                    sem = self.eng_sem[d.eng]
                                key = id(sem)
                                if hw.get(key, 0) < d.sem_val:
                                    eng.wait_ge(sem, d.sem_val)
                                    hw[key] = d.sem_val
                            if o.dma and o.sem_val > 16:
                                sem = self.dma_sem[o.eng][o.slot]
                                key = id(sem)
                                if hw.get(key, 0) < o.sem_val - 16:
                                    eng.wait_ge(sem, o.sem_val - 16)
                                    hw[key] = o.sem_val - 16
                            ins = o.fn(eng)
                            if o.need_inc:
                                if o.dma:
                                    ins.then_inc(self.dma_sem[o.eng][o.slot], 16)
                                else:
                                    ins.then_inc(self.eng_sem[o.eng], 1)
                        if eng_name == final_engine:
                            for sem, v in finals:
                                eng.wait_ge(sem, v)
                    return f

                getattr(block, eng_name)(make(items))
        return ({e: sum(1 for o in self.ops if o.eng == e) for e in ENGS},
                ctr, dctr)


_uid = [0]


def _nm(b):
    _uid[0] += 1
    return f"{b}_{_uid[0]}"


def build_nc(debug_mid=False, phases=("ffn1", "mamba", "ffn2")):
    nc = bass.Bass("TRN2", target_bir_lowering=False)

    xT = nc.dram_tensor("xT", [DM, LTOK], F32, kind="ExternalInput")
    wgT = nc.dram_tensor("wgT", [DM, FFNH], F32, kind="ExternalInput")
    wuT = nc.dram_tensor("wuT", [DM, FFNH], F32, kind="ExternalInput")
    wdT = nc.dram_tensor("wdT", [FFNH, DM], F32, kind="ExternalInput")
    winT = nc.dram_tensor("winT", [DM, PROJ], F32, kind="ExternalInput")
    woT = nc.dram_tensor("woT", [INTER, DM], F32, kind="ExternalInput")
    cw = nc.dram_tensor("cw", [CONVD, 4], F32, kind="ExternalInput")
    cb = nc.dram_tensor("cb", [CONVD], F32, kind="ExternalInput")
    dtb = nc.dram_tensor("dtb", [HEADS], F32, kind="ExternalInput")
    alog = nc.dram_tensor("alog", [HEADS], F32, kind="ExternalInput")
    dvec = nc.dram_tensor("dvec", [HEADS], F32, kind="ExternalInput")
    nw = nc.dram_tensor("nw", [INTER], F32, kind="ExternalInput")
    outT = nc.dram_tensor("outT", [DM, LTOK], F32, kind="ExternalOutput")

    mid = "ExternalOutput" if debug_mid else "Internal"
    gsc = nc.dram_tensor("gsc", [128, NGH * 16 * GH], BF, kind="Internal")
    usc = nc.dram_tensor("usc", [128, NGH * 16 * GH], BF, kind="Internal")
    dsc = nc.dram_tensor("dsc", [128, 16 * 8192], BF, kind="Internal")
    isc = nc.dram_tensor("isc", [128, 32 * 4096 + 3072], BF, kind="Internal")
    osc = nc.dram_tensor("osc", [128, 8 * 8192], BF, kind="Internal")
    X1 = nc.dram_tensor("X1", [DM, LTOK], F32, kind=mid)
    X2 = nc.dram_tensor("X2", [DM, LTOK], F32, kind=mid)

    CONSTS = [LN_EPS, RMS_EPS, -0.5, -1.0, 1.0, 0.5, 1.0 / DM, 1.0 / INTER]

    es = contextlib.ExitStack()
    with es:
        S = es.enter_context
        sch = Sched(nc, es)

        cst = S(nc.sbuf_tensor("cst", [128, len(CONSTS)], F32))
        for i, cv in enumerate(CONSTS):
            nc.const_aps.register(F32, float(cv), cst[:, i:i + 1])

        ones_col_f = S(nc.sbuf_tensor("ones_col_f", [128, 1], F32))
        ones_col_b = S(nc.sbuf_tensor("ones_col_b", [128, 1], BF))
        ones_row_f = S(nc.sbuf_tensor("ones_row_f", [1, 128], F32))
        I_f = S(nc.sbuf_tensor("I_f", [128, 128], F32))
        I_b = S(nc.sbuf_tensor("I_b", [128, 128], BF))
        LT = S(nc.sbuf_tensor("LT", [128, 128], F32))
        US_b = S(nc.sbuf_tensor("US_b", [128, 128], BF))
        p_cw = S(nc.sbuf_tensor("p_cw", [128, 33 * 4], F32))
        p_cb = S(nc.sbuf_tensor("p_cb", [128, 33], F32))
        p_nw = S(nc.sbuf_tensor("p_nw", [128, 32], F32))
        p_Abc = S(nc.sbuf_tensor("p_Abc", [128, 64], F32))
        p_dtbbc = S(nc.sbuf_tensor("p_dtbbc", [128, 64], F32))
        p_Dbc = S(nc.sbuf_tensor("p_Dbc", [128, 64], F32))
        prow = S(nc.sbuf_tensor("prow", [1, 192], F32))
        arow = S(nc.sbuf_tensor("arow", [1, 192], F32))
        pstage = S(nc.sbuf_tensor("pstage", [33, 128], F32))
        nstage = S(nc.sbuf_tensor("nstage", [32, 128], F32))

        PS = [S(nc.psum_tensor(f"ps{k}", [128, 512], F32)) for k in range(8)]
        PSB = [p.ap().bitcast(BF) for p in PS]
        psk = [f"ps{k}" for k in range(8)]

        def set_consts(e):
            ins = None
            for i, cv in enumerate(CONSTS):
                ins = e.memset(cst[:, i:i + 1], float(cv))
            return ins
        sch.op("gpsimd", set_consts, writes=["cst"])

        def g_masks(e):
            e.memset(ones_col_f[:], 1.0)
            e.memset(ones_col_b[:], 1.0)
            e.memset(ones_row_f[:], 1.0)
            e.memset(I_f[:], 1.0)
            e.affine_select(out=I_f[:], in_=I_f[:], compare_op=ALU.is_equal,
                            fill=0.0, base=0, pattern=[[1, 128]], channel_multiplier=-1)
            e.memset(I_b[:], 1.0)
            e.affine_select(out=I_b[:], in_=I_b[:], compare_op=ALU.is_equal,
                            fill=0.0, base=0, pattern=[[1, 128]], channel_multiplier=-1)
            e.memset(LT[:], 1.0)
            e.affine_select(out=LT[:], in_=LT[:], compare_op=ALU.is_ge,
                            fill=0.0, base=0, pattern=[[1, 128]], channel_multiplier=-1)
            e.memset(US_b[:], 1.0)
            return e.affine_select(out=US_b[:], in_=US_b[:], compare_op=ALU.is_ge,
                                   fill=0.0, base=-1, pattern=[[-1, 128]],
                                   channel_multiplier=1)
        sch.op("gpsimd", g_masks, writes=["masks"])

        sch.op("sync", lambda e: e.dma_start(
            prow[:, 0:64], alog.ap().rearrange("(o h) -> o h", o=1)),
            writes=["prow"], dma=True)
        sch.op("sync", lambda e: e.dma_start(
            prow[:, 64:128], dtb.ap().rearrange("(o h) -> o h", o=1)),
            writes=["prow"], dma=True)
        sch.op("sync", lambda e: e.dma_start(
            prow[:, 128:192], dvec.ap().rearrange("(o h) -> o h", o=1)),
            writes=["prow"], dma=True)
        sch.op("sync", lambda e: e.dma_start(
            pstage[:], cb.ap().rearrange("(c q) -> c q", q=128)),
            writes=["pstage"], dma=True)
        sch.op("sync", lambda e: e.dma_start(
            nstage[:], nw.ap().rearrange("(c q) -> c q", q=128)),
            writes=["nstage"], dma=True)
        sch.op("sync", lambda e: e.dma_start(
            p_cw.ap().rearrange("p (c k) -> p c k", k=4),
            cw.ap().rearrange("(c p) k -> p c k", p=128)),
            writes=["p_cw"], dma=True)

        sch.op("scalar", lambda e: e.activation(arow[:, 0:64], prow[:, 0:64], AF.Exp),
               reads=["prow"], writes=["arow"])
        sch.op("vector", lambda e: e.tensor_scalar_mul(arow[:, 0:64], arow[:, 0:64], -1.0),
               reads=["cst"], writes=["arow"])
        sch.op("vector", lambda e: e.tensor_copy(arow[:, 64:192], prow[:, 64:192]),
               reads=["prow"], writes=["arow"])

        def bc3(e):
            e.matmul(PS[0][:, 0:64], ones_row_f[:], arow[:, 0:64], start=True, stop=True)
            e.matmul(PS[0][:, 64:128], ones_row_f[:], arow[:, 64:128], start=True, stop=True)
            return e.matmul(PS[0][:, 128:192], ones_row_f[:], arow[:, 128:192],
                            start=True, stop=True)
        sch.op("tensor", bc3, reads=["arow", "masks"], writes=["ps0"])
        sch.op("vector", lambda e: e.tensor_copy(p_Abc[:], PS[0][:, 0:64]),
               reads=["ps0"], writes=["p_Abc"])
        sch.op("vector", lambda e: e.tensor_copy(p_dtbbc[:], PS[0][:, 64:128]),
               reads=["ps0"], writes=["p_dtbbc"])
        sch.op("vector", lambda e: e.tensor_copy(p_Dbc[:], PS[0][:, 128:192]),
               reads=["ps0"], writes=["p_Dbc"])

        def tr2(e):
            e.transpose(PS[1][:, 0:33], pstage[:], I_f[0:33, 0:33])
            return e.transpose(PS[1][:, 64:96], nstage[:], I_f[0:32, 0:32])
        sch.op("tensor", tr2, reads=["pstage", "nstage", "masks"], writes=["ps1"])
        sch.op("vector", lambda e: e.tensor_copy(p_cb[:], PS[1][:, 0:33]),
               reads=["ps1"], writes=["p_cb"])
        sch.op("vector", lambda e: e.tensor_copy(p_nw[:], PS[1][:, 64:96]),
               reads=["ps1"], writes=["p_nw"])

        conv_items = []

        def add_wgu(g):
            conv_items.append((f"gsc:{g}",
                lambda g=g: wgT.ap()[:, g * GH:(g + 1) * GH]
                    .rearrange("(c p) h -> p c h", p=128),
                lambda g=g: gsc.ap()[:, g * 4096:(g + 1) * 4096]
                    .rearrange("p (c h) -> p c h", h=GH)))
            conv_items.append((f"usc:{g}",
                lambda g=g: wuT.ap()[:, g * GH:(g + 1) * GH]
                    .rearrange("(c p) h -> p c h", p=128),
                lambda g=g: usc.ap()[:, g * 4096:(g + 1) * 4096]
                    .rearrange("p (c h) -> p c h", h=GH)))

        for g in range(NGH):
            add_wgu(g)
        for gg in range(16):
            for hh in range(2):
                conv_items.append((f"dsc:{gg}:{hh}",
                    lambda gg=gg, hh=hh: wdT.ap()[gg * 512 + hh * 256:
                                                  gg * 512 + hh * 256 + 256, :]
                        .rearrange("(hl p) d -> p hl d", p=128),
                    lambda gg=gg, hh=hh: dsc.ap()[:, gg * 8192 + hh * 4096:
                                                  gg * 8192 + hh * 4096 + 4096]
                        .rearrange("p (hl d) -> p hl d", d=2048)))
        for G in range(32):
            conv_items.append((f"isc:{G}",
                lambda G=G: winT.ap()[:, G * 256:(G + 1) * 256]
                    .rearrange("(c p) h -> p c h", p=128),
                lambda G=G: isc.ap()[:, G * 4096:(G + 1) * 4096]
                    .rearrange("p (c h) -> p c h", h=256)))
        conv_items.append(("isc:32",
            lambda: winT.ap()[:, 8192:8384].rearrange("(c p) h -> p c h", p=128),
            lambda: isc.ap()[:, 131072:131072 + 3072]
                .rearrange("p (c h) -> p c h", h=192)))
        for og in range(8):
            for ih in range(2):
                conv_items.append((f"osc:{og}:{ih}",
                    lambda og=og, ih=ih: woT.ap()[og * 512 + ih * 256:
                                                  og * 512 + ih * 256 + 256, :]
                        .rearrange("(il p) d -> p il d", p=128),
                    lambda og=og, ih=ih: osc.ap()[:, og * 8192 + ih * 4096:
                                                  og * 8192 + ih * 4096 + 4096]
                        .rearrange("p (il d) -> p il d", d=2048)))

        def emit_conversion():
            for key, src_fn, dst_fn in conv_items:
                sch.op("gpsimd", lambda e, src_fn=src_fn, dst_fn=dst_fn:
                       e.dma_start(dst_fn(), src_fn()),
                       writes=[key], dma=True)

        def emit_ln(T, resid, xln, sqbuf_fn, lnsc, rows, sq_keys,
                    rk="resid", xk="xln"):
            for dc in range(16):
                sch.op("vector", lambda e, dc=dc: e.tensor_copy(
                    xln[:, dc * T:(dc + 1) * T], resid[:, dc * T:(dc + 1) * T]),
                    reads=[rk], writes=[xk])
            for dc in range(16):
                sch.op("scalar", lambda e, dc=dc: e.activation(
                    sqbuf_fn(dc), resid[:, dc * T:(dc + 1) * T], AF.Square),
                    reads=[rk], writes=[sq_keys[dc]])

            def s1(e):
                for dc in range(16):
                    ins = e.matmul(PS[0][0:1, 0:T], ones_col_b[:],
                                   xln[:, dc * T:(dc + 1) * T],
                                   start=(dc == 0), stop=(dc == 15))
                return ins
            sch.op("tensor", s1, reads=[xk, "masks"], writes=["ps0"])

            def s2(e):
                for dc in range(16):
                    ins = e.matmul(PS[1][0:1, 0:T], ones_col_b[:], sqbuf_fn(dc),
                                   start=(dc == 0), stop=(dc == 15))
                return ins
            sch.op("tensor", s2, reads=list(sq_keys) + ["masks"], writes=["ps1"])

            m_r = rows[:, 0:T]
            t_r = rows[:, 512:512 + T]
            sch.op("scalar", lambda e: e.mul(m_r, PS[0][0:1, 0:T], 1.0 / DM),
                   reads=["ps0", "cst"], writes=["rows_m"])
            sch.op("scalar", lambda e: e.activation(t_r, m_r, AF.Square),
                   reads=["rows_m"], writes=["rows_t"])
            sch.op("vector", lambda e: e.scalar_tensor_tensor(
                t_r, PS[1][0:1, 0:T], 1.0 / DM, t_r, op0=ALU.mult, op1=ALU.subtract),
                reads=["ps1", "rows_t", "cst"], writes=["rows_t"])
            sch.op("scalar", lambda e: e.activation(t_r, t_r, AF.Ln, bias=LN_EPS),
                   reads=["rows_t", "cst"], writes=["rows_t"])
            sch.op("scalar", lambda e: e.activation(t_r, t_r, AF.Exp, scale=-0.5),
                   reads=["rows_t", "cst"], writes=["rows_t"])
            sch.op("vector", lambda e: e.tensor_mul(m_r, m_r, t_r),
                   reads=["rows_m", "rows_t"], writes=["rows_m"])
            sch.op("tensor", lambda e: e.matmul(PS[2][:, 0:T], ones_row_f[:], t_r,
                                                start=True, stop=True),
                   reads=["rows_t", "masks"], writes=["ps2"])
            sch.op("tensor", lambda e: e.matmul(PS[3][:, 0:T], ones_row_f[:], m_r,
                                                start=True, stop=True),
                   reads=["rows_m", "masks"], writes=["ps3"])
            for dc in range(16):
                sch.op("vector", lambda e, dc=dc: e.tensor_mul(
                    lnsc[:, 0:T], resid[:, dc * T:(dc + 1) * T], PS[2][:, 0:T]),
                    reads=[rk, "ps2"], writes=["lnsc"])
                sch.op("vector", lambda e, dc=dc: e.tensor_sub(
                    xln[:, dc * T:(dc + 1) * T], lnsc[:, 0:T], PS[3][:, 0:T]),
                    reads=["lnsc", "ps3"], writes=[xk])

        def emit_ffn(src, dst, do_conversion):
            with contextlib.ExitStack() as fs:
                FS = fs.enter_context
                resid = FS(nc.sbuf_tensor(_nm("f_resid"), [128, 16 * TFF], F32))
                xln = FS(nc.sbuf_tensor(_nm("f_xln"), [128, 16 * TFF], BF))
                actb = FS(nc.sbuf_tensor(_nm("f_act"), [128, 64 * TFF], BF))
                wgb = FS(nc.sbuf_tensor(_nm("f_wg"), [128, 2 * 4096], BF))
                wub = FS(nc.sbuf_tensor(_nm("f_wu"), [128, 2 * 4096], BF))
                wdb = FS(nc.sbuf_tensor(_nm("f_wd"), [128, 2 * 4096], BF))
                lnsc = FS(nc.sbuf_tensor(_nm("f_lnsc"), [128, TFF], F32))
                rows = FS(nc.sbuf_tensor(_nm("f_rows"), [1, 1024], F32))

                if do_conversion:
                    emit_conversion()

                for blk in range(NFB):
                    t0 = blk * TFF
                    sch.op("scalar", lambda e, t0=t0: e.dma_start(
                        resid.ap().rearrange("p (c t) -> p c t", t=TFF),
                        src.ap()[:, t0:t0 + TFF].rearrange("(c p) t -> p c t", p=128)),
                        writes=["resid"], dma=True)

                    emit_ln(TFF, resid, xln,
                            lambda dc: actb[:, dc * TFF:(dc + 1) * TFF], lnsc, rows,
                            [f"act:{dc}" for dc in range(16)])

                    for g in range(NGH):
                        par = g % 2
                        sch.op("sync", lambda e, g=g, par=par: e.dma_start(
                            wgb[:, par * 4096:(par + 1) * 4096],
                            gsc.ap()[:, g * 4096:(g + 1) * 4096]),
                            reads=[f"gsc:{g}"], writes=[f"wgb:{par}"], dma=True)
                        sch.op("sync", lambda e, g=g, par=par: e.dma_start(
                            wub[:, par * 4096:(par + 1) * 4096],
                            usc.ap()[:, g * 4096:(g + 1) * 4096]),
                            reads=[f"usc:{g}"], writes=[f"wub:{par}"], dma=True)
                        for hl in range(2):
                            ht = g * 2 + hl
                            pp = 4 + ht % 2

                            def gmm(e, par=par, hl=hl, pp=pp):
                                for dc in range(16):
                                    ins = e.matmul(
                                        PS[pp][:],
                                        wgb[:, par * 4096 + dc * GH + hl * 128:
                                            par * 4096 + dc * GH + hl * 128 + 128],
                                        xln[:, dc * TFF:(dc + 1) * TFF],
                                        start=(dc == 0), stop=(dc == 15))
                                return ins
                            sch.op("tensor", gmm, reads=[f"wgb:{par}", "xln"],
                                   writes=[psk[pp]])
                            sch.op("scalar", lambda e, ht=ht, pp=pp: e.activation(
                                actb[:, ht * TFF:(ht + 1) * TFF], PS[pp][:], AF.Silu),
                                reads=[psk[pp]], writes=[f"act:{ht}"])

                            pq = 6 + ht % 2

                            def umm(e, par=par, hl=hl, pq=pq):
                                for dc in range(16):
                                    ins = e.matmul(
                                        PS[pq][:],
                                        wub[:, par * 4096 + dc * GH + hl * 128:
                                            par * 4096 + dc * GH + hl * 128 + 128],
                                        xln[:, dc * TFF:(dc + 1) * TFF],
                                        start=(dc == 0), stop=(dc == 15))
                                return ins
                            sch.op("tensor", umm, reads=[f"wub:{par}", "xln"],
                                   writes=[psk[pq]])
                            sch.op("vector", lambda e, ht=ht, pq=pq: e.scalar_tensor_tensor(
                                actb[:, ht * TFF:(ht + 1) * TFF],
                                actb[:, ht * TFF:(ht + 1) * TFF], 0.5, PS[pq][:],
                                op0=ALU.mult, op1=ALU.mult),
                                reads=[f"act:{ht}", psk[pq], "cst"], writes=[f"act:{ht}"])

                    for hh in range(2):
                        for gg in range(16):
                            par = gg % 2
                            sch.op("sync", lambda e, gg=gg, hh=hh, par=par: e.dma_start(
                                wdb.ap()[:, par * 4096:(par + 1) * 4096]
                                .rearrange("p (hl d) -> p hl d", d=1024),
                                dsc.ap()[:, gg * 8192:(gg + 1) * 8192]
                                .rearrange("p (hl d) -> p hl d", d=2048)
                                [:, :, hh * 1024:(hh + 1) * 1024]),
                                reads=[f"dsc:{gg}:0", f"dsc:{gg}:1"],
                                writes=[f"wdb:{par}"], dma=True)
                            for oc8 in range(8):
                                def dmm(e, gg=gg, par=par, oc8=oc8):
                                    for hl in range(4):
                                        ins = e.matmul(
                                            PS[oc8][:],
                                            wdb[:, par * 4096 + hl * 1024 + oc8 * 128:
                                                par * 4096 + hl * 1024 + oc8 * 128 + 128],
                                            actb[:, (gg * 4 + hl) * TFF:
                                                 (gg * 4 + hl + 1) * TFF],
                                            start=(gg == 0 and hl == 0),
                                            stop=(gg == 15 and hl == 3))
                                    return ins
                                sch.op("tensor", dmm,
                                       reads=[f"wdb:{par}"] +
                                             [f"act:{gg * 4 + i}" for i in range(4)],
                                       writes=[psk[oc8]])
                        for oc8 in range(8):
                            oc = hh * 8 + oc8
                            sch.op("vector", lambda e, oc=oc, oc8=oc8: e.tensor_add(
                                resid[:, oc * TFF:(oc + 1) * TFF],
                                resid[:, oc * TFF:(oc + 1) * TFF], PS[oc8][:]),
                                reads=[psk[oc8]], writes=["resid"])

                    sch.op("scalar", lambda e, t0=t0: e.dma_start(
                        dst.ap()[:, t0:t0 + TFF].rearrange("(c p) t -> p c t", p=128),
                        resid.ap().rearrange("p (c t) -> p c t", t=TFF)),
                        reads=["resid"], dma=True)

        def emit_mamba(src, dst):
            with contextlib.ExitStack() as ms:
                MS = ms.enter_context
                residb = MS(nc.sbuf_tensor(_nm("m_resid"), [128, 2 * 16 * TM], F32))
                xlnb = MS(nc.sbuf_tensor(_nm("m_xln"), [128, 2 * 16 * TM], BF))
                wib = MS(nc.sbuf_tensor(_nm("m_wi"), [128, 2 * 4096], BF))
                wob = MS(nc.sbuf_tensor(_nm("m_wo"), [128, 2 * 8192], BF))
                sg = MS(nc.sbuf_tensor(_nm("m_sg"), [128, 32 * TM], BF))
                cx = MS(nc.sbuf_tensor(_nm("m_cx"), [128, 32 * TM], BF))
                cB = MS(nc.sbuf_tensor(_nm("m_cB"), [64, TM], BF))
                cC = MS(nc.sbuf_tensor(_nm("m_cC"), [64, TM], BF))
                dtf = MS(nc.sbuf_tensor(_nm("m_dtf"), [64, TM], F32))
                cin = MS(nc.sbuf_tensor(_nm("m_cin"), [128, 2 * (TM + 3)], F32))
                cacc = MS(nc.sbuf_tensor(_nm("m_cacc"), [128, 2 * TM], F32))
                halo = MS(nc.sbuf_tensor(_nm("m_halo"), [128, 99], F32))
                Sf = MS(nc.sbuf_tensor(_nm("m_Sf"), [64, HEADS * PD], F32))
                Sb = MS(nc.sbuf_tensor(_nm("m_Sb"), [64, HEADS * PD], BF))
                xst = MS(nc.sbuf_tensor(_nm("m_xst"), [128, 32 * 128], BF))
                sp1 = MS(nc.sbuf_tensor(_nm("m_sp1"), [128, 2 * 64], F32))
                sp2 = MS(nc.sbuf_tensor(_nm("m_sp2"), [128, 2 * 64], F32))
                dtp = MS(nc.sbuf_tensor(_nm("m_dtp"), [128, 2 * 64], F32))
                a_tm = MS(nc.sbuf_tensor(_nm("m_a"), [128, 2 * 64], F32))
                c_sb = MS(nc.sbuf_tensor(_nm("m_c"), [128, 2 * 64], F32))
                cq_sb = MS(nc.sbuf_tensor(_nm("m_cq"), [1, 2 * 64], F32))
                wdec = MS(nc.sbuf_tensor(_nm("m_wdec"), [128, 2 * 64], F32))
                wdt = MS(nc.sbuf_tensor(_nm("m_wdt"), [128, 2 * 64], F32))
                ec_sb = MS(nc.sbuf_tensor(_nm("m_ec"), [128, 2 * 64], F32))
                Ebc = MS(nc.sbuf_tensor(_nm("m_Ebc"), [64, 2 * 64], F32))
                Gt_sb = MS(nc.sbuf_tensor(_nm("m_Gt"), [128, 2 * 128], F32))
                BCt = MS(nc.sbuf_tensor(_nm("m_BCt"), [128, 2 * 64], BF))
                ATq = MS(nc.sbuf_tensor(_nm("m_ATq"), [128, 2 * 512], BF))
                expmq = MS(nc.sbuf_tensor(_nm("m_expq"), [128, 2 * 512], BF))
                GmTq = MS(nc.sbuf_tensor(_nm("m_GmTq"), [128, 2 * 512], BF))
                Bdq = MS(nc.sbuf_tensor(_nm("m_Bdq"), [128, 2 * 256], BF))
                y2b = MS(nc.sbuf_tensor(_nm("m_y2"), [128, 2 * 256], F32))
                xdt = MS(nc.sbuf_tensor(_nm("m_xdt"), [128, 2 * 256], BF))
                ybuf2 = MS(nc.sbuf_tensor(_nm("m_yb2"), [128, 2 * 256], F32))
                tmpS = MS(nc.sbuf_tensor(_nm("m_tmpS"), [64, 256], F32))
                ssqp = MS(nc.sbuf_tensor(_nm("m_ssqp"), [128, 16], F32))
                ssq_tm = MS(nc.sbuf_tensor(_nm("m_ssqtm"), [128, NQ], F32))
                rstd_row = MS(nc.sbuf_tensor(_nm("m_rsr"), [1, TM], F32))
                rstd_bc = MS(nc.sbuf_tensor(_nm("m_rsb"), [128, TM], F32))
                lnsc = MS(nc.sbuf_tensor(_nm("m_lnsc"), [128, TM], F32))
                rows = MS(nc.sbuf_tensor(_nm("m_rows"), [1, 1024], F32))
                sqb = MS(nc.sbuf_tensor(_nm("m_sq"), [128, 16 * TM], BF))

                def g_init(e):
                    e.memset(halo[:], 0.0)
                    e.memset(Sf[:], 0.0)
                    return e.memset(Sb[:], 0.0)
                sch.op("gpsimd", g_init,
                       writes=[f"halo:{c}" for c in range(33)] +
                              [f"Sf:{hq}" for hq in range(16)] +
                              [f"Sb:{hq}" for hq in range(16)])

                def rep4(ap2d, nfree):
                    return ap2d.rearrange("p (o n) -> p o n", o=1) \
                               .broadcast_to([ap2d.shape[0], 4, nfree])

                def col4(ap2d, nfree):
                    return ap2d.rearrange("p (h o) -> p h o", o=1) \
                               .broadcast_to([ap2d.shape[0], 4, nfree])

                def emit_prep(q):
                    qa = q * QS
                    o = q * 64
                    sch.op("tensor", lambda e, qa=qa: e.transpose(
                        PS[6][:, 0:64], dtf[:, qa:qa + QS], I_f[0:64, 0:64]),
                        reads=["dtf", "masks"], writes=["ps6"])
                    sch.op("vector", lambda e, o=o: e.tensor_copy(
                        sp1[:, o:o + 64], PS[6][:, 0:64]),
                        reads=["ps6"], writes=[f"sp1:{q}"])
                    sch.op("gpsimd", lambda e, o=o: e.tensor_add(
                        sp1[:, o:o + 64], sp1[:, o:o + 64], p_dtbbc[:]),
                        reads=["p_dtbbc", f"sp1:{q}"], writes=[f"sp1:{q}"])
                    sch.op("scalar", lambda e, o=o: e.activation(
                        sp2[:, o:o + 64], sp1[:, o:o + 64], AF.Abs),
                        reads=[f"sp1:{q}"], writes=[f"sp2:{q}"])
                    sch.op("scalar", lambda e, o=o: e.activation(
                        sp2[:, o:o + 64], sp2[:, o:o + 64], AF.Exp, scale=-1.0),
                        reads=[f"sp2:{q}", "cst"], writes=[f"sp2:{q}"])
                    sch.op("scalar", lambda e, o=o: e.activation(
                        sp2[:, o:o + 64], sp2[:, o:o + 64], AF.Ln, bias=1.0),
                        reads=[f"sp2:{q}", "cst"], writes=[f"sp2:{q}"])
                    sch.op("gpsimd", lambda e, o=o: e.tensor_scalar_max(
                        dtp[:, o:o + 64], sp1[:, o:o + 64], 0.0),
                        reads=[f"sp1:{q}"], writes=[f"dtp:{q}"])
                    sch.op("gpsimd", lambda e, o=o: e.tensor_add(
                        dtp[:, o:o + 64], dtp[:, o:o + 64], sp2[:, o:o + 64]),
                        reads=[f"sp2:{q}", f"dtp:{q}"], writes=[f"dtp:{q}"])
                    sch.op("gpsimd", lambda e, o=o: e.tensor_mul(
                        a_tm[:, o:o + 64], dtp[:, o:o + 64], p_Abc[:]),
                        reads=[f"dtp:{q}", "p_Abc"], writes=[f"a_tm:{q}"])
                    sch.op("tensor", lambda e, o=o: e.matmul(
                        PS[0][:, 0:64], LT[:], a_tm[:, o:o + 64],
                        start=True, stop=True),
                        reads=[f"a_tm:{q}", "masks"], writes=["ps0"])
                    sch.op("tensor", lambda e, o=o: e.matmul(
                        PS[1][0:1, 0:64], ones_col_f[:], a_tm[:, o:o + 64],
                        start=True, stop=True),
                        reads=[f"a_tm:{q}", "masks"], writes=["ps1"])
                    sch.op("vector", lambda e, o=o: e.tensor_copy(
                        c_sb[:, o:o + 64], PS[0][:, 0:64]),
                        reads=["ps0"], writes=[f"c_sb:{q}"])
                    sch.op("vector", lambda e, o=o: e.tensor_copy(
                        cq_sb[:, o:o + 64], PS[1][0:1, 0:64]),
                        reads=["ps1"], writes=[f"cq_sb:{q}"])
                    sch.op("tensor", lambda e, o=o: e.matmul(
                        PS[2][:, 0:64], ones_row_f[:], cq_sb[:, o:o + 64],
                        start=True, stop=True),
                        reads=[f"cq_sb:{q}", "masks"], writes=["ps2"])
                    sch.op("tensor", lambda e, o=o: e.matmul(
                        PS[3][0:64, 0:64], ones_row_f[:, 0:64], cq_sb[:, o:o + 64],
                        start=True, stop=True),
                        reads=[f"cq_sb:{q}", "masks"], writes=["ps3"])
                    sch.op("vector", lambda e, o=o: e.tensor_sub(
                        wdec[:, o:o + 64], PS[2][:, 0:64], c_sb[:, o:o + 64]),
                        reads=["ps2", f"c_sb:{q}"], writes=[f"wdec:{q}"])
                    sch.op("scalar", lambda e, o=o: e.activation(
                        wdec[:, o:o + 64], wdec[:, o:o + 64], AF.Exp),
                        reads=[f"wdec:{q}"], writes=[f"wdec:{q}"])
                    sch.op("scalar", lambda e, o=o: e.activation(
                        ec_sb[:, o:o + 64], c_sb[:, o:o + 64], AF.Exp),
                        reads=[f"c_sb:{q}"], writes=[f"ec_sb:{q}"])
                    sch.op("scalar", lambda e, o=o: e.activation(
                        Ebc[:, o:o + 64], PS[3][0:64, 0:64], AF.Exp),
                        reads=["ps3"], writes=[f"Ebc:{q}"])
                    sch.op("vector", lambda e, o=o: e.tensor_mul(
                        wdt[:, o:o + 64], wdec[:, o:o + 64], dtp[:, o:o + 64]),
                        reads=[f"wdec:{q}", f"dtp:{q}"], writes=[f"wdt:{q}"])
                    sch.op("tensor", lambda e, qa=qa, o=o: e.matmul(
                        PS[6][:, 128:256], cB[:, qa:qa + QS], cC[:, qa:qa + QS],
                        start=True, stop=True),
                        reads=["cB", "cC", "masks"], writes=["ps6"])
                    sch.op("vector", lambda e, q=q: e.tensor_mul(
                        Gt_sb[:, q * 128:(q + 1) * 128], PS[6][:, 128:256], LT[:]),
                        reads=["ps6", "masks"], writes=[f"Gt_sb:{q}"])
                    sch.op("tensor", lambda e, qa=qa: e.transpose(
                        PSB[7][:, 0:64], cB[:, qa:qa + QS], I_b[0:64, 0:64]),
                        reads=["cB", "masks"], writes=["ps7"])
                    sch.op("vector", lambda e, o=o: e.tensor_copy(
                        BCt[:, o:o + 64], PSB[7][:, 0:64]),
                        reads=["ps7"], writes=[f"BCt:{q}"])

                pin_ctr = [0]
                for mt in range(NMT):
                    t0 = mt * TM
                    pmt = mt % 2
                    ro = pmt * 16 * TM
                    resid = residb[:, ro:ro + 16 * TM]
                    xln = xlnb[:, ro:ro + 16 * TM]
                    rk = f"resid:{pmt}"
                    xk = f"xln:{pmt}"
                    sch.op("scalar", lambda e, t0=t0, ro=ro: e.dma_start(
                        residb.ap()[:, ro:ro + 16 * TM]
                        .rearrange("p (c t) -> p c t", t=TM),
                        src.ap()[:, t0:t0 + TM].rearrange("(c p) t -> p c t", p=128)),
                        writes=[rk], dma=True)

                    emit_ln(TM, resid, xln,
                            lambda dc: sqb[:, dc * TM:(dc + 1) * TM], lnsc, rows,
                            [f"sq:{dc}" for dc in range(16)], rk=rk, xk=xk)

                    for gpi, G in enumerate([32] + list(range(32))):
                        par = gpi % 2
                        nw_ = 4096 if G < 32 else 3072
                        fcs = [2 * G, 2 * G + 1] if G < 32 else [64, 65]
                        sch.op("sync", lambda e, G=G, par=par, nw_=nw_: e.dma_start(
                            wib[:, par * 4096:par * 4096 + nw_],
                            isc.ap()[:, G * 4096:G * 4096 + nw_]),
                            reads=[f"isc:{G}"], writes=[f"wib:{par}"], dma=True)
                        for j, fc in enumerate(fcs):
                            pp = 4 + pin_ctr[0] % 2
                            pin_ctr[0] += 1
                            W = 256 if G < 32 else 192
                            base = par * 4096
                            coff = j * 128
                            mrows = 64 if fc == 65 else 128

                            def imm(e, base=base, W=W, coff=coff, mrows=mrows, pp=pp,
                                    xln=xln):
                                for dc in range(16):
                                    ins = e.matmul(
                                        PS[pp][0:mrows, 0:TM],
                                        wib[:, base + dc * W + coff:
                                            base + dc * W + coff + mrows],
                                        xln[:, dc * TM:(dc + 1) * TM],
                                        start=(dc == 0), stop=(dc == 15))
                                return ins
                            sch.op("tensor", imm, reads=[f"wib:{par}", xk],
                                   writes=[psk[pp]])

                            if fc < 32:
                                sch.op("scalar", lambda e, fc=fc, pp=pp: e.activation(
                                    sg[:, fc * TM:(fc + 1) * TM], PS[pp][:, 0:TM],
                                    AF.Silu),
                                    reads=[psk[pp]], writes=[f"sg:{fc}"])
                            elif fc < 65:
                                cc = fc - 32
                                cpar = cc % 2
                                tape = "vector"
                                cb0 = cpar * (TM + 3)
                                sch.op("gpsimd", lambda e, cc=cc, cb0=cb0: e.tensor_copy(
                                    cin[:, cb0:cb0 + 3], halo[:, cc * 3:cc * 3 + 3]),
                                    reads=[f"halo:{cc}"], writes=[f"cin:{cpar}"])
                                sch.op("scalar", lambda e, pp=pp, cb0=cb0: e.copy(
                                    cin[:, cb0 + 3:cb0 + 3 + TM], PS[pp][:, 0:TM]),
                                    reads=[psk[pp]], writes=[f"cin:{cpar}"])
                                sch.op("gpsimd", lambda e, cc=cc, cb0=cb0: e.tensor_copy(
                                    halo[:, cc * 3:cc * 3 + 3],
                                    cin[:, cb0 + TM:cb0 + TM + 3]),
                                    reads=[f"cin:{cpar}"], writes=[f"halo:{cc}"])
                                sch.op("gpsimd", lambda e, cc=cc, cpar=cpar, cb0=cb0:
                                       e.tensor_scalar(
                                    cacc[:, cpar * TM:(cpar + 1) * TM],
                                    cin[:, cb0:cb0 + TM],
                                    p_cw[:, cc * 4:cc * 4 + 1], None, op0=ALU.mult),
                                    reads=[f"cin:{cpar}", "p_cw"], writes=[f"cacc:{cpar}"])
                                for k in range(1, 4):
                                    sch.op(tape, lambda e, cc=cc, cpar=cpar, k=k, cb0=cb0:
                                           e.scalar_tensor_tensor(
                                        cacc[:, cpar * TM:(cpar + 1) * TM],
                                        cin[:, cb0 + k:cb0 + k + TM],
                                        p_cw[:, cc * 4 + k:cc * 4 + k + 1],
                                        cacc[:, cpar * TM:(cpar + 1) * TM],
                                        op0=ALU.mult, op1=ALU.add),
                                        reads=[f"cin:{cpar}", "p_cw", f"cacc:{cpar}"],
                                        writes=[f"cacc:{cpar}"])
                                if cc < 32:
                                    sch.op("scalar", lambda e, cc=cc, cpar=cpar:
                                           e.activation(
                                        cx[:, cc * TM:(cc + 1) * TM],
                                        cacc[:, cpar * TM:(cpar + 1) * TM], AF.Silu,
                                        bias=p_cb[:, cc:cc + 1]),
                                        reads=[f"cacc:{cpar}", "p_cb"],
                                        writes=[f"cx:{cc}"])
                                else:
                                    sch.op("scalar", lambda e, cpar=cpar: e.activation(
                                        cB[:, 0:TM],
                                        cacc[0:64, cpar * TM:cpar * TM + TM], AF.Silu,
                                        bias=p_cb[0:64, 32:33]),
                                        reads=[f"cacc:{cpar}", "p_cb"], writes=["cB"])
                                    sch.op("scalar", lambda e, cpar=cpar: e.activation(
                                        cC[:, 0:TM],
                                        cacc[64:128, cpar * TM:cpar * TM + TM], AF.Silu,
                                        bias=p_cb[64:128, 32:33]),
                                        reads=[f"cacc:{cpar}", "p_cb"], writes=["cC"])
                            else:
                                sch.op("vector", lambda e, pp=pp: e.tensor_copy(
                                    dtf[:, 0:TM], PS[pp][0:64, 0:TM]),
                                    reads=[psk[pp]], writes=["dtf"])
                        if G == 32:
                            emit_prep(0)
                            emit_prep(1)

                    for q in range(NQ):
                        qa = q * QS
                        qo = q * 64
                        for rnd in range(4):
                            def xtr(e, rnd=rnd, qa=qa):
                                for j in range(8):
                                    fc = rnd * 8 + j
                                    ins = e.transpose(
                                        PSB[7][:, j * 128:(j + 1) * 128],
                                        cx[:, fc * TM + qa:fc * TM + qa + QS], I_b[:])
                                return ins
                            sch.op("tensor", xtr,
                                   reads=[f"cx:{rnd * 8 + j}" for j in range(8)] + ["masks"],
                                   writes=["ps7"])
                            sch.op("vector", lambda e, rnd=rnd: e.tensor_copy(
                                xst[:, rnd * 1024:(rnd + 1) * 1024], PSB[7][:]),
                                reads=["ps7"], writes=["xst"])

                        for hq in range(16):
                            h0 = hq * 4
                            pq = hq % 2
                            aq = pq * 512
                            bq = pq * 256

                            sch.op("gpsimd", lambda e, h0=h0, aq=aq, qo=qo: e.tensor_tensor(
                                out=ATq[:, aq:aq + 512]
                                    .rearrange("p (h t) -> p h t", h=4),
                                in0=rep4(LT[:], 128),
                                in1=col4(a_tm[:, qo + h0:qo + h0 + 4], 128), op=ALU.mult),
                                reads=[f"a_tm:{q}", "masks"], writes=[f"ATq:{pq}"])

                            def margmm(e, aq=aq, pq=pq):
                                for k in range(4):
                                    ins = e.matmul(
                                        PS[pq][:, k * 128:(k + 1) * 128], US_b[:],
                                        ATq[:, aq + k * 128:aq + (k + 1) * 128],
                                        start=True, stop=True)
                                return ins
                            sch.op("tensor", margmm, reads=[f"ATq:{pq}", "masks"],
                                   writes=[psk[pq]])
                            sch.op("scalar", lambda e, aq=aq, pq=pq: e.activation(
                                expmq[:, aq:aq + 512], PS[pq][:, 0:512], AF.Exp),
                                reads=[psk[pq]], writes=[f"expm:{pq}"])
                            sch.op("vector", lambda e, aq=aq, q=q: e.tensor_tensor(
                                out=GmTq[:, aq:aq + 512]
                                    .rearrange("p (h t) -> p h t", h=4),
                                in0=expmq[:, aq:aq + 512]
                                    .rearrange("p (h t) -> p h t", h=4),
                                in1=rep4(Gt_sb[:, q * 128:(q + 1) * 128], 128),
                                op=ALU.mult),
                                reads=[f"expm:{pq}", f"Gt_sb:{q}"],
                                writes=[f"GmT:{pq}"])
                            sch.op("vector", lambda e, bq=bq, h0=h0, qo=qo:
                                   e.tensor_tensor(
                                out=xdt[:, bq:bq + 256]
                                    .rearrange("p (h d) -> p h d", h=4),
                                in0=xst[:, h0 * 64:(h0 + 4) * 64]
                                    .rearrange("p (h d) -> p h d", h=4),
                                in1=col4(dtp[:, qo + h0:qo + h0 + 4], 64),
                                op=ALU.mult),
                                reads=["xst", f"dtp:{q}"], writes=[f"xdt:{pq}"])
                            sch.op("gpsimd", lambda e, h0=h0, bq=bq, qo=qo:
                                   e.tensor_tensor(
                                out=Bdq[:, bq:bq + 256]
                                    .rearrange("p (h n) -> p h n", h=4),
                                in0=rep4(BCt[:, qo:qo + 64], 64),
                                in1=col4(wdt[:, qo + h0:qo + h0 + 4], 64), op=ALU.mult),
                                reads=[f"BCt:{q}", f"wdt:{q}"], writes=[f"Bdq:{pq}"])

                            def y1mm(e, aq=aq, pq=pq, bq=bq):
                                for k in range(4):
                                    ins = e.matmul(
                                        PS[2 + pq][:, k * 64:(k + 1) * 64],
                                        GmTq[:, aq + k * 128:aq + (k + 1) * 128],
                                        xdt[:, bq + k * 64:bq + (k + 1) * 64],
                                        start=True, stop=True)
                                return ins
                            sch.op("tensor", y1mm, reads=[f"GmT:{pq}", f"xdt:{pq}"],
                                   writes=[psk[2 + pq]])

                            def dsmm(e, bq=bq, h0=h0):
                                for k in range(4):
                                    ins = e.matmul(
                                        PS[5][0:64, k * 64:(k + 1) * 64],
                                        Bdq[:, bq + k * 64:bq + (k + 1) * 64],
                                        xst[:, (h0 + k) * 64:(h0 + k + 1) * 64],
                                        start=True, stop=True)
                                return ins
                            sch.op("tensor", dsmm, reads=[f"Bdq:{pq}", "xst"],
                                   writes=["ps5"])

                            for k2 in range(2):
                                bank = 4 if k2 == 0 else 7

                                def y2mm(e, k2=k2, bank=bank, h0=h0, qa=qa):
                                    for j in range(2):
                                        h = h0 + 2 * k2 + j
                                        ins = e.matmul(
                                            PS[bank][:, j * 64:(j + 1) * 64],
                                            cC[:, qa:qa + QS],
                                            Sb[:, h * 64:(h + 1) * 64],
                                            start=True, stop=True)
                                    return ins
                                sch.op("tensor", y2mm, reads=["cC", f"Sb:{hq}"],
                                       writes=[psk[bank]])

                            sch.op("vector", lambda e, bq=bq, h0=h0: e.tensor_tensor(
                                out=y2b[:, bq:bq + 256]
                                    .rearrange("p (h d) -> p h d", h=4),
                                in0=xst[:, h0 * 64:(h0 + 4) * 64]
                                    .rearrange("p (h d) -> p h d", h=4),
                                in1=col4(p_Dbc[:, h0:h0 + 4], 64), op=ALU.mult),
                                reads=["xst", "p_Dbc"], writes=[f"y2b:{pq}"])
                            sch.op("vector", lambda e, bq=bq, pq=pq: e.tensor_add(
                                y2b[:, bq:bq + 256], y2b[:, bq:bq + 256],
                                PS[2 + pq][:, 0:256]),
                                reads=[psk[2 + pq], f"y2b:{pq}"], writes=[f"y2b:{pq}"])
                            for k in range(4):
                                bank = 4 if k < 2 else 7
                                col = (k % 2) * 64
                                sch.op("scalar", lambda e, bq=bq, k=k, h0=h0,
                                       bank=bank, col=col, qo=qo: e.activation(
                                    ybuf2[:, bq + k * 64:bq + (k + 1) * 64],
                                    PS[bank][:, col:col + 64], AF.Copy,
                                    scale=ec_sb[:, qo + h0 + k:qo + h0 + k + 1]),
                                    reads=[psk[bank], f"ec_sb:{q}"],
                                    writes=[f"yb2:{pq}"])
                            sch.op("vector", lambda e, bq=bq: e.tensor_add(
                                y2b[:, bq:bq + 256], y2b[:, bq:bq + 256],
                                ybuf2[:, bq:bq + 256]),
                                reads=[f"yb2:{pq}", f"y2b:{pq}"], writes=[f"y2b:{pq}"])
                            sch.op("gpsimd", lambda e, h0=h0, qo=qo: e.tensor_tensor(
                                out=tmpS[:].rearrange("p (h d) -> p h d", h=4),
                                in0=Sf[:, h0 * 64:(h0 + 4) * 64]
                                    .rearrange("p (h d) -> p h d", h=4),
                                in1=col4(Ebc[:, qo + h0:qo + h0 + 4], 64), op=ALU.mult),
                                reads=[f"Sf:{hq}", f"Ebc:{q}"], writes=["tmpS"])
                            sch.op("vector", lambda e, h0=h0: e.tensor_add(
                                Sf[:, h0 * 64:(h0 + 4) * 64], tmpS[:],
                                PS[5][0:64, 0:256]),
                                reads=["tmpS", "ps5"], writes=[f"Sf:{hq}"])
                            sch.op("scalar", lambda e, h0=h0: e.copy(
                                Sb[:, h0 * 64:(h0 + 4) * 64],
                                Sf[:, h0 * 64:(h0 + 4) * 64]),
                                reads=[f"Sf:{hq}"], writes=[f"Sb:{hq}"])
                            sch.op("scalar", lambda e, bq=bq: e.activation(
                                ybuf2[:, bq:bq + 256], y2b[:, bq:bq + 256], AF.Square),
                                reads=[f"y2b:{pq}"], writes=[f"yb2:{pq}"])
                            sch.op("vector", lambda e, hq=hq, bq=bq: e.tensor_reduce(
                                ssqp[:, hq:hq + 1], ybuf2[:, bq:bq + 256],
                                mybir.AxisListType.X, ALU.add),
                                reads=[f"yb2:{pq}"], writes=["ssqp"])

                            def prtr(e, bq=bq):
                                for j2 in range(2):
                                    ins = e.transpose(
                                        PS[6][:, j2 * 128:(j2 + 1) * 128],
                                        y2b[:, bq + j2 * 128:bq + (j2 + 1) * 128],
                                        I_f[:])
                                return ins
                            sch.op("tensor", prtr, reads=[f"y2b:{pq}", "masks"],
                                   writes=["ps6"])
                            for j2 in range(2):
                                pr = hq * 2 + j2
                                sch.op("vector", lambda e, pr=pr, j2=j2, qa=qa:
                                       e.scalar_tensor_tensor(
                                    cx[:, pr * TM + qa:pr * TM + qa + QS],
                                    PS[6][:, j2 * 128:(j2 + 1) * 128],
                                    p_nw[:, pr:pr + 1],
                                    sg[:, pr * TM + qa:pr * TM + qa + QS],
                                    op0=ALU.mult, op1=ALU.mult),
                                    reads=["ps6", "p_nw", f"sg:{pr}"],
                                    writes=[f"cx:{pr}"])
                        sch.op("vector", lambda e, q=q: e.tensor_reduce(
                            ssq_tm[:, q:q + 1], ssqp[:], mybir.AxisListType.X, ALU.add),
                            reads=["ssqp"], writes=["ssq_tm"])

                    sch.op("gpsimd", lambda e: e.tensor_scalar(
                        ssq_tm[:], ssq_tm[:], 1.0 / INTER, None, op0=ALU.mult),
                        reads=["cst", "ssq_tm"], writes=["ssq_tm"])
                    sch.op("scalar", lambda e: e.activation(
                        ssq_tm[:], ssq_tm[:], AF.Ln, bias=RMS_EPS),
                        reads=["ssq_tm", "cst"], writes=["ssq_tm"])
                    sch.op("scalar", lambda e: e.activation(
                        ssq_tm[:], ssq_tm[:], AF.Exp, scale=-0.5),
                        reads=["ssq_tm", "cst"], writes=["ssq_tm"])

                    def rtr(e):
                        for q in range(NQ):
                            ins = e.transpose(PS[6][0:1, q * 128:(q + 1) * 128],
                                              ssq_tm[:, q:q + 1], I_f[:])
                        return ins
                    sch.op("tensor", rtr, reads=["ssq_tm", "masks"], writes=["ps6"])
                    sch.op("vector", lambda e: e.tensor_copy(
                        rstd_row[0:1, 0:TM], PS[6][0:1, 0:TM]),
                        reads=["ps6"], writes=["rstd_row"])
                    sch.op("tensor", lambda e: e.matmul(
                        PS[6][:, 0:TM], ones_row_f[:], rstd_row[0:1, 0:TM],
                        start=True, stop=True),
                        reads=["rstd_row", "masks"], writes=["ps6"])
                    sch.op("vector", lambda e: e.tensor_copy(rstd_bc[:], PS[6][:, 0:TM]),
                           reads=["ps6"], writes=["rstd_bc"])

                    for og in range(8):
                        par = og % 2
                        sch.op("sync", lambda e, og=og, par=par: e.dma_start(
                            wob[:, par * 8192:(par + 1) * 8192],
                            osc.ap()[:, og * 8192:(og + 1) * 8192]),
                            reads=[f"osc:{og}:0", f"osc:{og}:1"],
                            writes=[f"wob:{par}"], dma=True)

                        def omm(e, og=og, par=par):
                            for il in range(4):
                                for oc in range(16):
                                    ins = e.matmul(
                                        PS[oc // 2][0:128, (oc % 2) * 256:
                                                    (oc % 2) * 256 + 256],
                                        wob[:, par * 8192 + il * 2048 + oc * 128:
                                            par * 8192 + il * 2048 + oc * 128 + 128],
                                        cx[:, (og * 4 + il) * TM:(og * 4 + il + 1) * TM],
                                        start=(og == 0 and il == 0 and oc % 2 == 0),
                                        stop=(og == 7 and il == 3 and oc % 2 == 1),
                                        skip_group_check=True)
                            return ins
                        sch.op("tensor", omm,
                               reads=[f"wob:{par}"] +
                                     [f"cx:{og * 4 + i}" for i in range(4)],
                               writes=psk)
                    for oc in range(16):
                        op2 = oc % 2
                        sch.op("vector", lambda e, oc=oc, op2=op2: e.tensor_mul(
                            ybuf2[:, op2 * 256:op2 * 256 + 256],
                            PS[oc // 2][:, (oc % 2) * 256:(oc % 2) * 256 + 256],
                            rstd_bc[:]),
                            reads=[psk[oc // 2], "rstd_bc"], writes=[f"yb2:{op2}"])
                        sch.op("vector", lambda e, oc=oc, op2=op2, resid=resid:
                               e.tensor_add(
                            resid[:, oc * TM:(oc + 1) * TM],
                            resid[:, oc * TM:(oc + 1) * TM],
                            ybuf2[:, op2 * 256:op2 * 256 + 256]),
                            reads=[f"yb2:{op2}"], writes=[rk])

                    sch.op("scalar", lambda e, t0=t0, ro=ro: e.dma_start(
                        dst.ap()[:, t0:t0 + TM].rearrange("(c p) t -> p c t", p=128),
                        residb.ap()[:, ro:ro + 16 * TM]
                        .rearrange("p (c t) -> p c t", t=TM)),
                        reads=[rk], dma=True)

        if "ffn1" in phases:
            emit_ffn(xT, X1, do_conversion=True)
            sch.barrier()
        if "mamba" in phases:
            emit_mamba(X1, X2)
            sch.barrier()
        if "ffn2" in phases:
            emit_ffn(X2, outT, do_conversion="ffn1" not in phases)

        counts, ctr, dctr = sch.emit()
        print("op counts:", counts, "incs:", ctr, "dma:", dctr)
    return nc


_NC_CACHE = {}


def kernel(**inputs):
    x = np.ascontiguousarray(inputs["L0_attn_hidden_states"], dtype=np.float32)
    key = "full"
    if key not in _NC_CACHE:
        _NC_CACHE[key] = build_nc()
    nc = _NC_CACHE[key]
    base = {
        "wgT": np.ascontiguousarray(np.asarray(inputs["L0_ffn_w_gate"], np.float32).T),
        "wuT": np.ascontiguousarray(np.asarray(inputs["L0_ffn_w_up"], np.float32).T),
        "wdT": np.ascontiguousarray(np.asarray(inputs["L0_ffn_w_down"], np.float32).T),
        "winT": np.ascontiguousarray(np.asarray(inputs["L0_attn_in_proj_w"], np.float32).T),
        "woT": np.ascontiguousarray(np.asarray(inputs["L0_attn_out_proj_w"], np.float32).T),
        "cw": np.ascontiguousarray(inputs["L0_attn_conv_w"], np.float32),
        "cb": np.ascontiguousarray(inputs["L0_attn_conv_b"], np.float32),
        "dtb": np.ascontiguousarray(inputs["L0_attn_dt_bias"], np.float32),
        "alog": np.ascontiguousarray(inputs["L0_attn_A_log"], np.float32),
        "dvec": np.ascontiguousarray(inputs["L0_attn_D"], np.float32),
        "nw": np.ascontiguousarray(inputs["L0_attn_norm_weight"], np.float32),
    }
    in_maps = []
    for c in range(NCORES):
        m = dict(base)
        m["xT"] = np.ascontiguousarray(x[c].T)
        in_maps.append(m)
    res = run_bass_kernel_spmd(nc, in_maps, core_ids=list(range(NCORES)))
    out = np.stack([np.ascontiguousarray(np.asarray(r["outT"]).T)
                    for r in res.results], axis=0)
    return np.ascontiguousarray(out.astype(np.float32, copy=False))


if __name__ == "__main__":
    import time
    t = time.time()
    nc = build_nc()
    print("built in %.1fs" % (time.time() - t))


# version 10
# speedup vs baseline: 1.0605x; 1.0605x over previous
import contextlib
import numpy as np

import concourse.bass as bass
import concourse.mybir as mybir
from concourse.bass_utils import run_bass_kernel_spmd

F32 = mybir.dt.float32
BF = mybir.dt.bfloat16
AF = mybir.ActivationFunctionType
ALU = mybir.AluOpType

NCORES = 8
DM = 2048
LTOK = 2048
FFNH = 8192
INTER = 4096
HEADS = 64
PD = 64
PROJ = 8384
CONVD = 4224
LN_EPS = 1e-5
RMS_EPS = 1e-6

TFF = 512
NFB = LTOK // TFF
GH = 256
NGH = FFNH // GH
TM = 256
NMT = LTOK // TM
QS = 128
NQ = TM // QS

ENGS = ("sync", "scalar", "tensor", "vector", "gpsimd")


NDMASEM = 4


class _Op:
    __slots__ = ("eng", "fn", "dma", "deps", "sem_val", "need_inc", "idx", "slot")

    def __init__(self, eng, fn, dma):
        self.eng = eng
        self.fn = fn
        self.dma = dma
        self.deps = []
        self.sem_val = None
        self.need_inc = dma
        self.idx = -1
        self.slot = 0


class Sched:

    def __init__(self, nc, stack, tag=""):
        self.nc = nc
        self.ops = []
        self.last_writer = {}
        self.readers = {}
        self.guards = []
        S = stack.enter_context
        self.eng_sem = {e: S(nc.semaphore(f"{tag}e_{e}")) for e in ENGS}
        self.dma_sem = {e: [S(nc.semaphore(f"{tag}d_{e}{i}")) for i in range(NDMASEM)]
                        for e in ("sync", "scalar", "gpsimd")}
        self.dma_rr = {e: 0 for e in self.dma_sem}

    def op(self, eng, fn, reads=(), writes=(), dma=False, after=()):
        o = _Op(eng, fn, dma)
        o.idx = len(self.ops)
        if dma:
            o.slot = self.dma_rr[eng] % NDMASEM
            self.dma_rr[eng] += 1
        deps = set(self.guards)
        for r in reads:
            w = self.last_writer.get(r)
            if w is not None:
                deps.add(w)
        for w in writes:
            lw = self.last_writer.get(w)
            if lw is not None:
                deps.add(lw)
            deps.update(self.readers.get(w, ()))
        for d in after:
            if d is not None:
                deps.add(d)
        deps.discard(o)
        o.deps = sorted(deps, key=lambda d: d.idx)
        self.ops.append(o)
        for r in reads:
            self.readers.setdefault(r, []).append(o)
        for w in writes:
            self.last_writer[w] = o
            self.readers[w] = []
        return o

    def barrier(self):
        last = {}
        for o in self.ops:
            last[(o.eng, o.dma)] = o
        g = self.op("sync", lambda e: e.nop(), after=list(last.values()))
        self.guards = [g]
        return g

    def emit(self, final_engine="sync"):
        for o in self.ops:
            for d in o.deps:
                d.need_inc = True
        last_per_eng = {}
        for o in self.ops:
            last_per_eng[o.eng] = o
        for o in last_per_eng.values():
            o.need_inc = True
        ctr = {e: 0 for e in ENGS}
        dctr = {(e, i): 0 for e in self.dma_sem for i in range(NDMASEM)}
        for o in self.ops:
            if o.dma:
                dctr[(o.eng, o.slot)] += 16
                o.sem_val = dctr[(o.eng, o.slot)]
            elif o.need_inc:
                ctr[o.eng] += 1
                o.sem_val = ctr[o.eng]
        for e, c in ctr.items():
            if c >= 60000:
                raise RuntimeError(f"semaphore range risk on {e}: {c}")
        for k, c in dctr.items():
            if c >= 60000:
                raise RuntimeError(f"semaphore range risk on {k}: {c}")
        finals = []
        for (e, i), c in dctr.items():
            if c:
                finals.append((self.dma_sem[e][i], c))
        for e, c in ctr.items():
            if c and e != final_engine:
                finals.append((self.eng_sem[e], c))
        nc = self.nc
        with nc.Block() as block:
            for eng_name in ENGS:
                items = [o for o in self.ops if o.eng == eng_name]
                if not items and eng_name != final_engine:
                    continue

                def make(items, eng_name=eng_name):
                    def f(eng):
                        hw = {}
                        for o in items:
                            for d in o.deps:
                                if d.dma:
                                    sem = self.dma_sem[d.eng][d.slot]
                                else:
                                    sem = self.eng_sem[d.eng]
                                key = id(sem)
                                if hw.get(key, 0) < d.sem_val:
                                    eng.wait_ge(sem, d.sem_val)
                                    hw[key] = d.sem_val
                            if o.dma and o.sem_val > 16:
                                sem = self.dma_sem[o.eng][o.slot]
                                key = id(sem)
                                if hw.get(key, 0) < o.sem_val - 16:
                                    eng.wait_ge(sem, o.sem_val - 16)
                                    hw[key] = o.sem_val - 16
                            ins = o.fn(eng)
                            if o.need_inc:
                                if o.dma:
                                    ins.then_inc(self.dma_sem[o.eng][o.slot], 16)
                                else:
                                    ins.then_inc(self.eng_sem[o.eng], 1)
                        if eng_name == final_engine:
                            for sem, v in finals:
                                eng.wait_ge(sem, v)
                    return f

                getattr(block, eng_name)(make(items))
        return ({e: sum(1 for o in self.ops if o.eng == e) for e in ENGS},
                ctr, dctr)


_uid = [0]


def _nm(b):
    _uid[0] += 1
    return f"{b}_{_uid[0]}"


def build_nc(debug_mid=False, phases=("ffn1", "mamba", "ffn2")):
    nc = bass.Bass("TRN2", target_bir_lowering=False)

    xT = nc.dram_tensor("xT", [DM, LTOK], F32, kind="ExternalInput")
    wgT = nc.dram_tensor("wgT", [DM, FFNH], F32, kind="ExternalInput")
    wuT = nc.dram_tensor("wuT", [DM, FFNH], F32, kind="ExternalInput")
    wdT = nc.dram_tensor("wdT", [FFNH, DM], F32, kind="ExternalInput")
    winT = nc.dram_tensor("winT", [DM, PROJ], F32, kind="ExternalInput")
    woT = nc.dram_tensor("woT", [INTER, DM], F32, kind="ExternalInput")
    cw = nc.dram_tensor("cw", [CONVD, 4], F32, kind="ExternalInput")
    cb = nc.dram_tensor("cb", [CONVD], F32, kind="ExternalInput")
    dtb = nc.dram_tensor("dtb", [HEADS], F32, kind="ExternalInput")
    alog = nc.dram_tensor("alog", [HEADS], F32, kind="ExternalInput")
    dvec = nc.dram_tensor("dvec", [HEADS], F32, kind="ExternalInput")
    nw = nc.dram_tensor("nw", [INTER], F32, kind="ExternalInput")
    outT = nc.dram_tensor("outT", [DM, LTOK], F32, kind="ExternalOutput")

    mid = "ExternalOutput" if debug_mid else "Internal"
    gsc = nc.dram_tensor("gsc", [128, NGH * 16 * GH], BF, kind="Internal")
    usc = nc.dram_tensor("usc", [128, NGH * 16 * GH], BF, kind="Internal")
    dsc = nc.dram_tensor("dsc", [128, 16 * 8192], BF, kind="Internal")
    isc = nc.dram_tensor("isc", [128, 32 * 4096 + 3072], BF, kind="Internal")
    osc = nc.dram_tensor("osc", [128, 8 * 8192], BF, kind="Internal")
    X1 = nc.dram_tensor("X1", [DM, LTOK], F32, kind=mid)
    X2 = nc.dram_tensor("X2", [DM, LTOK], F32, kind=mid)

    CONSTS = [LN_EPS, RMS_EPS, -0.5, -1.0, 1.0, 0.5, 1.0 / DM, 1.0 / INTER]

    es = contextlib.ExitStack()
    with es:
        S = es.enter_context
        sch = Sched(nc, es)

        cst = S(nc.sbuf_tensor("cst", [128, len(CONSTS)], F32))
        for i, cv in enumerate(CONSTS):
            nc.const_aps.register(F32, float(cv), cst[:, i:i + 1])

        ones_col_f = S(nc.sbuf_tensor("ones_col_f", [128, 1], F32))
        ones_col_b = S(nc.sbuf_tensor("ones_col_b", [128, 1], BF))
        ones_row_f = S(nc.sbuf_tensor("ones_row_f", [1, 128], F32))
        I_f = S(nc.sbuf_tensor("I_f", [128, 128], F32))
        I_b = S(nc.sbuf_tensor("I_b", [128, 128], BF))
        LT = S(nc.sbuf_tensor("LT", [128, 128], F32))
        US_b = S(nc.sbuf_tensor("US_b", [128, 128], BF))
        p_cw = S(nc.sbuf_tensor("p_cw", [128, 33 * 4], F32))
        p_cb = S(nc.sbuf_tensor("p_cb", [128, 33], F32))
        p_nw = S(nc.sbuf_tensor("p_nw", [128, 32], F32))
        p_Abc = S(nc.sbuf_tensor("p_Abc", [128, 64], F32))
        p_dtbbc = S(nc.sbuf_tensor("p_dtbbc", [128, 64], F32))
        p_Dbc = S(nc.sbuf_tensor("p_Dbc", [128, 64], F32))
        prow = S(nc.sbuf_tensor("prow", [1, 192], F32))
        arow = S(nc.sbuf_tensor("arow", [1, 192], F32))
        pstage = S(nc.sbuf_tensor("pstage", [33, 128], F32))
        nstage = S(nc.sbuf_tensor("nstage", [32, 128], F32))

        PS = [S(nc.psum_tensor(f"ps{k}", [128, 512], F32)) for k in range(8)]
        PSB = [p.ap().bitcast(BF) for p in PS]
        psk = [f"ps{k}" for k in range(8)]

        def set_consts(e):
            ins = None
            for i, cv in enumerate(CONSTS):
                ins = e.memset(cst[:, i:i + 1], float(cv))
            return ins
        sch.op("gpsimd", set_consts, writes=["cst"])

        def g_masks(e):
            e.memset(ones_col_f[:], 1.0)
            e.memset(ones_col_b[:], 1.0)
            e.memset(ones_row_f[:], 1.0)
            e.memset(I_f[:], 1.0)
            e.affine_select(out=I_f[:], in_=I_f[:], compare_op=ALU.is_equal,
                            fill=0.0, base=0, pattern=[[1, 128]], channel_multiplier=-1)
            e.memset(I_b[:], 1.0)
            e.affine_select(out=I_b[:], in_=I_b[:], compare_op=ALU.is_equal,
                            fill=0.0, base=0, pattern=[[1, 128]], channel_multiplier=-1)
            e.memset(LT[:], 1.0)
            e.affine_select(out=LT[:], in_=LT[:], compare_op=ALU.is_ge,
                            fill=0.0, base=0, pattern=[[1, 128]], channel_multiplier=-1)
            e.memset(US_b[:], 1.0)
            return e.affine_select(out=US_b[:], in_=US_b[:], compare_op=ALU.is_ge,
                                   fill=0.0, base=-1, pattern=[[-1, 128]],
                                   channel_multiplier=1)
        sch.op("gpsimd", g_masks, writes=["masks"])

        sch.op("sync", lambda e: e.dma_start(
            prow[:, 0:64], alog.ap().rearrange("(o h) -> o h", o=1)),
            writes=["prow"], dma=True)
        sch.op("sync", lambda e: e.dma_start(
            prow[:, 64:128], dtb.ap().rearrange("(o h) -> o h", o=1)),
            writes=["prow"], dma=True)
        sch.op("sync", lambda e: e.dma_start(
            prow[:, 128:192], dvec.ap().rearrange("(o h) -> o h", o=1)),
            writes=["prow"], dma=True)
        sch.op("sync", lambda e: e.dma_start(
            pstage[:], cb.ap().rearrange("(c q) -> c q", q=128)),
            writes=["pstage"], dma=True)
        sch.op("sync", lambda e: e.dma_start(
            nstage[:], nw.ap().rearrange("(c q) -> c q", q=128)),
            writes=["nstage"], dma=True)
        sch.op("sync", lambda e: e.dma_start(
            p_cw.ap().rearrange("p (c k) -> p c k", k=4),
            cw.ap().rearrange("(c p) k -> p c k", p=128)),
            writes=["p_cw"], dma=True)

        sch.op("scalar", lambda e: e.activation(arow[:, 0:64], prow[:, 0:64], AF.Exp),
               reads=["prow"], writes=["arow"])
        sch.op("vector", lambda e: e.tensor_scalar_mul(arow[:, 0:64], arow[:, 0:64], -1.0),
               reads=["cst"], writes=["arow"])
        sch.op("vector", lambda e: e.tensor_copy(arow[:, 64:192], prow[:, 64:192]),
               reads=["prow"], writes=["arow"])

        def bc3(e):
            e.matmul(PS[0][:, 0:64], ones_row_f[:], arow[:, 0:64], start=True, stop=True)
            e.matmul(PS[0][:, 64:128], ones_row_f[:], arow[:, 64:128], start=True, stop=True)
            return e.matmul(PS[0][:, 128:192], ones_row_f[:], arow[:, 128:192],
                            start=True, stop=True)
        sch.op("tensor", bc3, reads=["arow", "masks"], writes=["ps0"])
        sch.op("vector", lambda e: e.tensor_copy(p_Abc[:], PS[0][:, 0:64]),
               reads=["ps0"], writes=["p_Abc"])
        sch.op("vector", lambda e: e.tensor_copy(p_dtbbc[:], PS[0][:, 64:128]),
               reads=["ps0"], writes=["p_dtbbc"])
        sch.op("vector", lambda e: e.tensor_copy(p_Dbc[:], PS[0][:, 128:192]),
               reads=["ps0"], writes=["p_Dbc"])

        def tr2(e):
            e.transpose(PS[1][:, 0:33], pstage[:], I_f[0:33, 0:33])
            return e.transpose(PS[1][:, 64:96], nstage[:], I_f[0:32, 0:32])
        sch.op("tensor", tr2, reads=["pstage", "nstage", "masks"], writes=["ps1"])
        sch.op("vector", lambda e: e.tensor_copy(p_cb[:], PS[1][:, 0:33]),
               reads=["ps1"], writes=["p_cb"])
        sch.op("vector", lambda e: e.tensor_copy(p_nw[:], PS[1][:, 64:96]),
               reads=["ps1"], writes=["p_nw"])

        conv_items = []

        def add_wgu(g):
            conv_items.append((f"gsc:{g}",
                lambda g=g: wgT.ap()[:, g * GH:(g + 1) * GH]
                    .rearrange("(c p) h -> p c h", p=128),
                lambda g=g: gsc.ap()[:, g * 4096:(g + 1) * 4096]
                    .rearrange("p (c h) -> p c h", h=GH)))
            conv_items.append((f"usc:{g}",
                lambda g=g: wuT.ap()[:, g * GH:(g + 1) * GH]
                    .rearrange("(c p) h -> p c h", p=128),
                lambda g=g: usc.ap()[:, g * 4096:(g + 1) * 4096]
                    .rearrange("p (c h) -> p c h", h=GH)))

        for g in range(NGH):
            add_wgu(g)
        for gg in range(16):
            for hh in range(2):
                conv_items.append((f"dsc:{gg}:{hh}",
                    lambda gg=gg, hh=hh: wdT.ap()[gg * 512 + hh * 256:
                                                  gg * 512 + hh * 256 + 256, :]
                        .rearrange("(hl p) d -> p hl d", p=128),
                    lambda gg=gg, hh=hh: dsc.ap()[:, gg * 8192 + hh * 4096:
                                                  gg * 8192 + hh * 4096 + 4096]
                        .rearrange("p (hl d) -> p hl d", d=2048)))
        for G in range(32):
            conv_items.append((f"isc:{G}",
                lambda G=G: winT.ap()[:, G * 256:(G + 1) * 256]
                    .rearrange("(c p) h -> p c h", p=128),
                lambda G=G: isc.ap()[:, G * 4096:(G + 1) * 4096]
                    .rearrange("p (c h) -> p c h", h=256)))
        conv_items.append(("isc:32",
            lambda: winT.ap()[:, 8192:8384].rearrange("(c p) h -> p c h", p=128),
            lambda: isc.ap()[:, 131072:131072 + 3072]
                .rearrange("p (c h) -> p c h", h=192)))
        for og in range(8):
            for ih in range(2):
                conv_items.append((f"osc:{og}:{ih}",
                    lambda og=og, ih=ih: woT.ap()[og * 512 + ih * 256:
                                                  og * 512 + ih * 256 + 256, :]
                        .rearrange("(il p) d -> p il d", p=128),
                    lambda og=og, ih=ih: osc.ap()[:, og * 8192 + ih * 4096:
                                                  og * 8192 + ih * 4096 + 4096]
                        .rearrange("p (il d) -> p il d", d=2048)))

        def emit_conversion():
            for key, src_fn, dst_fn in conv_items:
                sch.op("gpsimd", lambda e, src_fn=src_fn, dst_fn=dst_fn:
                       e.dma_start(dst_fn(), src_fn()),
                       writes=[key], dma=True)

        def emit_ln(T, resid, xln, sqbuf_fn, lnsc, rows, sq_keys,
                    rk="resid", xk="xln"):
            for dc in range(16):
                sch.op("vector", lambda e, dc=dc: e.tensor_copy(
                    xln[:, dc * T:(dc + 1) * T], resid[:, dc * T:(dc + 1) * T]),
                    reads=[rk], writes=[xk])
            for dc in range(16):
                sch.op("scalar", lambda e, dc=dc: e.activation(
                    sqbuf_fn(dc), resid[:, dc * T:(dc + 1) * T], AF.Square),
                    reads=[rk], writes=[sq_keys[dc]])

            def s1(e):
                for dc in range(16):
                    ins = e.matmul(PS[0][0:1, 0:T], ones_col_b[:],
                                   xln[:, dc * T:(dc + 1) * T],
                                   start=(dc == 0), stop=(dc == 15))
                return ins
            sch.op("tensor", s1, reads=[xk, "masks"], writes=["ps0"])

            def s2(e):
                for dc in range(16):
                    ins = e.matmul(PS[1][0:1, 0:T], ones_col_b[:], sqbuf_fn(dc),
                                   start=(dc == 0), stop=(dc == 15))
                return ins
            sch.op("tensor", s2, reads=list(sq_keys) + ["masks"], writes=["ps1"])

            m_r = rows[:, 0:T]
            t_r = rows[:, 512:512 + T]
            sch.op("scalar", lambda e: e.mul(m_r, PS[0][0:1, 0:T], 1.0 / DM),
                   reads=["ps0", "cst"], writes=["rows_m"])
            sch.op("scalar", lambda e: e.activation(t_r, m_r, AF.Square),
                   reads=["rows_m"], writes=["rows_t"])
            sch.op("vector", lambda e: e.scalar_tensor_tensor(
                t_r, PS[1][0:1, 0:T], 1.0 / DM, t_r, op0=ALU.mult, op1=ALU.subtract),
                reads=["ps1", "rows_t", "cst"], writes=["rows_t"])
            sch.op("scalar", lambda e: e.activation(t_r, t_r, AF.Ln, bias=LN_EPS),
                   reads=["rows_t", "cst"], writes=["rows_t"])
            sch.op("scalar", lambda e: e.activation(t_r, t_r, AF.Exp, scale=-0.5),
                   reads=["rows_t", "cst"], writes=["rows_t"])
            sch.op("vector", lambda e: e.tensor_mul(m_r, m_r, t_r),
                   reads=["rows_m", "rows_t"], writes=["rows_m"])
            sch.op("tensor", lambda e: e.matmul(PS[2][:, 0:T], ones_row_f[:], t_r,
                                                start=True, stop=True),
                   reads=["rows_t", "masks"], writes=["ps2"])
            sch.op("tensor", lambda e: e.matmul(PS[3][:, 0:T], ones_row_f[:], m_r,
                                                start=True, stop=True),
                   reads=["rows_m", "masks"], writes=["ps3"])
            for dc in range(16):
                sch.op("vector", lambda e, dc=dc: e.tensor_mul(
                    lnsc[:, 0:T], resid[:, dc * T:(dc + 1) * T], PS[2][:, 0:T]),
                    reads=[rk, "ps2"], writes=["lnsc"])
                sch.op("vector", lambda e, dc=dc: e.tensor_sub(
                    xln[:, dc * T:(dc + 1) * T], lnsc[:, 0:T], PS[3][:, 0:T]),
                    reads=["lnsc", "ps3"], writes=[xk])

        def emit_ffn(src, dst, do_conversion):
            with contextlib.ExitStack() as fs:
                FS = fs.enter_context
                resid = FS(nc.sbuf_tensor(_nm("f_resid"), [128, 16 * TFF], F32))
                xln = FS(nc.sbuf_tensor(_nm("f_xln"), [128, 16 * TFF], BF))
                actb = FS(nc.sbuf_tensor(_nm("f_act"), [128, 64 * TFF], BF))
                wgb = FS(nc.sbuf_tensor(_nm("f_wg"), [128, 2 * 4096], BF))
                wub = FS(nc.sbuf_tensor(_nm("f_wu"), [128, 2 * 4096], BF))
                wdb = FS(nc.sbuf_tensor(_nm("f_wd"), [128, 2 * 4096], BF))
                lnsc = FS(nc.sbuf_tensor(_nm("f_lnsc"), [128, TFF], F32))
                rows = FS(nc.sbuf_tensor(_nm("f_rows"), [1, 1024], F32))

                if do_conversion:
                    emit_conversion()

                for blk in range(NFB):
                    t0 = blk * TFF
                    sch.op("scalar", lambda e, t0=t0: e.dma_start(
                        resid.ap().rearrange("p (c t) -> p c t", t=TFF),
                        src.ap()[:, t0:t0 + TFF].rearrange("(c p) t -> p c t", p=128)),
                        writes=["resid"], dma=True)

                    emit_ln(TFF, resid, xln,
                            lambda dc: actb[:, dc * TFF:(dc + 1) * TFF], lnsc, rows,
                            [f"act:{dc}" for dc in range(16)])

                    for g in range(NGH):
                        par = g % 2
                        sch.op("sync", lambda e, g=g, par=par: e.dma_start(
                            wgb[:, par * 4096:(par + 1) * 4096],
                            gsc.ap()[:, g * 4096:(g + 1) * 4096]),
                            reads=[f"gsc:{g}"], writes=[f"wgb:{par}"], dma=True)
                        sch.op("sync", lambda e, g=g, par=par: e.dma_start(
                            wub[:, par * 4096:(par + 1) * 4096],
                            usc.ap()[:, g * 4096:(g + 1) * 4096]),
                            reads=[f"usc:{g}"], writes=[f"wub:{par}"], dma=True)
                        for hl in range(2):
                            ht = g * 2 + hl
                            pp = 4 + ht % 2

                            def gmm(e, par=par, hl=hl, pp=pp):
                                for dc in range(16):
                                    ins = e.matmul(
                                        PS[pp][:],
                                        wgb[:, par * 4096 + dc * GH + hl * 128:
                                            par * 4096 + dc * GH + hl * 128 + 128],
                                        xln[:, dc * TFF:(dc + 1) * TFF],
                                        start=(dc == 0), stop=(dc == 15))
                                return ins
                            sch.op("tensor", gmm, reads=[f"wgb:{par}", "xln"],
                                   writes=[psk[pp]])
                            sch.op("scalar", lambda e, ht=ht, pp=pp: e.activation(
                                actb[:, ht * TFF:(ht + 1) * TFF], PS[pp][:], AF.Silu),
                                reads=[psk[pp]], writes=[f"act:{ht}"])

                            pq = 6 + ht % 2

                            def umm(e, par=par, hl=hl, pq=pq):
                                for dc in range(16):
                                    ins = e.matmul(
                                        PS[pq][:],
                                        wub[:, par * 4096 + dc * GH + hl * 128:
                                            par * 4096 + dc * GH + hl * 128 + 128],
                                        xln[:, dc * TFF:(dc + 1) * TFF],
                                        start=(dc == 0), stop=(dc == 15))
                                return ins
                            sch.op("tensor", umm, reads=[f"wub:{par}", "xln"],
                                   writes=[psk[pq]])
                            sch.op("vector", lambda e, ht=ht, pq=pq: e.scalar_tensor_tensor(
                                actb[:, ht * TFF:(ht + 1) * TFF],
                                actb[:, ht * TFF:(ht + 1) * TFF], 0.5, PS[pq][:],
                                op0=ALU.mult, op1=ALU.mult),
                                reads=[f"act:{ht}", psk[pq], "cst"], writes=[f"act:{ht}"])

                    for hh in range(2):
                        for gg in range(16):
                            par = gg % 2
                            sch.op("sync", lambda e, gg=gg, hh=hh, par=par: e.dma_start(
                                wdb.ap()[:, par * 4096:(par + 1) * 4096]
                                .rearrange("p (hl d) -> p hl d", d=1024),
                                dsc.ap()[:, gg * 8192:(gg + 1) * 8192]
                                .rearrange("p (hl d) -> p hl d", d=2048)
                                [:, :, hh * 1024:(hh + 1) * 1024]),
                                reads=[f"dsc:{gg}:0", f"dsc:{gg}:1"],
                                writes=[f"wdb:{par}"], dma=True)
                            for oc8 in range(8):
                                def dmm(e, gg=gg, par=par, oc8=oc8):
                                    for hl in range(4):
                                        ins = e.matmul(
                                            PS[oc8][:],
                                            wdb[:, par * 4096 + hl * 1024 + oc8 * 128:
                                                par * 4096 + hl * 1024 + oc8 * 128 + 128],
                                            actb[:, (gg * 4 + hl) * TFF:
                                                 (gg * 4 + hl + 1) * TFF],
                                            start=(gg == 0 and hl == 0),
                                            stop=(gg == 15 and hl == 3))
                                    return ins
                                sch.op("tensor", dmm,
                                       reads=[f"wdb:{par}"] +
                                             [f"act:{gg * 4 + i}" for i in range(4)],
                                       writes=[psk[oc8]])
                        for oc8 in range(8):
                            oc = hh * 8 + oc8
                            sch.op("vector", lambda e, oc=oc, oc8=oc8: e.tensor_add(
                                resid[:, oc * TFF:(oc + 1) * TFF],
                                resid[:, oc * TFF:(oc + 1) * TFF], PS[oc8][:]),
                                reads=[psk[oc8]], writes=["resid"])

                    sch.op("scalar", lambda e, t0=t0: e.dma_start(
                        dst.ap()[:, t0:t0 + TFF].rearrange("(c p) t -> p c t", p=128),
                        resid.ap().rearrange("p (c t) -> p c t", t=TFF)),
                        reads=["resid"], dma=True)

        def emit_mamba(src, dst):
            with contextlib.ExitStack() as ms:
                MS = ms.enter_context
                residb = MS(nc.sbuf_tensor(_nm("m_resid"), [128, 2 * 16 * TM], F32))
                xlnb = MS(nc.sbuf_tensor(_nm("m_xln"), [128, 2 * 16 * TM], BF))
                wib = MS(nc.sbuf_tensor(_nm("m_wi"), [128, 3 * 4096], BF))
                wob = MS(nc.sbuf_tensor(_nm("m_wo"), [128, 2 * 8192], BF))
                sg = MS(nc.sbuf_tensor(_nm("m_sg"), [128, 32 * TM], BF))
                cx = MS(nc.sbuf_tensor(_nm("m_cx"), [128, 32 * TM], BF))
                cB = MS(nc.sbuf_tensor(_nm("m_cB"), [64, TM], BF))
                cC = MS(nc.sbuf_tensor(_nm("m_cC"), [64, TM], BF))
                dtf = MS(nc.sbuf_tensor(_nm("m_dtf"), [64, TM], F32))
                cin = MS(nc.sbuf_tensor(_nm("m_cin"), [128, 2 * (TM + 3)], F32))
                cacc = MS(nc.sbuf_tensor(_nm("m_cacc"), [128, 2 * TM], F32))
                halo = MS(nc.sbuf_tensor(_nm("m_halo"), [128, 99], F32))
                Sf = MS(nc.sbuf_tensor(_nm("m_Sf"), [64, HEADS * PD], F32))
                Sb = MS(nc.sbuf_tensor(_nm("m_Sb"), [64, HEADS * PD], BF))
                xst = MS(nc.sbuf_tensor(_nm("m_xst"), [128, 32 * 128], BF))
                sp1 = MS(nc.sbuf_tensor(_nm("m_sp1"), [128, 2 * 64], F32))
                sp2 = MS(nc.sbuf_tensor(_nm("m_sp2"), [128, 2 * 64], F32))
                dtp = MS(nc.sbuf_tensor(_nm("m_dtp"), [128, 2 * 64], F32))
                a_tm = MS(nc.sbuf_tensor(_nm("m_a"), [128, 2 * 64], F32))
                c_sb = MS(nc.sbuf_tensor(_nm("m_c"), [128, 2 * 64], F32))
                cq_sb = MS(nc.sbuf_tensor(_nm("m_cq"), [1, 2 * 64], F32))
                wdec = MS(nc.sbuf_tensor(_nm("m_wdec"), [128, 2 * 64], F32))
                wdt = MS(nc.sbuf_tensor(_nm("m_wdt"), [128, 2 * 64], F32))
                ec_sb = MS(nc.sbuf_tensor(_nm("m_ec"), [128, 2 * 64], F32))
                Ebc = MS(nc.sbuf_tensor(_nm("m_Ebc"), [64, 2 * 64], F32))
                Gt_sb = MS(nc.sbuf_tensor(_nm("m_Gt"), [128, 2 * 128], F32))
                BCt = MS(nc.sbuf_tensor(_nm("m_BCt"), [128, 2 * 64], BF))
                ATq = MS(nc.sbuf_tensor(_nm("m_ATq"), [128, 2 * 512], BF))
                expmq = MS(nc.sbuf_tensor(_nm("m_expq"), [128, 2 * 512], BF))
                GmTq = MS(nc.sbuf_tensor(_nm("m_GmTq"), [128, 2 * 512], BF))
                Bdq = MS(nc.sbuf_tensor(_nm("m_Bdq"), [128, 2 * 256], BF))
                y2b = MS(nc.sbuf_tensor(_nm("m_y2"), [128, 2 * 256], F32))
                xdt = MS(nc.sbuf_tensor(_nm("m_xdt"), [128, 2 * 256], BF))
                ybuf2 = MS(nc.sbuf_tensor(_nm("m_yb2"), [128, 2 * 256], F32))
                tmpS = MS(nc.sbuf_tensor(_nm("m_tmpS"), [64, 256], F32))
                ssqp = MS(nc.sbuf_tensor(_nm("m_ssqp"), [128, 16], F32))
                ssq_tm = MS(nc.sbuf_tensor(_nm("m_ssqtm"), [128, NQ], F32))
                rstd_row = MS(nc.sbuf_tensor(_nm("m_rsr"), [1, TM], F32))
                rstd_bc = MS(nc.sbuf_tensor(_nm("m_rsb"), [128, TM], F32))
                lnsc = MS(nc.sbuf_tensor(_nm("m_lnsc"), [128, TM], F32))
                rows = MS(nc.sbuf_tensor(_nm("m_rows"), [1, 1024], F32))

                def g_init(e):
                    e.memset(halo[:], 0.0)
                    e.memset(Sf[:], 0.0)
                    return e.memset(Sb[:], 0.0)
                sch.op("gpsimd", g_init,
                       writes=[f"halo:{c}" for c in range(33)] +
                              [f"Sf:{hq}" for hq in range(16)] +
                              [f"Sb:{hq}" for hq in range(16)])

                def rep4(ap2d, nfree):
                    return ap2d.rearrange("p (o n) -> p o n", o=1) \
                               .broadcast_to([ap2d.shape[0], 4, nfree])

                def col4(ap2d, nfree):
                    return ap2d.rearrange("p (h o) -> p h o", o=1) \
                               .broadcast_to([ap2d.shape[0], 4, nfree])

                def emit_prep(q):
                    qa = q * QS
                    o = q * 64
                    sch.op("tensor", lambda e, qa=qa: e.transpose(
                        PS[6][:, 0:64], dtf[:, qa:qa + QS], I_f[0:64, 0:64]),
                        reads=["dtf", "masks"], writes=["ps6"])
                    sch.op("vector", lambda e, o=o: e.tensor_copy(
                        sp1[:, o:o + 64], PS[6][:, 0:64]),
                        reads=["ps6"], writes=[f"sp1:{q}"])
                    sch.op("gpsimd", lambda e, o=o: e.tensor_add(
                        sp1[:, o:o + 64], sp1[:, o:o + 64], p_dtbbc[:]),
                        reads=["p_dtbbc", f"sp1:{q}"], writes=[f"sp1:{q}"])
                    sch.op("scalar", lambda e, o=o: e.activation(
                        sp2[:, o:o + 64], sp1[:, o:o + 64], AF.Abs),
                        reads=[f"sp1:{q}"], writes=[f"sp2:{q}"])
                    sch.op("scalar", lambda e, o=o: e.activation(
                        sp2[:, o:o + 64], sp2[:, o:o + 64], AF.Exp, scale=-1.0),
                        reads=[f"sp2:{q}", "cst"], writes=[f"sp2:{q}"])
                    sch.op("scalar", lambda e, o=o: e.activation(
                        sp2[:, o:o + 64], sp2[:, o:o + 64], AF.Ln, bias=1.0),
                        reads=[f"sp2:{q}", "cst"], writes=[f"sp2:{q}"])
                    sch.op("gpsimd", lambda e, o=o: e.tensor_scalar_max(
                        dtp[:, o:o + 64], sp1[:, o:o + 64], 0.0),
                        reads=[f"sp1:{q}"], writes=[f"dtp:{q}"])
                    sch.op("gpsimd", lambda e, o=o: e.tensor_add(
                        dtp[:, o:o + 64], dtp[:, o:o + 64], sp2[:, o:o + 64]),
                        reads=[f"sp2:{q}", f"dtp:{q}"], writes=[f"dtp:{q}"])
                    sch.op("gpsimd", lambda e, o=o: e.tensor_mul(
                        a_tm[:, o:o + 64], dtp[:, o:o + 64], p_Abc[:]),
                        reads=[f"dtp:{q}", "p_Abc"], writes=[f"a_tm:{q}"])
                    sch.op("tensor", lambda e, o=o: e.matmul(
                        PS[0][:, 0:64], LT[:], a_tm[:, o:o + 64],
                        start=True, stop=True),
                        reads=[f"a_tm:{q}", "masks"], writes=["ps0"])
                    sch.op("tensor", lambda e, o=o: e.matmul(
                        PS[1][0:1, 0:64], ones_col_f[:], a_tm[:, o:o + 64],
                        start=True, stop=True),
                        reads=[f"a_tm:{q}", "masks"], writes=["ps1"])
                    sch.op("vector", lambda e, o=o: e.tensor_copy(
                        c_sb[:, o:o + 64], PS[0][:, 0:64]),
                        reads=["ps0"], writes=[f"c_sb:{q}"])
                    sch.op("vector", lambda e, o=o: e.tensor_copy(
                        cq_sb[:, o:o + 64], PS[1][0:1, 0:64]),
                        reads=["ps1"], writes=[f"cq_sb:{q}"])
                    sch.op("tensor", lambda e, o=o: e.matmul(
                        PS[2][:, 0:64], ones_row_f[:], cq_sb[:, o:o + 64],
                        start=True, stop=True),
                        reads=[f"cq_sb:{q}", "masks"], writes=["ps2"])
                    sch.op("tensor", lambda e, o=o: e.matmul(
                        PS[3][0:64, 0:64], ones_row_f[:, 0:64], cq_sb[:, o:o + 64],
                        start=True, stop=True),
                        reads=[f"cq_sb:{q}", "masks"], writes=["ps3"])
                    sch.op("vector", lambda e, o=o: e.tensor_sub(
                        wdec[:, o:o + 64], PS[2][:, 0:64], c_sb[:, o:o + 64]),
                        reads=["ps2", f"c_sb:{q}"], writes=[f"wdec:{q}"])
                    sch.op("scalar", lambda e, o=o: e.activation(
                        wdec[:, o:o + 64], wdec[:, o:o + 64], AF.Exp),
                        reads=[f"wdec:{q}"], writes=[f"wdec:{q}"])
                    sch.op("scalar", lambda e, o=o: e.activation(
                        ec_sb[:, o:o + 64], c_sb[:, o:o + 64], AF.Exp),
                        reads=[f"c_sb:{q}"], writes=[f"ec_sb:{q}"])
                    sch.op("scalar", lambda e, o=o: e.activation(
                        Ebc[:, o:o + 64], PS[3][0:64, 0:64], AF.Exp),
                        reads=["ps3"], writes=[f"Ebc:{q}"])
                    sch.op("vector", lambda e, o=o: e.tensor_mul(
                        wdt[:, o:o + 64], wdec[:, o:o + 64], dtp[:, o:o + 64]),
                        reads=[f"wdec:{q}", f"dtp:{q}"], writes=[f"wdt:{q}"])
                    sch.op("tensor", lambda e, qa=qa, o=o: e.matmul(
                        PS[6][:, 128:256], cB[:, qa:qa + QS], cC[:, qa:qa + QS],
                        start=True, stop=True),
                        reads=["cB", "cC", "masks"], writes=["ps6"])
                    sch.op("vector", lambda e, q=q: e.tensor_mul(
                        Gt_sb[:, q * 128:(q + 1) * 128], PS[6][:, 128:256], LT[:]),
                        reads=["ps6", "masks"], writes=[f"Gt_sb:{q}"])
                    sch.op("tensor", lambda e, qa=qa: e.transpose(
                        PSB[7][:, 0:64], cB[:, qa:qa + QS], I_b[0:64, 0:64]),
                        reads=["cB", "masks"], writes=["ps7"])
                    sch.op("vector", lambda e, o=o: e.tensor_copy(
                        BCt[:, o:o + 64], PSB[7][:, 0:64]),
                        reads=["ps7"], writes=[f"BCt:{q}"])

                pin_ctr = [0]
                for mt in range(NMT):
                    t0 = mt * TM
                    pmt = mt % 2
                    ro = pmt * 16 * TM
                    resid = residb[:, ro:ro + 16 * TM]
                    xln = xlnb[:, ro:ro + 16 * TM]
                    rk = f"resid:{pmt}"
                    xk = f"xln:{pmt}"
                    sch.op("scalar", lambda e, t0=t0, ro=ro: e.dma_start(
                        residb.ap()[:, ro:ro + 16 * TM]
                        .rearrange("p (c t) -> p c t", t=TM),
                        src.ap()[:, t0:t0 + TM].rearrange("(c p) t -> p c t", p=128)),
                        writes=[rk], dma=True)

                    emit_ln(TM, resid, xln,
                            lambda dc: sg[:, dc * TM:(dc + 1) * TM], lnsc, rows,
                            [f"sg:{dc}" for dc in range(16)], rk=rk, xk=xk)

                    for gpi, G in enumerate([32] + list(range(32))):
                        par = gpi % 3
                        nw_ = 4096 if G < 32 else 3072
                        fcs = [2 * G, 2 * G + 1] if G < 32 else [64, 65]
                        sch.op("sync", lambda e, G=G, par=par, nw_=nw_: e.dma_start(
                            wib[:, par * 4096:par * 4096 + nw_],
                            isc.ap()[:, G * 4096:G * 4096 + nw_]),
                            reads=[f"isc:{G}"], writes=[f"wib:{par}"], dma=True)
                        for j, fc in enumerate(fcs):
                            pp = 4 + pin_ctr[0] % 3
                            pin_ctr[0] += 1
                            W = 256 if G < 32 else 192
                            base = par * 4096
                            coff = j * 128
                            mrows = 64 if fc == 65 else 128

                            def imm(e, base=base, W=W, coff=coff, mrows=mrows, pp=pp,
                                    xln=xln):
                                for dc in range(16):
                                    ins = e.matmul(
                                        PS[pp][0:mrows, 0:TM],
                                        wib[:, base + dc * W + coff:
                                            base + dc * W + coff + mrows],
                                        xln[:, dc * TM:(dc + 1) * TM],
                                        start=(dc == 0), stop=(dc == 15))
                                return ins
                            sch.op("tensor", imm, reads=[f"wib:{par}", xk],
                                   writes=[psk[pp]])

                            if fc < 32:
                                sch.op("scalar", lambda e, fc=fc, pp=pp: e.activation(
                                    sg[:, fc * TM:(fc + 1) * TM], PS[pp][:, 0:TM],
                                    AF.Silu),
                                    reads=[psk[pp]], writes=[f"sg:{fc}"])
                            elif fc < 65:
                                cc = fc - 32
                                cpar = cc % 2
                                tape = "vector"
                                cb0 = cpar * (TM + 3)
                                sch.op("gpsimd", lambda e, cc=cc, cb0=cb0: e.tensor_copy(
                                    cin[:, cb0:cb0 + 3], halo[:, cc * 3:cc * 3 + 3]),
                                    reads=[f"halo:{cc}"], writes=[f"cin:{cpar}"])
                                sch.op("scalar", lambda e, pp=pp, cb0=cb0: e.copy(
                                    cin[:, cb0 + 3:cb0 + 3 + TM], PS[pp][:, 0:TM]),
                                    reads=[psk[pp]], writes=[f"cin:{cpar}"])
                                sch.op("gpsimd", lambda e, cc=cc, cb0=cb0: e.tensor_copy(
                                    halo[:, cc * 3:cc * 3 + 3],
                                    cin[:, cb0 + TM:cb0 + TM + 3]),
                                    reads=[f"cin:{cpar}"], writes=[f"halo:{cc}"])
                                sch.op("gpsimd", lambda e, cc=cc, cpar=cpar, cb0=cb0:
                                       e.tensor_scalar(
                                    cacc[:, cpar * TM:(cpar + 1) * TM],
                                    cin[:, cb0:cb0 + TM],
                                    p_cw[:, cc * 4:cc * 4 + 1], None, op0=ALU.mult),
                                    reads=[f"cin:{cpar}", "p_cw"], writes=[f"cacc:{cpar}"])
                                for k in range(1, 4):
                                    sch.op(tape, lambda e, cc=cc, cpar=cpar, k=k, cb0=cb0:
                                           e.scalar_tensor_tensor(
                                        cacc[:, cpar * TM:(cpar + 1) * TM],
                                        cin[:, cb0 + k:cb0 + k + TM],
                                        p_cw[:, cc * 4 + k:cc * 4 + k + 1],
                                        cacc[:, cpar * TM:(cpar + 1) * TM],
                                        op0=ALU.mult, op1=ALU.add),
                                        reads=[f"cin:{cpar}", "p_cw", f"cacc:{cpar}"],
                                        writes=[f"cacc:{cpar}"])
                                if cc < 32:
                                    sch.op("scalar", lambda e, cc=cc, cpar=cpar:
                                           e.activation(
                                        cx[:, cc * TM:(cc + 1) * TM],
                                        cacc[:, cpar * TM:(cpar + 1) * TM], AF.Silu,
                                        bias=p_cb[:, cc:cc + 1]),
                                        reads=[f"cacc:{cpar}", "p_cb"],
                                        writes=[f"cx:{cc}"])
                                else:
                                    sch.op("scalar", lambda e, cpar=cpar: e.activation(
                                        cB[:, 0:TM],
                                        cacc[0:64, cpar * TM:cpar * TM + TM], AF.Silu,
                                        bias=p_cb[0:64, 32:33]),
                                        reads=[f"cacc:{cpar}", "p_cb"], writes=["cB"])
                                    sch.op("scalar", lambda e, cpar=cpar: e.activation(
                                        cC[:, 0:TM],
                                        cacc[64:128, cpar * TM:cpar * TM + TM], AF.Silu,
                                        bias=p_cb[64:128, 32:33]),
                                        reads=[f"cacc:{cpar}", "p_cb"], writes=["cC"])
                            else:
                                sch.op("vector", lambda e, pp=pp: e.tensor_copy(
                                    dtf[:, 0:TM], PS[pp][0:64, 0:TM]),
                                    reads=[psk[pp]], writes=["dtf"])
                        if G == 32:
                            emit_prep(0)
                            emit_prep(1)

                    for q in range(NQ):
                        qa = q * QS
                        qo = q * 64
                        for rnd in range(4):
                            def xtr(e, rnd=rnd, qa=qa):
                                for j in range(8):
                                    fc = rnd * 8 + j
                                    ins = e.transpose(
                                        PSB[7][:, j * 128:(j + 1) * 128],
                                        cx[:, fc * TM + qa:fc * TM + qa + QS], I_b[:])
                                return ins
                            sch.op("tensor", xtr,
                                   reads=[f"cx:{rnd * 8 + j}" for j in range(8)] + ["masks"],
                                   writes=["ps7"])
                            sch.op("vector", lambda e, rnd=rnd: e.tensor_copy(
                                xst[:, rnd * 1024:(rnd + 1) * 1024], PSB[7][:]),
                                reads=["ps7"], writes=["xst"])

                        for hq in range(16):
                            h0 = hq * 4
                            pq = hq % 2
                            aq = pq * 512
                            bq = pq * 256

                            sch.op("gpsimd", lambda e, h0=h0, aq=aq, qo=qo: e.tensor_tensor(
                                out=ATq[:, aq:aq + 512]
                                    .rearrange("p (h t) -> p h t", h=4),
                                in0=rep4(LT[:], 128),
                                in1=col4(a_tm[:, qo + h0:qo + h0 + 4], 128), op=ALU.mult),
                                reads=[f"a_tm:{q}", "masks"], writes=[f"ATq:{pq}"])

                            def margmm(e, aq=aq, pq=pq):
                                for k in range(4):
                                    ins = e.matmul(
                                        PS[pq][:, k * 128:(k + 1) * 128], US_b[:],
                                        ATq[:, aq + k * 128:aq + (k + 1) * 128],
                                        start=True, stop=True)
                                return ins
                            sch.op("tensor", margmm, reads=[f"ATq:{pq}", "masks"],
                                   writes=[psk[pq]])
                            sch.op("scalar", lambda e, aq=aq, pq=pq: e.activation(
                                expmq[:, aq:aq + 512], PS[pq][:, 0:512], AF.Exp),
                                reads=[psk[pq]], writes=[f"expm:{pq}"])
                            sch.op("vector", lambda e, aq=aq, q=q: e.tensor_tensor(
                                out=GmTq[:, aq:aq + 512]
                                    .rearrange("p (h t) -> p h t", h=4),
                                in0=expmq[:, aq:aq + 512]
                                    .rearrange("p (h t) -> p h t", h=4),
                                in1=rep4(Gt_sb[:, q * 128:(q + 1) * 128], 128),
                                op=ALU.mult),
                                reads=[f"expm:{pq}", f"Gt_sb:{q}"],
                                writes=[f"GmT:{pq}"])
                            sch.op("vector", lambda e, bq=bq, h0=h0, qo=qo:
                                   e.tensor_tensor(
                                out=xdt[:, bq:bq + 256]
                                    .rearrange("p (h d) -> p h d", h=4),
                                in0=xst[:, h0 * 64:(h0 + 4) * 64]
                                    .rearrange("p (h d) -> p h d", h=4),
                                in1=col4(dtp[:, qo + h0:qo + h0 + 4], 64),
                                op=ALU.mult),
                                reads=["xst", f"dtp:{q}"], writes=[f"xdt:{pq}"])
                            sch.op("gpsimd", lambda e, h0=h0, bq=bq, qo=qo:
                                   e.tensor_tensor(
                                out=Bdq[:, bq:bq + 256]
                                    .rearrange("p (h n) -> p h n", h=4),
                                in0=rep4(BCt[:, qo:qo + 64], 64),
                                in1=col4(wdt[:, qo + h0:qo + h0 + 4], 64), op=ALU.mult),
                                reads=[f"BCt:{q}", f"wdt:{q}"], writes=[f"Bdq:{pq}"])

                            def y1mm(e, aq=aq, pq=pq, bq=bq):
                                for k in range(4):
                                    ins = e.matmul(
                                        PS[2 + pq][:, k * 64:(k + 1) * 64],
                                        GmTq[:, aq + k * 128:aq + (k + 1) * 128],
                                        xdt[:, bq + k * 64:bq + (k + 1) * 64],
                                        start=True, stop=True)
                                return ins
                            sch.op("tensor", y1mm, reads=[f"GmT:{pq}", f"xdt:{pq}"],
                                   writes=[psk[2 + pq]])

                            def dsmm(e, bq=bq, h0=h0):
                                for k in range(4):
                                    ins = e.matmul(
                                        PS[5][0:64, k * 64:(k + 1) * 64],
                                        Bdq[:, bq + k * 64:bq + (k + 1) * 64],
                                        xst[:, (h0 + k) * 64:(h0 + k + 1) * 64],
                                        start=True, stop=True)
                                return ins
                            sch.op("tensor", dsmm, reads=[f"Bdq:{pq}", "xst"],
                                   writes=["ps5"])

                            for k2 in range(2):
                                bank = 4 if k2 == 0 else 7

                                def y2mm(e, k2=k2, bank=bank, h0=h0, qa=qa):
                                    for j in range(2):
                                        h = h0 + 2 * k2 + j
                                        ins = e.matmul(
                                            PS[bank][:, j * 64:(j + 1) * 64],
                                            cC[:, qa:qa + QS],
                                            Sb[:, h * 64:(h + 1) * 64],
                                            start=True, stop=True)
                                    return ins
                                sch.op("tensor", y2mm, reads=["cC", f"Sb:{hq}"],
                                       writes=[psk[bank]])

                            sch.op("vector", lambda e, bq=bq, h0=h0: e.tensor_tensor(
                                out=y2b[:, bq:bq + 256]
                                    .rearrange("p (h d) -> p h d", h=4),
                                in0=xst[:, h0 * 64:(h0 + 4) * 64]
                                    .rearrange("p (h d) -> p h d", h=4),
                                in1=col4(p_Dbc[:, h0:h0 + 4], 64), op=ALU.mult),
                                reads=["xst", "p_Dbc"], writes=[f"y2b:{pq}"])
                            sch.op("vector", lambda e, bq=bq, pq=pq: e.tensor_add(
                                y2b[:, bq:bq + 256], y2b[:, bq:bq + 256],
                                PS[2 + pq][:, 0:256]),
                                reads=[psk[2 + pq], f"y2b:{pq}"], writes=[f"y2b:{pq}"])
                            for k in range(4):
                                bank = 4 if k < 2 else 7
                                col = (k % 2) * 64
                                sch.op("scalar", lambda e, bq=bq, k=k, h0=h0,
                                       bank=bank, col=col, qo=qo: e.activation(
                                    ybuf2[:, bq + k * 64:bq + (k + 1) * 64],
                                    PS[bank][:, col:col + 64], AF.Copy,
                                    scale=ec_sb[:, qo + h0 + k:qo + h0 + k + 1]),
                                    reads=[psk[bank], f"ec_sb:{q}"],
                                    writes=[f"yb2:{pq}"])
                            sch.op("vector", lambda e, bq=bq: e.tensor_add(
                                y2b[:, bq:bq + 256], y2b[:, bq:bq + 256],
                                ybuf2[:, bq:bq + 256]),
                                reads=[f"yb2:{pq}", f"y2b:{pq}"], writes=[f"y2b:{pq}"])
                            sch.op("gpsimd", lambda e, h0=h0, qo=qo: e.tensor_tensor(
                                out=tmpS[:].rearrange("p (h d) -> p h d", h=4),
                                in0=Sf[:, h0 * 64:(h0 + 4) * 64]
                                    .rearrange("p (h d) -> p h d", h=4),
                                in1=col4(Ebc[:, qo + h0:qo + h0 + 4], 64), op=ALU.mult),
                                reads=[f"Sf:{hq}", f"Ebc:{q}"], writes=["tmpS"])
                            sch.op("vector", lambda e, h0=h0: e.tensor_add(
                                Sf[:, h0 * 64:(h0 + 4) * 64], tmpS[:],
                                PS[5][0:64, 0:256]),
                                reads=["tmpS", "ps5"], writes=[f"Sf:{hq}"])
                            sch.op("scalar", lambda e, h0=h0: e.copy(
                                Sb[:, h0 * 64:(h0 + 4) * 64],
                                Sf[:, h0 * 64:(h0 + 4) * 64]),
                                reads=[f"Sf:{hq}"], writes=[f"Sb:{hq}"])
                            sch.op("scalar", lambda e, bq=bq: e.activation(
                                ybuf2[:, bq:bq + 256], y2b[:, bq:bq + 256], AF.Square),
                                reads=[f"y2b:{pq}"], writes=[f"yb2:{pq}"])
                            sch.op("vector", lambda e, hq=hq, bq=bq: e.tensor_reduce(
                                ssqp[:, hq:hq + 1], ybuf2[:, bq:bq + 256],
                                mybir.AxisListType.X, ALU.add),
                                reads=[f"yb2:{pq}"], writes=["ssqp"])

                            def prtr(e, bq=bq):
                                for j2 in range(2):
                                    ins = e.transpose(
                                        PS[6][:, j2 * 128:(j2 + 1) * 128],
                                        y2b[:, bq + j2 * 128:bq + (j2 + 1) * 128],
                                        I_f[:])
                                return ins
                            sch.op("tensor", prtr, reads=[f"y2b:{pq}", "masks"],
                                   writes=["ps6"])
                            for j2 in range(2):
                                pr = hq * 2 + j2
                                sch.op("vector", lambda e, pr=pr, j2=j2, qa=qa:
                                       e.scalar_tensor_tensor(
                                    cx[:, pr * TM + qa:pr * TM + qa + QS],
                                    PS[6][:, j2 * 128:(j2 + 1) * 128],
                                    p_nw[:, pr:pr + 1],
                                    sg[:, pr * TM + qa:pr * TM + qa + QS],
                                    op0=ALU.mult, op1=ALU.mult),
                                    reads=["ps6", "p_nw", f"sg:{pr}"],
                                    writes=[f"cx:{pr}"])
                        sch.op("vector", lambda e, q=q: e.tensor_reduce(
                            ssq_tm[:, q:q + 1], ssqp[:], mybir.AxisListType.X, ALU.add),
                            reads=["ssqp"], writes=["ssq_tm"])

                    sch.op("gpsimd", lambda e: e.tensor_scalar(
                        ssq_tm[:], ssq_tm[:], 1.0 / INTER, None, op0=ALU.mult),
                        reads=["cst", "ssq_tm"], writes=["ssq_tm"])
                    sch.op("scalar", lambda e: e.activation(
                        ssq_tm[:], ssq_tm[:], AF.Ln, bias=RMS_EPS),
                        reads=["ssq_tm", "cst"], writes=["ssq_tm"])
                    sch.op("scalar", lambda e: e.activation(
                        ssq_tm[:], ssq_tm[:], AF.Exp, scale=-0.5),
                        reads=["ssq_tm", "cst"], writes=["ssq_tm"])

                    def rtr(e):
                        for q in range(NQ):
                            ins = e.transpose(PS[6][0:1, q * 128:(q + 1) * 128],
                                              ssq_tm[:, q:q + 1], I_f[:])
                        return ins
                    sch.op("tensor", rtr, reads=["ssq_tm", "masks"], writes=["ps6"])
                    sch.op("vector", lambda e: e.tensor_copy(
                        rstd_row[0:1, 0:TM], PS[6][0:1, 0:TM]),
                        reads=["ps6"], writes=["rstd_row"])
                    sch.op("tensor", lambda e: e.matmul(
                        PS[6][:, 0:TM], ones_row_f[:], rstd_row[0:1, 0:TM],
                        start=True, stop=True),
                        reads=["rstd_row", "masks"], writes=["ps6"])
                    sch.op("vector", lambda e: e.tensor_copy(rstd_bc[:], PS[6][:, 0:TM]),
                           reads=["ps6"], writes=["rstd_bc"])

                    for og in range(8):
                        par = og % 2
                        sch.op("sync", lambda e, og=og, par=par: e.dma_start(
                            wob[:, par * 8192:(par + 1) * 8192],
                            osc.ap()[:, og * 8192:(og + 1) * 8192]),
                            reads=[f"osc:{og}:0", f"osc:{og}:1"],
                            writes=[f"wob:{par}"], dma=True)

                        def omm(e, og=og, par=par):
                            for il in range(4):
                                for oc in range(16):
                                    ins = e.matmul(
                                        PS[oc // 2][0:128, (oc % 2) * 256:
                                                    (oc % 2) * 256 + 256],
                                        wob[:, par * 8192 + il * 2048 + oc * 128:
                                            par * 8192 + il * 2048 + oc * 128 + 128],
                                        cx[:, (og * 4 + il) * TM:(og * 4 + il + 1) * TM],
                                        start=(og == 0 and il == 0 and oc % 2 == 0),
                                        stop=(og == 7 and il == 3 and oc % 2 == 1),
                                        skip_group_check=True)
                            return ins
                        sch.op("tensor", omm,
                               reads=[f"wob:{par}"] +
                                     [f"cx:{og * 4 + i}" for i in range(4)],
                               writes=psk)
                    for oc in range(16):
                        op2 = oc % 2
                        sch.op("vector", lambda e, oc=oc, op2=op2: e.tensor_mul(
                            ybuf2[:, op2 * 256:op2 * 256 + 256],
                            PS[oc // 2][:, (oc % 2) * 256:(oc % 2) * 256 + 256],
                            rstd_bc[:]),
                            reads=[psk[oc // 2], "rstd_bc"], writes=[f"yb2:{op2}"])
                        sch.op("vector", lambda e, oc=oc, op2=op2, resid=resid:
                               e.tensor_add(
                            resid[:, oc * TM:(oc + 1) * TM],
                            resid[:, oc * TM:(oc + 1) * TM],
                            ybuf2[:, op2 * 256:op2 * 256 + 256]),
                            reads=[f"yb2:{op2}"], writes=[rk])

                    sch.op("scalar", lambda e, t0=t0, ro=ro: e.dma_start(
                        dst.ap()[:, t0:t0 + TM].rearrange("(c p) t -> p c t", p=128),
                        residb.ap()[:, ro:ro + 16 * TM]
                        .rearrange("p (c t) -> p c t", t=TM)),
                        reads=[rk], dma=True)

        if "ffn1" in phases:
            emit_ffn(xT, X1, do_conversion=True)
            sch.barrier()
        if "mamba" in phases:
            emit_mamba(X1, X2)
            sch.barrier()
        if "ffn2" in phases:
            emit_ffn(X2, outT, do_conversion="ffn1" not in phases)

        counts, ctr, dctr = sch.emit()
        print("op counts:", counts, "incs:", ctr, "dma:", dctr)
    return nc


_NC_CACHE = {}


def kernel(**inputs):
    x = np.ascontiguousarray(inputs["L0_attn_hidden_states"], dtype=np.float32)
    key = "full"
    if key not in _NC_CACHE:
        _NC_CACHE[key] = build_nc()
    nc = _NC_CACHE[key]
    base = {
        "wgT": np.ascontiguousarray(np.asarray(inputs["L0_ffn_w_gate"], np.float32).T),
        "wuT": np.ascontiguousarray(np.asarray(inputs["L0_ffn_w_up"], np.float32).T),
        "wdT": np.ascontiguousarray(np.asarray(inputs["L0_ffn_w_down"], np.float32).T),
        "winT": np.ascontiguousarray(np.asarray(inputs["L0_attn_in_proj_w"], np.float32).T),
        "woT": np.ascontiguousarray(np.asarray(inputs["L0_attn_out_proj_w"], np.float32).T),
        "cw": np.ascontiguousarray(inputs["L0_attn_conv_w"], np.float32),
        "cb": np.ascontiguousarray(inputs["L0_attn_conv_b"], np.float32),
        "dtb": np.ascontiguousarray(inputs["L0_attn_dt_bias"], np.float32),
        "alog": np.ascontiguousarray(inputs["L0_attn_A_log"], np.float32),
        "dvec": np.ascontiguousarray(inputs["L0_attn_D"], np.float32),
        "nw": np.ascontiguousarray(inputs["L0_attn_norm_weight"], np.float32),
    }
    in_maps = []
    for c in range(NCORES):
        m = dict(base)
        m["xT"] = np.ascontiguousarray(x[c].T)
        in_maps.append(m)
    res = run_bass_kernel_spmd(nc, in_maps, core_ids=list(range(NCORES)))
    out = np.stack([np.ascontiguousarray(np.asarray(r["outT"]).T)
                    for r in res.results], axis=0)
    return np.ascontiguousarray(out.astype(np.float32, copy=False))


if __name__ == "__main__":
    import time
    t = time.time()
    nc = build_nc()
    print("built in %.1fs" % (time.time() - t))


# version 11
# speedup vs baseline: 1.1497x; 1.0841x over previous
import contextlib
import numpy as np

import concourse.bass as bass
import concourse.mybir as mybir
from concourse.bass_utils import run_bass_kernel_spmd

F32 = mybir.dt.float32
BF = mybir.dt.bfloat16
AF = mybir.ActivationFunctionType
ALU = mybir.AluOpType

NCORES = 8
DM = 2048
LTOK = 2048
FFNH = 8192
INTER = 4096
HEADS = 64
PD = 64
PROJ = 8384
CONVD = 4224
LN_EPS = 1e-5
RMS_EPS = 1e-6

TFF = 512
NFB = LTOK // TFF
GH = 256
NGH = FFNH // GH
TM = 256
NMT = LTOK // TM
QS = 128
NQ = TM // QS

ENGS = ("sync", "scalar", "tensor", "vector", "gpsimd")


NDMASEM = 4


class _Op:
    __slots__ = ("eng", "fn", "dma", "deps", "sem_val", "need_inc", "idx", "slot")

    def __init__(self, eng, fn, dma):
        self.eng = eng
        self.fn = fn
        self.dma = dma
        self.deps = []
        self.sem_val = None
        self.need_inc = dma
        self.idx = -1
        self.slot = 0


class Sched:

    def __init__(self, nc, stack, tag=""):
        self.nc = nc
        self.ops = []
        self.last_writer = {}
        self.readers = {}
        self.guards = []
        S = stack.enter_context
        self.eng_sem = {e: S(nc.semaphore(f"{tag}e_{e}")) for e in ENGS}
        self.dma_sem = {e: [S(nc.semaphore(f"{tag}d_{e}{i}")) for i in range(NDMASEM)]
                        for e in ("sync", "scalar", "gpsimd")}
        self.dma_rr = {e: 0 for e in self.dma_sem}

    def op(self, eng, fn, reads=(), writes=(), dma=False, after=()):
        o = _Op(eng, fn, dma)
        o.idx = len(self.ops)
        if dma:
            o.slot = self.dma_rr[eng] % NDMASEM
            self.dma_rr[eng] += 1
        deps = set(self.guards)
        for r in reads:
            w = self.last_writer.get(r)
            if w is not None:
                deps.add(w)
        for w in writes:
            lw = self.last_writer.get(w)
            if lw is not None:
                deps.add(lw)
            deps.update(self.readers.get(w, ()))
        for d in after:
            if d is not None:
                deps.add(d)
        deps.discard(o)
        o.deps = sorted(deps, key=lambda d: d.idx)
        self.ops.append(o)
        for r in reads:
            self.readers.setdefault(r, []).append(o)
        for w in writes:
            self.last_writer[w] = o
            self.readers[w] = []
        return o

    def barrier(self):
        last = {}
        for o in self.ops:
            last[(o.eng, o.dma)] = o
        g = self.op("sync", lambda e: e.nop(), after=list(last.values()))
        self.guards = [g]
        return g

    def emit(self, final_engine="sync"):
        for o in self.ops:
            for d in o.deps:
                if d.dma or o.dma or d.eng != o.eng:
                    d.need_inc = True
        last_per_eng = {}
        for o in self.ops:
            last_per_eng[o.eng] = o
        for o in last_per_eng.values():
            o.need_inc = True
        ctr = {e: 0 for e in ENGS}
        dctr = {(e, i): 0 for e in self.dma_sem for i in range(NDMASEM)}
        for o in self.ops:
            if o.dma:
                dctr[(o.eng, o.slot)] += 16
                o.sem_val = dctr[(o.eng, o.slot)]
            elif o.need_inc:
                ctr[o.eng] += 1
                o.sem_val = ctr[o.eng]
        for e, c in ctr.items():
            if c >= 60000:
                raise RuntimeError(f"semaphore range risk on {e}: {c}")
        for k, c in dctr.items():
            if c >= 60000:
                raise RuntimeError(f"semaphore range risk on {k}: {c}")
        finals = []
        for (e, i), c in dctr.items():
            if c:
                finals.append((self.dma_sem[e][i], c))
        for e, c in ctr.items():
            if c and e != final_engine:
                finals.append((self.eng_sem[e], c))
        nc = self.nc
        with nc.Block() as block:
            for eng_name in ENGS:
                items = [o for o in self.ops if o.eng == eng_name]
                if not items and eng_name != final_engine:
                    continue

                def make(items, eng_name=eng_name):
                    def f(eng):
                        hw = {}
                        for o in items:
                            for d in o.deps:
                                if d.dma:
                                    sem = self.dma_sem[d.eng][d.slot]
                                elif d.eng != o.eng or o.dma:
                                    sem = self.eng_sem[d.eng]
                                else:
                                    continue
                                key = id(sem)
                                if hw.get(key, 0) < d.sem_val:
                                    eng.wait_ge(sem, d.sem_val)
                                    hw[key] = d.sem_val
                            if o.dma and o.sem_val > 16:
                                sem = self.dma_sem[o.eng][o.slot]
                                key = id(sem)
                                if hw.get(key, 0) < o.sem_val - 16:
                                    eng.wait_ge(sem, o.sem_val - 16)
                                    hw[key] = o.sem_val - 16
                            ins = o.fn(eng)
                            if o.need_inc:
                                if o.dma:
                                    ins.then_inc(self.dma_sem[o.eng][o.slot], 16)
                                else:
                                    ins.then_inc(self.eng_sem[o.eng], 1)
                        if eng_name == final_engine:
                            for sem, v in finals:
                                eng.wait_ge(sem, v)
                    return f

                getattr(block, eng_name)(make(items))
        return ({e: sum(1 for o in self.ops if o.eng == e) for e in ENGS},
                ctr, dctr)


_uid = [0]


def _nm(b):
    _uid[0] += 1
    return f"{b}_{_uid[0]}"


def build_nc(debug_mid=False, phases=("ffn1", "mamba", "ffn2")):
    nc = bass.Bass("TRN2", target_bir_lowering=False)

    xT = nc.dram_tensor("xT", [DM, LTOK], F32, kind="ExternalInput")
    wgT = nc.dram_tensor("wgT", [DM, FFNH], F32, kind="ExternalInput")
    wuT = nc.dram_tensor("wuT", [DM, FFNH], F32, kind="ExternalInput")
    wdT = nc.dram_tensor("wdT", [FFNH, DM], F32, kind="ExternalInput")
    winT = nc.dram_tensor("winT", [DM, PROJ], F32, kind="ExternalInput")
    woT = nc.dram_tensor("woT", [INTER, DM], F32, kind="ExternalInput")
    cw = nc.dram_tensor("cw", [CONVD, 4], F32, kind="ExternalInput")
    cb = nc.dram_tensor("cb", [CONVD], F32, kind="ExternalInput")
    dtb = nc.dram_tensor("dtb", [HEADS], F32, kind="ExternalInput")
    alog = nc.dram_tensor("alog", [HEADS], F32, kind="ExternalInput")
    dvec = nc.dram_tensor("dvec", [HEADS], F32, kind="ExternalInput")
    nw = nc.dram_tensor("nw", [INTER], F32, kind="ExternalInput")
    outT = nc.dram_tensor("outT", [DM, LTOK], F32, kind="ExternalOutput")

    mid = "ExternalOutput" if debug_mid else "Internal"
    gsc = nc.dram_tensor("gsc", [128, NGH * 16 * GH], BF, kind="Internal")
    usc = nc.dram_tensor("usc", [128, NGH * 16 * GH], BF, kind="Internal")
    dsc = nc.dram_tensor("dsc", [128, 16 * 8192], BF, kind="Internal")
    isc = nc.dram_tensor("isc", [128, 32 * 4096 + 3072], BF, kind="Internal")
    osc = nc.dram_tensor("osc", [128, 8 * 8192], BF, kind="Internal")
    X1 = nc.dram_tensor("X1", [DM, LTOK], F32, kind=mid)
    X2 = nc.dram_tensor("X2", [DM, LTOK], F32, kind=mid)

    CONSTS = [LN_EPS, RMS_EPS, -0.5, -1.0, 1.0, 0.5, 1.0 / DM, 1.0 / INTER]

    es = contextlib.ExitStack()
    with es:
        S = es.enter_context
        sch = Sched(nc, es)

        cst = S(nc.sbuf_tensor("cst", [128, len(CONSTS)], F32))
        for i, cv in enumerate(CONSTS):
            nc.const_aps.register(F32, float(cv), cst[:, i:i + 1])

        ones_col_f = S(nc.sbuf_tensor("ones_col_f", [128, 1], F32))
        ones_col_b = S(nc.sbuf_tensor("ones_col_b", [128, 1], BF))
        ones_row_f = S(nc.sbuf_tensor("ones_row_f", [1, 128], F32))
        I_f = S(nc.sbuf_tensor("I_f", [128, 128], F32))
        I_b = S(nc.sbuf_tensor("I_b", [128, 128], BF))
        LT = S(nc.sbuf_tensor("LT", [128, 128], F32))
        US = S(nc.sbuf_tensor("US", [128, 128], F32))
        US_b = S(nc.sbuf_tensor("US_b", [128, 128], BF))
        NEGM = S(nc.sbuf_tensor("NEGM", [128, 128], F32))
        p_cw = S(nc.sbuf_tensor("p_cw", [128, 33 * 4], F32))
        p_cb = S(nc.sbuf_tensor("p_cb", [128, 33], F32))
        p_nw = S(nc.sbuf_tensor("p_nw", [128, 32], F32))
        p_Abc = S(nc.sbuf_tensor("p_Abc", [128, 64], F32))
        p_dtbbc = S(nc.sbuf_tensor("p_dtbbc", [128, 64], F32))
        p_Dbc = S(nc.sbuf_tensor("p_Dbc", [128, 64], F32))
        prow = S(nc.sbuf_tensor("prow", [1, 192], F32))
        arow = S(nc.sbuf_tensor("arow", [1, 192], F32))
        pstage = S(nc.sbuf_tensor("pstage", [33, 128], F32))
        nstage = S(nc.sbuf_tensor("nstage", [32, 128], F32))

        PS = [S(nc.psum_tensor(f"ps{k}", [128, 512], F32)) for k in range(8)]
        PSB = [p.ap().bitcast(BF) for p in PS]
        psk = [f"ps{k}" for k in range(8)]

        def set_consts(e):
            ins = None
            for i, cv in enumerate(CONSTS):
                ins = e.memset(cst[:, i:i + 1], float(cv))
            return ins
        sch.op("gpsimd", set_consts, writes=["cst"])

        def g_masks(e):
            e.memset(ones_col_f[:], 1.0)
            e.memset(ones_col_b[:], 1.0)
            e.memset(ones_row_f[:], 1.0)
            e.memset(I_f[:], 1.0)
            e.affine_select(out=I_f[:], in_=I_f[:], compare_op=ALU.is_equal,
                            fill=0.0, base=0, pattern=[[1, 128]], channel_multiplier=-1)
            e.memset(I_b[:], 1.0)
            e.affine_select(out=I_b[:], in_=I_b[:], compare_op=ALU.is_equal,
                            fill=0.0, base=0, pattern=[[1, 128]], channel_multiplier=-1)
            e.memset(LT[:], 1.0)
            e.affine_select(out=LT[:], in_=LT[:], compare_op=ALU.is_ge,
                            fill=0.0, base=0, pattern=[[1, 128]], channel_multiplier=-1)
            e.memset(US[:], 1.0)
            e.affine_select(out=US[:], in_=US[:], compare_op=ALU.is_ge,
                            fill=0.0, base=-1, pattern=[[-1, 128]], channel_multiplier=1)
            e.memset(US_b[:], 1.0)
            e.affine_select(out=US_b[:], in_=US_b[:], compare_op=ALU.is_ge,
                            fill=0.0, base=-1, pattern=[[-1, 128]], channel_multiplier=1)
            e.memset(NEGM[:], 0.0)
            return e.affine_select(out=NEGM[:], in_=NEGM[:], compare_op=ALU.is_ge,
                                   fill=-1.0e30, base=0, pattern=[[1, 128]],
                                   channel_multiplier=-1)
        sch.op("gpsimd", g_masks, writes=["masks"])

        sch.op("sync", lambda e: e.dma_start(
            prow[:, 0:64], alog.ap().rearrange("(o h) -> o h", o=1)),
            writes=["prow"], dma=True)
        sch.op("sync", lambda e: e.dma_start(
            prow[:, 64:128], dtb.ap().rearrange("(o h) -> o h", o=1)),
            writes=["prow"], dma=True)
        sch.op("sync", lambda e: e.dma_start(
            prow[:, 128:192], dvec.ap().rearrange("(o h) -> o h", o=1)),
            writes=["prow"], dma=True)
        sch.op("sync", lambda e: e.dma_start(
            pstage[:], cb.ap().rearrange("(c q) -> c q", q=128)),
            writes=["pstage"], dma=True)
        sch.op("sync", lambda e: e.dma_start(
            nstage[:], nw.ap().rearrange("(c q) -> c q", q=128)),
            writes=["nstage"], dma=True)
        sch.op("sync", lambda e: e.dma_start(
            p_cw.ap().rearrange("p (c k) -> p c k", k=4),
            cw.ap().rearrange("(c p) k -> p c k", p=128)),
            writes=["p_cw"], dma=True)

        sch.op("scalar", lambda e: e.activation(arow[:, 0:64], prow[:, 0:64], AF.Exp),
               reads=["prow"], writes=["arow"])
        sch.op("vector", lambda e: e.tensor_scalar_mul(arow[:, 0:64], arow[:, 0:64], -1.0),
               reads=["cst"], writes=["arow"])
        sch.op("vector", lambda e: e.tensor_copy(arow[:, 64:192], prow[:, 64:192]),
               reads=["prow"], writes=["arow"])

        def bc3(e):
            e.matmul(PS[0][:, 0:64], ones_row_f[:], arow[:, 0:64], start=True, stop=True)
            e.matmul(PS[0][:, 64:128], ones_row_f[:], arow[:, 64:128], start=True, stop=True)
            return e.matmul(PS[0][:, 128:192], ones_row_f[:], arow[:, 128:192],
                            start=True, stop=True)
        sch.op("tensor", bc3, reads=["arow", "masks"], writes=["ps0"])
        sch.op("vector", lambda e: e.tensor_copy(p_Abc[:], PS[0][:, 0:64]),
               reads=["ps0"], writes=["p_Abc"])
        sch.op("vector", lambda e: e.tensor_copy(p_dtbbc[:], PS[0][:, 64:128]),
               reads=["ps0"], writes=["p_dtbbc"])
        sch.op("vector", lambda e: e.tensor_copy(p_Dbc[:], PS[0][:, 128:192]),
               reads=["ps0"], writes=["p_Dbc"])

        def tr2(e):
            e.transpose(PS[1][:, 0:33], pstage[:], I_f[0:33, 0:33])
            return e.transpose(PS[1][:, 64:96], nstage[:], I_f[0:32, 0:32])
        sch.op("tensor", tr2, reads=["pstage", "nstage", "masks"], writes=["ps1"])
        sch.op("vector", lambda e: e.tensor_copy(p_cb[:], PS[1][:, 0:33]),
               reads=["ps1"], writes=["p_cb"])
        sch.op("vector", lambda e: e.tensor_copy(p_nw[:], PS[1][:, 64:96]),
               reads=["ps1"], writes=["p_nw"])

        conv_items = []

        def add_wgu(g):
            conv_items.append((f"gsc:{g}",
                lambda g=g: wgT.ap()[:, g * GH:(g + 1) * GH]
                    .rearrange("(c p) h -> p c h", p=128),
                lambda g=g: gsc.ap()[:, g * 4096:(g + 1) * 4096]
                    .rearrange("p (c h) -> p c h", h=GH)))
            conv_items.append((f"usc:{g}",
                lambda g=g: wuT.ap()[:, g * GH:(g + 1) * GH]
                    .rearrange("(c p) h -> p c h", p=128),
                lambda g=g: usc.ap()[:, g * 4096:(g + 1) * 4096]
                    .rearrange("p (c h) -> p c h", h=GH)))

        for g in range(NGH):
            add_wgu(g)
        for gg in range(16):
            for hh in range(2):
                conv_items.append((f"dsc:{gg}:{hh}",
                    lambda gg=gg, hh=hh: wdT.ap()[gg * 512 + hh * 256:
                                                  gg * 512 + hh * 256 + 256, :]
                        .rearrange("(hl p) d -> p hl d", p=128),
                    lambda gg=gg, hh=hh: dsc.ap()[:, gg * 8192 + hh * 4096:
                                                  gg * 8192 + hh * 4096 + 4096]
                        .rearrange("p (hl d) -> p hl d", d=2048)))
        for G in range(32):
            conv_items.append((f"isc:{G}",
                lambda G=G: winT.ap()[:, G * 256:(G + 1) * 256]
                    .rearrange("(c p) h -> p c h", p=128),
                lambda G=G: isc.ap()[:, G * 4096:(G + 1) * 4096]
                    .rearrange("p (c h) -> p c h", h=256)))
        conv_items.append(("isc:32",
            lambda: winT.ap()[:, 8192:8384].rearrange("(c p) h -> p c h", p=128),
            lambda: isc.ap()[:, 131072:131072 + 3072]
                .rearrange("p (c h) -> p c h", h=192)))
        for og in range(8):
            for ih in range(2):
                conv_items.append((f"osc:{og}:{ih}",
                    lambda og=og, ih=ih: woT.ap()[og * 512 + ih * 256:
                                                  og * 512 + ih * 256 + 256, :]
                        .rearrange("(il p) d -> p il d", p=128),
                    lambda og=og, ih=ih: osc.ap()[:, og * 8192 + ih * 4096:
                                                  og * 8192 + ih * 4096 + 4096]
                        .rearrange("p (il d) -> p il d", d=2048)))

        def emit_conversion():
            for key, src_fn, dst_fn in conv_items:
                sch.op("gpsimd", lambda e, src_fn=src_fn, dst_fn=dst_fn:
                       e.dma_start(dst_fn(), src_fn()),
                       writes=[key], dma=True)

        def emit_ln(T, resid, xln, sqbuf_fn, lnsc, rows, sq_keys):
            for dc in range(16):
                sch.op("vector", lambda e, dc=dc: e.tensor_copy(
                    xln[:, dc * T:(dc + 1) * T], resid[:, dc * T:(dc + 1) * T]),
                    reads=["resid"], writes=["xln"])
            for dc in range(16):
                sch.op("scalar", lambda e, dc=dc: e.activation(
                    sqbuf_fn(dc), resid[:, dc * T:(dc + 1) * T], AF.Square),
                    reads=["resid"], writes=[sq_keys[dc]])

            def s1(e):
                for dc in range(16):
                    ins = e.matmul(PS[0][0:1, 0:T], ones_col_b[:],
                                   xln[:, dc * T:(dc + 1) * T],
                                   start=(dc == 0), stop=(dc == 15))
                return ins
            sch.op("tensor", s1, reads=["xln", "masks"], writes=["ps0"])

            def s2(e):
                for dc in range(16):
                    ins = e.matmul(PS[1][0:1, 0:T], ones_col_b[:], sqbuf_fn(dc),
                                   start=(dc == 0), stop=(dc == 15))
                return ins
            sch.op("tensor", s2, reads=list(sq_keys) + ["masks"], writes=["ps1"])

            m_r = rows[:, 0:T]
            t_r = rows[:, 512:512 + T]
            sch.op("scalar", lambda e: e.mul(m_r, PS[0][0:1, 0:T], 1.0 / DM),
                   reads=["ps0", "cst"], writes=["rows_m"])
            sch.op("scalar", lambda e: e.activation(t_r, m_r, AF.Square),
                   reads=["rows_m"], writes=["rows_t"])
            sch.op("vector", lambda e: e.scalar_tensor_tensor(
                t_r, PS[1][0:1, 0:T], 1.0 / DM, t_r, op0=ALU.mult, op1=ALU.subtract),
                reads=["ps1", "rows_t", "cst"], writes=["rows_t"])
            sch.op("scalar", lambda e: e.activation(t_r, t_r, AF.Ln, bias=LN_EPS),
                   reads=["rows_t", "cst"], writes=["rows_t"])
            sch.op("scalar", lambda e: e.activation(t_r, t_r, AF.Exp, scale=-0.5),
                   reads=["rows_t", "cst"], writes=["rows_t"])
            sch.op("vector", lambda e: e.tensor_mul(m_r, m_r, t_r),
                   reads=["rows_m", "rows_t"], writes=["rows_m"])
            sch.op("tensor", lambda e: e.matmul(PS[2][:, 0:T], ones_row_f[:], t_r,
                                                start=True, stop=True),
                   reads=["rows_t", "masks"], writes=["ps2"])
            sch.op("tensor", lambda e: e.matmul(PS[3][:, 0:T], ones_row_f[:], m_r,
                                                start=True, stop=True),
                   reads=["rows_m", "masks"], writes=["ps3"])
            for dc in range(16):
                sch.op("vector", lambda e, dc=dc: e.tensor_mul(
                    lnsc[:, 0:T], resid[:, dc * T:(dc + 1) * T], PS[2][:, 0:T]),
                    reads=["resid", "ps2"], writes=["lnsc"])
                sch.op("vector", lambda e, dc=dc: e.tensor_sub(
                    xln[:, dc * T:(dc + 1) * T], lnsc[:, 0:T], PS[3][:, 0:T]),
                    reads=["lnsc", "ps3"], writes=["xln"])

        def emit_ffn(src, dst, do_conversion):
            with contextlib.ExitStack() as fs:
                FS = fs.enter_context
                resid = FS(nc.sbuf_tensor(_nm("f_resid"), [128, 16 * TFF], F32))
                xln = FS(nc.sbuf_tensor(_nm("f_xln"), [128, 16 * TFF], BF))
                actb = FS(nc.sbuf_tensor(_nm("f_act"), [128, 64 * TFF], BF))
                wgb = FS(nc.sbuf_tensor(_nm("f_wg"), [128, 2 * 4096], BF))
                wub = FS(nc.sbuf_tensor(_nm("f_wu"), [128, 2 * 4096], BF))
                wdb = FS(nc.sbuf_tensor(_nm("f_wd"), [128, 2 * 4096], BF))
                lnsc = FS(nc.sbuf_tensor(_nm("f_lnsc"), [128, TFF], F32))
                rows = FS(nc.sbuf_tensor(_nm("f_rows"), [1, 1024], F32))

                if do_conversion:
                    emit_conversion()

                for blk in range(NFB):
                    t0 = blk * TFF
                    sch.op("scalar", lambda e, t0=t0: e.dma_start(
                        resid.ap().rearrange("p (c t) -> p c t", t=TFF),
                        src.ap()[:, t0:t0 + TFF].rearrange("(c p) t -> p c t", p=128)),
                        writes=["resid"], dma=True)

                    emit_ln(TFF, resid, xln,
                            lambda dc: actb[:, dc * TFF:(dc + 1) * TFF], lnsc, rows,
                            [f"act:{dc}" for dc in range(16)])

                    for g in range(NGH):
                        par = g % 2
                        sch.op("sync", lambda e, g=g, par=par: e.dma_start(
                            wgb[:, par * 4096:(par + 1) * 4096],
                            gsc.ap()[:, g * 4096:(g + 1) * 4096]),
                            reads=[f"gsc:{g}"], writes=[f"wgb:{par}"], dma=True)
                        sch.op("sync", lambda e, g=g, par=par: e.dma_start(
                            wub[:, par * 4096:(par + 1) * 4096],
                            usc.ap()[:, g * 4096:(g + 1) * 4096]),
                            reads=[f"usc:{g}"], writes=[f"wub:{par}"], dma=True)
                        for hl in range(2):
                            ht = g * 2 + hl
                            pp = 4 + ht % 2

                            def gmm(e, par=par, hl=hl, pp=pp):
                                for dc in range(16):
                                    ins = e.matmul(
                                        PS[pp][:],
                                        wgb[:, par * 4096 + dc * GH + hl * 128:
                                            par * 4096 + dc * GH + hl * 128 + 128],
                                        xln[:, dc * TFF:(dc + 1) * TFF],
                                        start=(dc == 0), stop=(dc == 15))
                                return ins
                            sch.op("tensor", gmm, reads=[f"wgb:{par}", "xln"],
                                   writes=[psk[pp]])
                            sch.op("scalar", lambda e, ht=ht, pp=pp: e.activation(
                                actb[:, ht * TFF:(ht + 1) * TFF], PS[pp][:], AF.Silu),
                                reads=[psk[pp]], writes=[f"act:{ht}"])

                            pq = 6 + ht % 2

                            def umm(e, par=par, hl=hl, pq=pq):
                                for dc in range(16):
                                    ins = e.matmul(
                                        PS[pq][:],
                                        wub[:, par * 4096 + dc * GH + hl * 128:
                                            par * 4096 + dc * GH + hl * 128 + 128],
                                        xln[:, dc * TFF:(dc + 1) * TFF],
                                        start=(dc == 0), stop=(dc == 15))
                                return ins
                            sch.op("tensor", umm, reads=[f"wub:{par}", "xln"],
                                   writes=[psk[pq]])
                            sch.op("vector", lambda e, ht=ht, pq=pq: e.scalar_tensor_tensor(
                                actb[:, ht * TFF:(ht + 1) * TFF],
                                actb[:, ht * TFF:(ht + 1) * TFF], 0.5, PS[pq][:],
                                op0=ALU.mult, op1=ALU.mult),
                                reads=[f"act:{ht}", psk[pq], "cst"], writes=[f"act:{ht}"])

                    for hh in range(2):
                        for gg in range(16):
                            par = gg % 2
                            sch.op("sync", lambda e, gg=gg, hh=hh, par=par: e.dma_start(
                                wdb.ap()[:, par * 4096:(par + 1) * 4096]
                                .rearrange("p (hl d) -> p hl d", d=1024),
                                dsc.ap()[:, gg * 8192:(gg + 1) * 8192]
                                .rearrange("p (hl d) -> p hl d", d=2048)
                                [:, :, hh * 1024:(hh + 1) * 1024]),
                                reads=[f"dsc:{gg}:0", f"dsc:{gg}:1"],
                                writes=[f"wdb:{par}"], dma=True)
                            for oc8 in range(8):
                                def dmm(e, gg=gg, par=par, oc8=oc8):
                                    for hl in range(4):
                                        ins = e.matmul(
                                            PS[oc8][:],
                                            wdb[:, par * 4096 + hl * 1024 + oc8 * 128:
                                                par * 4096 + hl * 1024 + oc8 * 128 + 128],
                                            actb[:, (gg * 4 + hl) * TFF:
                                                 (gg * 4 + hl + 1) * TFF],
                                            start=(gg == 0 and hl == 0),
                                            stop=(gg == 15 and hl == 3))
                                    return ins
                                sch.op("tensor", dmm,
                                       reads=[f"wdb:{par}"] +
                                             [f"act:{gg * 4 + i}" for i in range(4)],
                                       writes=[psk[oc8]])
                        for oc8 in range(8):
                            oc = hh * 8 + oc8
                            sch.op("vector", lambda e, oc=oc, oc8=oc8: e.tensor_add(
                                resid[:, oc * TFF:(oc + 1) * TFF],
                                resid[:, oc * TFF:(oc + 1) * TFF], PS[oc8][:]),
                                reads=[psk[oc8]], writes=["resid"])

                    sch.op("scalar", lambda e, t0=t0: e.dma_start(
                        dst.ap()[:, t0:t0 + TFF].rearrange("(c p) t -> p c t", p=128),
                        resid.ap().rearrange("p (c t) -> p c t", t=TFF)),
                        reads=["resid"], dma=True)

        def emit_mamba(src, dst):
            with contextlib.ExitStack() as ms:
                MS = ms.enter_context
                resid = MS(nc.sbuf_tensor(_nm("m_resid"), [128, 16 * TM], F32))
                xln = MS(nc.sbuf_tensor(_nm("m_xln"), [128, 16 * TM], BF))
                wib = MS(nc.sbuf_tensor(_nm("m_wi"), [128, 2 * 8192], BF))
                wob = MS(nc.sbuf_tensor(_nm("m_wo"), [128, 2 * 8192], BF))
                sg = MS(nc.sbuf_tensor(_nm("m_sg"), [128, 32 * TM], BF))
                cx = MS(nc.sbuf_tensor(_nm("m_cx"), [128, 32 * TM], BF))
                cB = MS(nc.sbuf_tensor(_nm("m_cB"), [64, TM], BF))
                cC = MS(nc.sbuf_tensor(_nm("m_cC"), [64, TM], BF))
                dtf = MS(nc.sbuf_tensor(_nm("m_dtf"), [64, TM], F32))
                cin = MS(nc.sbuf_tensor(_nm("m_cin"), [128, 2 * (TM + 3)], F32))
                cacc = MS(nc.sbuf_tensor(_nm("m_cacc"), [128, 2 * TM], F32))
                halo = MS(nc.sbuf_tensor(_nm("m_halo"), [128, 99], F32))
                Sf = MS(nc.sbuf_tensor(_nm("m_Sf"), [64, HEADS * PD], F32))
                Sb = MS(nc.sbuf_tensor(_nm("m_Sb"), [64, HEADS * PD], BF))
                xst = MS(nc.sbuf_tensor(_nm("m_xst"), [128, 32 * 128], BF))
                sp1 = MS(nc.sbuf_tensor(_nm("m_sp1"), [128, 64], F32))
                sp2 = MS(nc.sbuf_tensor(_nm("m_sp2"), [128, 64], F32))
                dtp = MS(nc.sbuf_tensor(_nm("m_dtp"), [128, 64], F32))
                a_tm = MS(nc.sbuf_tensor(_nm("m_a"), [128, 64], F32))
                c_sb = MS(nc.sbuf_tensor(_nm("m_c"), [128, 64], F32))
                cq_sb = MS(nc.sbuf_tensor(_nm("m_cq"), [1, 64], F32))
                wdec = MS(nc.sbuf_tensor(_nm("m_wdec"), [128, 64], F32))
                wdt = MS(nc.sbuf_tensor(_nm("m_wdt"), [128, 64], F32))
                ec_sb = MS(nc.sbuf_tensor(_nm("m_ec"), [128, 64], F32))
                Ebc = MS(nc.sbuf_tensor(_nm("m_Ebc"), [64, 64], F32))
                Gt_sb = MS(nc.sbuf_tensor(_nm("m_Gt"), [128, 128], F32))
                BCt = MS(nc.sbuf_tensor(_nm("m_BCt"), [128, 64], BF))
                ATq = MS(nc.sbuf_tensor(_nm("m_ATq"), [128, 2 * 512], BF))
                expmq = MS(nc.sbuf_tensor(_nm("m_expq"), [128, 2 * 512], BF))
                GmTq = MS(nc.sbuf_tensor(_nm("m_GmTq"), [128, 2 * 512], BF))
                Bdq = MS(nc.sbuf_tensor(_nm("m_Bdq"), [128, 2 * 256], BF))
                y2b = MS(nc.sbuf_tensor(_nm("m_y2"), [128, 2 * 256], F32))
                tmpS = MS(nc.sbuf_tensor(_nm("m_tmpS"), [64, 256], F32))
                ysqq = MS(nc.sbuf_tensor(_nm("m_ysqq"), [128, 256], F32))
                ssqp = MS(nc.sbuf_tensor(_nm("m_ssqp"), [128, 16], F32))
                ssq_tm = MS(nc.sbuf_tensor(_nm("m_ssqtm"), [128, NQ], F32))
                rstd_row = MS(nc.sbuf_tensor(_nm("m_rsr"), [1, TM], F32))
                rstd_bc = MS(nc.sbuf_tensor(_nm("m_rsb"), [128, TM], F32))
                oacc = MS(nc.sbuf_tensor(_nm("m_oacc"), [128, TM], F32))
                lnsc = MS(nc.sbuf_tensor(_nm("m_lnsc"), [128, TM], F32))
                rows = MS(nc.sbuf_tensor(_nm("m_rows"), [1, 1024], F32))
                sqb = MS(nc.sbuf_tensor(_nm("m_sq"), [128, 16 * TM], BF))

                def g_init(e):
                    e.memset(halo[:], 0.0)
                    e.memset(Sf[:], 0.0)
                    return e.memset(Sb[:], 0.0)
                sch.op("gpsimd", g_init,
                       writes=[f"halo:{c}" for c in range(33)] +
                              [f"Sf:{hq}" for hq in range(16)] +
                              [f"Sb:{hq}" for hq in range(16)])

                def rep4(ap2d, nfree):
                    return ap2d.rearrange("p (o n) -> p o n", o=1) \
                               .broadcast_to([ap2d.shape[0], 4, nfree])

                def col4(ap2d, nfree):
                    return ap2d.rearrange("p (h o) -> p h o", o=1) \
                               .broadcast_to([ap2d.shape[0], 4, nfree])

                pin_ctr = [0]
                for mt in range(NMT):
                    t0 = mt * TM
                    sch.op("scalar", lambda e, t0=t0: e.dma_start(
                        resid.ap().rearrange("p (c t) -> p c t", t=TM),
                        src.ap()[:, t0:t0 + TM].rearrange("(c p) t -> p c t", p=128)),
                        writes=["resid"], dma=True)

                    emit_ln(TM, resid, xln,
                            lambda dc: sqb[:, dc * TM:(dc + 1) * TM], lnsc, rows,
                            [f"sq:{dc}" for dc in range(16)])

                    for Gp in range(17):
                        par = Gp % 2
                        if Gp < 16:
                            nw_ = 8192
                            fcs = [4 * Gp, 4 * Gp + 1, 4 * Gp + 2, 4 * Gp + 3]
                        else:
                            nw_ = 3072
                            fcs = [64, 65]
                        sch.op("sync", lambda e, Gp=Gp, par=par, nw_=nw_: e.dma_start(
                            wib[:, par * 8192:par * 8192 + nw_],
                            isc.ap()[:, Gp * 8192:Gp * 8192 + nw_]),
                            reads=[f"isc:{2 * Gp}", f"isc:{2 * Gp + 1}"]
                            if Gp < 16 else ["isc:32"],
                            writes=[f"wib:{par}"], dma=True)
                        for j, fc in enumerate(fcs):
                            pp = 4 + pin_ctr[0] % 2
                            pin_ctr[0] += 1
                            if Gp < 16:
                                W = 256
                                base = par * 8192 + (j // 2) * 4096
                                coff = (j % 2) * 128
                                mrows = 128
                            else:
                                W = 192
                                base = par * 8192
                                coff = j * 128
                                mrows = 64 if fc == 65 else 128

                            def imm(e, base=base, W=W, coff=coff, mrows=mrows, pp=pp):
                                for dc in range(16):
                                    ins = e.matmul(
                                        PS[pp][0:mrows, 0:TM],
                                        wib[:, base + dc * W + coff:
                                            base + dc * W + coff + mrows],
                                        xln[:, dc * TM:(dc + 1) * TM],
                                        start=(dc == 0), stop=(dc == 15))
                                return ins
                            sch.op("tensor", imm, reads=[f"wib:{par}", "xln"],
                                   writes=[psk[pp]])

                            if fc < 32:
                                sch.op("scalar", lambda e, fc=fc, pp=pp: e.activation(
                                    sg[:, fc * TM:(fc + 1) * TM], PS[pp][:, 0:TM],
                                    AF.Silu),
                                    reads=[psk[pp]], writes=[f"sg:{fc}"])
                            elif fc < 65:
                                cc = fc - 32
                                cpar = cc % 2
                                tape = "vector"
                                cb0 = cpar * (TM + 3)
                                sch.op("gpsimd", lambda e, cc=cc, cb0=cb0: e.tensor_copy(
                                    cin[:, cb0:cb0 + 3], halo[:, cc * 3:cc * 3 + 3]),
                                    reads=[f"halo:{cc}"], writes=[f"cin:{cpar}"])
                                sch.op("scalar", lambda e, pp=pp, cb0=cb0: e.copy(
                                    cin[:, cb0 + 3:cb0 + 3 + TM], PS[pp][:, 0:TM]),
                                    reads=[psk[pp]], writes=[f"cin:{cpar}"])
                                sch.op("gpsimd", lambda e, cc=cc, cb0=cb0: e.tensor_copy(
                                    halo[:, cc * 3:cc * 3 + 3],
                                    cin[:, cb0 + TM:cb0 + TM + 3]),
                                    reads=[f"cin:{cpar}"], writes=[f"halo:{cc}"])
                                sch.op(tape, lambda e, cc=cc, cpar=cpar, cb0=cb0:
                                       e.tensor_scalar(
                                    cacc[:, cpar * TM:(cpar + 1) * TM],
                                    cin[:, cb0:cb0 + TM],
                                    p_cw[:, cc * 4:cc * 4 + 1], None, op0=ALU.mult),
                                    reads=[f"cin:{cpar}", "p_cw"], writes=[f"cacc:{cpar}"])
                                for k in range(1, 4):
                                    sch.op(tape, lambda e, cc=cc, cpar=cpar, k=k, cb0=cb0:
                                           e.scalar_tensor_tensor(
                                        cacc[:, cpar * TM:(cpar + 1) * TM],
                                        cin[:, cb0 + k:cb0 + k + TM],
                                        p_cw[:, cc * 4 + k:cc * 4 + k + 1],
                                        cacc[:, cpar * TM:(cpar + 1) * TM],
                                        op0=ALU.mult, op1=ALU.add),
                                        reads=[f"cin:{cpar}", "p_cw", f"cacc:{cpar}"],
                                        writes=[f"cacc:{cpar}"])
                                if cc < 32:
                                    sch.op("scalar", lambda e, cc=cc, cpar=cpar:
                                           e.activation(
                                        cx[:, cc * TM:(cc + 1) * TM],
                                        cacc[:, cpar * TM:(cpar + 1) * TM], AF.Silu,
                                        bias=p_cb[:, cc:cc + 1]),
                                        reads=[f"cacc:{cpar}", "p_cb"],
                                        writes=[f"cx:{cc}"])
                                else:
                                    sch.op("scalar", lambda e, cpar=cpar: e.activation(
                                        cB[:, 0:TM],
                                        cacc[0:64, cpar * TM:cpar * TM + TM], AF.Silu,
                                        bias=p_cb[0:64, 32:33]),
                                        reads=[f"cacc:{cpar}", "p_cb"], writes=["cB"])
                                    sch.op("scalar", lambda e, cpar=cpar: e.activation(
                                        cC[:, 0:TM],
                                        cacc[64:128, cpar * TM:cpar * TM + TM], AF.Silu,
                                        bias=p_cb[64:128, 32:33]),
                                        reads=[f"cacc:{cpar}", "p_cb"], writes=["cC"])
                            else:
                                sch.op("vector", lambda e, pp=pp: e.tensor_copy(
                                    dtf[:, 0:TM], PS[pp][0:64, 0:TM]),
                                    reads=[psk[pp]], writes=["dtf"])

                    for q in range(NQ):
                        qa = q * QS
                        sch.op("tensor", lambda e, qa=qa: e.transpose(
                            PS[6][:, 0:64], dtf[:, qa:qa + QS], I_f[0:64, 0:64]),
                            reads=["dtf", "masks"], writes=["ps6"])
                        sch.op("vector", lambda e: e.tensor_copy(sp1[:], PS[6][:, 0:64]),
                               reads=["ps6"], writes=["sp1"])
                        sch.op("gpsimd", lambda e: e.tensor_add(sp1[:], sp1[:], p_dtbbc[:]),
                               reads=["p_dtbbc", "sp1"], writes=["sp1"])
                        sch.op("scalar", lambda e: e.activation(sp2[:], sp1[:], AF.Abs),
                               reads=["sp1"], writes=["sp2"])
                        sch.op("scalar", lambda e: e.activation(sp2[:], sp2[:], AF.Exp,
                                                                scale=-1.0),
                               reads=["sp2", "cst"], writes=["sp2"])
                        sch.op("scalar", lambda e: e.activation(sp2[:], sp2[:], AF.Ln,
                                                                bias=1.0),
                               reads=["sp2", "cst"], writes=["sp2"])
                        sch.op("gpsimd", lambda e: e.tensor_scalar_max(dtp[:], sp1[:], 0.0),
                               reads=["sp1"], writes=["dtp"])
                        sch.op("gpsimd", lambda e: e.tensor_add(dtp[:], dtp[:], sp2[:]),
                               reads=["sp2", "dtp"], writes=["dtp"])
                        sch.op("gpsimd", lambda e: e.tensor_mul(a_tm[:], dtp[:], p_Abc[:]),
                               reads=["dtp", "p_Abc"], writes=["a_tm"])
                        sch.op("tensor", lambda e: e.matmul(
                            PS[0][:, 0:64], LT[:], a_tm[:], start=True, stop=True),
                            reads=["a_tm", "masks"], writes=["ps0"])
                        sch.op("tensor", lambda e: e.matmul(
                            PS[1][0:1, 0:64], ones_col_f[:], a_tm[:], start=True, stop=True),
                            reads=["a_tm", "masks"], writes=["ps1"])
                        sch.op("vector", lambda e: e.tensor_copy(c_sb[:], PS[0][:, 0:64]),
                               reads=["ps0"], writes=["c_sb"])
                        sch.op("vector", lambda e: e.tensor_copy(cq_sb[:], PS[1][0:1, 0:64]),
                               reads=["ps1"], writes=["cq_sb"])
                        sch.op("tensor", lambda e: e.matmul(
                            PS[2][:, 0:64], ones_row_f[:], cq_sb[:], start=True, stop=True),
                            reads=["cq_sb", "masks"], writes=["ps2"])
                        sch.op("tensor", lambda e: e.matmul(
                            PS[3][0:64, 0:64], ones_row_f[:, 0:64], cq_sb[:],
                            start=True, stop=True),
                            reads=["cq_sb", "masks"], writes=["ps3"])
                        sch.op("vector", lambda e: e.tensor_sub(
                            wdec[:], PS[2][:, 0:64], c_sb[:]),
                            reads=["ps2", "c_sb"], writes=["wdec"])
                        sch.op("scalar", lambda e: e.activation(wdec[:], wdec[:], AF.Exp),
                               reads=["wdec"], writes=["wdec"])
                        sch.op("scalar", lambda e: e.activation(ec_sb[:], c_sb[:], AF.Exp),
                               reads=["c_sb"], writes=["ec_sb"])
                        sch.op("scalar", lambda e: e.activation(
                            Ebc[:], PS[3][0:64, 0:64], AF.Exp),
                            reads=["ps3"], writes=["Ebc"])
                        sch.op("tensor", lambda e, qa=qa: e.matmul(
                            PS[4][:, 0:128], cB[:, qa:qa + QS], cC[:, qa:qa + QS],
                            start=True, stop=True),
                            reads=["cB", "cC"], writes=["ps4"])
                        sch.op("vector", lambda e: e.tensor_mul(
                            Gt_sb[:], PS[4][:, 0:128], LT[:]),
                            reads=["ps4", "masks"], writes=["Gt_sb"])
                        sch.op("vector", lambda e: e.tensor_mul(
                            wdt[:], wdec[:], dtp[:]),
                            reads=["wdec", "dtp"], writes=["wdt"])
                        sch.op("tensor", lambda e, qa=qa: e.transpose(
                            PSB[7][:, 0:64], cB[:, qa:qa + QS], I_b[0:64, 0:64]),
                            reads=["cB", "masks"], writes=["ps7"])
                        sch.op("vector", lambda e: e.tensor_copy(BCt[:], PSB[7][:, 0:64]),
                               reads=["ps7"], writes=["BCt"])
                        for rnd in range(4):
                            def xtr(e, rnd=rnd, qa=qa):
                                for j in range(8):
                                    fc = rnd * 8 + j
                                    ins = e.transpose(
                                        PSB[7][:, j * 128:(j + 1) * 128],
                                        cx[:, fc * TM + qa:fc * TM + qa + QS], I_b[:])
                                return ins
                            sch.op("tensor", xtr,
                                   reads=[f"cx:{rnd * 8 + j}" for j in range(8)] + ["masks"],
                                   writes=["ps7"])
                            sch.op("vector", lambda e, rnd=rnd: e.tensor_copy(
                                xst[:, rnd * 1024:(rnd + 1) * 1024], PSB[7][:]),
                                reads=["ps7"], writes=["xst"])

                        for hq in range(16):
                            h0 = hq * 4
                            pq = hq % 2
                            aq = pq * 512
                            bq = pq * 256

                            sch.op("gpsimd", lambda e, h0=h0, aq=aq: e.tensor_tensor(
                                out=ATq[:, aq:aq + 512]
                                    .rearrange("p (h t) -> p h t", h=4),
                                in0=rep4(LT[:], 128),
                                in1=col4(a_tm[:, h0:h0 + 4], 128), op=ALU.mult),
                                reads=["a_tm", "masks"], writes=[f"ATq:{pq}"])

                            def margmm(e, aq=aq, pq=pq):
                                for k in range(4):
                                    ins = e.matmul(
                                        PS[pq][:, k * 128:(k + 1) * 128], US_b[:],
                                        ATq[:, aq + k * 128:aq + (k + 1) * 128],
                                        start=True, stop=True)
                                return ins
                            sch.op("tensor", margmm, reads=[f"ATq:{pq}", "masks"],
                                   writes=[psk[pq]])
                            sch.op("scalar", lambda e, aq=aq, pq=pq: e.activation(
                                expmq[:, aq:aq + 512], PS[pq][:, 0:512], AF.Exp),
                                reads=[psk[pq]], writes=[f"expm:{pq}"])
                            for k in range(4):
                                sch.op("vector", lambda e, aq=aq, k=k, h0=h0:
                                       e.scalar_tensor_tensor(
                                    GmTq[:, aq + k * 128:aq + (k + 1) * 128],
                                    expmq[:, aq + k * 128:aq + (k + 1) * 128],
                                    dtp[:, h0 + k:h0 + k + 1], Gt_sb[:],
                                    op0=ALU.mult, op1=ALU.mult),
                                    reads=[f"expm:{pq}", "dtp", "Gt_sb"],
                                    writes=[f"GmT:{pq}"])
                            sch.op("gpsimd", lambda e, h0=h0, bq=bq: e.tensor_tensor(
                                out=Bdq[:, bq:bq + 256]
                                    .rearrange("p (h n) -> p h n", h=4),
                                in0=rep4(BCt[:, 0:64], 64),
                                in1=col4(wdt[:, h0:h0 + 4], 64), op=ALU.mult),
                                reads=["BCt", "wdt"], writes=[f"Bdq:{pq}"])

                            def y1mm(e, aq=aq, pq=pq, h0=h0):
                                for k in range(4):
                                    ins = e.matmul(
                                        PS[2 + pq][:, k * 64:(k + 1) * 64],
                                        GmTq[:, aq + k * 128:aq + (k + 1) * 128],
                                        xst[:, (h0 + k) * 64:(h0 + k + 1) * 64],
                                        start=True, stop=True)
                                return ins
                            sch.op("tensor", y1mm, reads=[f"GmT:{pq}", "xst"],
                                   writes=[psk[2 + pq]])

                            def dsmm(e, bq=bq, h0=h0):
                                for k in range(4):
                                    ins = e.matmul(
                                        PS[5][0:64, k * 64:(k + 1) * 64],
                                        Bdq[:, bq + k * 64:bq + (k + 1) * 64],
                                        xst[:, (h0 + k) * 64:(h0 + k + 1) * 64],
                                        start=True, stop=True)
                                return ins
                            sch.op("tensor", dsmm, reads=[f"Bdq:{pq}", "xst"],
                                   writes=["ps5"])

                            for k2 in range(2):
                                bank = 4 if k2 == 0 else 7

                                def y2mm(e, k2=k2, bank=bank, h0=h0, qa=qa):
                                    for j in range(2):
                                        h = h0 + 2 * k2 + j
                                        ins = e.matmul(
                                            PS[bank][:, j * 64:(j + 1) * 64],
                                            cC[:, qa:qa + QS],
                                            Sb[:, h * 64:(h + 1) * 64],
                                            start=True, stop=True)
                                    return ins
                                sch.op("tensor", y2mm, reads=["cC", f"Sb:{hq}"],
                                       writes=[psk[bank]])

                            sch.op("vector", lambda e, bq=bq, h0=h0: e.tensor_tensor(
                                out=y2b[:, bq:bq + 256]
                                    .rearrange("p (h d) -> p h d", h=4),
                                in0=xst[:, h0 * 64:(h0 + 4) * 64]
                                    .rearrange("p (h d) -> p h d", h=4),
                                in1=col4(p_Dbc[:, h0:h0 + 4], 64), op=ALU.mult),
                                reads=["xst", "p_Dbc"], writes=[f"y2b:{pq}"])
                            sch.op("vector", lambda e, bq=bq, pq=pq: e.tensor_add(
                                y2b[:, bq:bq + 256], y2b[:, bq:bq + 256],
                                PS[2 + pq][:, 0:256]),
                                reads=[psk[2 + pq], f"y2b:{pq}"], writes=[f"y2b:{pq}"])
                            for k in range(4):
                                bank = 4 if k < 2 else 7
                                col = (k % 2) * 64
                                sch.op("vector", lambda e, bq=bq, k=k, h0=h0,
                                       bank=bank, col=col: e.scalar_tensor_tensor(
                                    y2b[:, bq + k * 64:bq + (k + 1) * 64],
                                    PS[bank][:, col:col + 64],
                                    ec_sb[:, h0 + k:h0 + k + 1],
                                    y2b[:, bq + k * 64:bq + (k + 1) * 64],
                                    op0=ALU.mult, op1=ALU.add),
                                    reads=[psk[bank], "ec_sb", f"y2b:{pq}"],
                                    writes=[f"y2b:{pq}"])
                            sch.op("vector", lambda e, h0=h0: e.tensor_tensor(
                                out=tmpS[:].rearrange("p (h d) -> p h d", h=4),
                                in0=Sf[:, h0 * 64:(h0 + 4) * 64]
                                    .rearrange("p (h d) -> p h d", h=4),
                                in1=col4(Ebc[:, h0:h0 + 4], 64), op=ALU.mult),
                                reads=[f"Sf:{hq}", "Ebc"], writes=["tmpS"])
                            sch.op("vector", lambda e, h0=h0: e.tensor_add(
                                Sf[:, h0 * 64:(h0 + 4) * 64], tmpS[:],
                                PS[5][0:64, 0:256]),
                                reads=["tmpS", "ps5"], writes=[f"Sf:{hq}"])
                            sch.op("vector", lambda e, h0=h0: e.tensor_copy(
                                Sb[:, h0 * 64:(h0 + 4) * 64],
                                Sf[:, h0 * 64:(h0 + 4) * 64]),
                                reads=[f"Sf:{hq}"], writes=[f"Sb:{hq}"])
                            sch.op("gpsimd", lambda e, bq=bq: e.tensor_mul(
                                ysqq[:], y2b[:, bq:bq + 256], y2b[:, bq:bq + 256]),
                                reads=[f"y2b:{pq}"], writes=["ysqq"])
                            sch.op("vector", lambda e, hq=hq: e.tensor_reduce(
                                ssqp[:, hq:hq + 1], ysqq[:],
                                mybir.AxisListType.X, ALU.add),
                                reads=["ysqq"], writes=["ssqp"])

                            def prtr(e, bq=bq):
                                for j2 in range(2):
                                    ins = e.transpose(
                                        PS[6][:, j2 * 128:(j2 + 1) * 128],
                                        y2b[:, bq + j2 * 128:bq + (j2 + 1) * 128],
                                        I_f[:])
                                return ins
                            sch.op("tensor", prtr, reads=[f"y2b:{pq}", "masks"],
                                   writes=["ps6"])
                            for j2 in range(2):
                                pr = hq * 2 + j2
                                sch.op("vector", lambda e, pr=pr, j2=j2, qa=qa:
                                       e.scalar_tensor_tensor(
                                    cx[:, pr * TM + qa:pr * TM + qa + QS],
                                    PS[6][:, j2 * 128:(j2 + 1) * 128],
                                    p_nw[:, pr:pr + 1],
                                    sg[:, pr * TM + qa:pr * TM + qa + QS],
                                    op0=ALU.mult, op1=ALU.mult),
                                    reads=["ps6", "p_nw", f"sg:{pr}"],
                                    writes=[f"cx:{pr}"])
                        sch.op("vector", lambda e, q=q: e.tensor_reduce(
                            ssq_tm[:, q:q + 1], ssqp[:], mybir.AxisListType.X, ALU.add),
                            reads=["ssqp"], writes=["ssq_tm"])

                    sch.op("gpsimd", lambda e: e.tensor_scalar(
                        ssq_tm[:], ssq_tm[:], 1.0 / INTER, None, op0=ALU.mult),
                        reads=["cst", "ssq_tm"], writes=["ssq_tm"])
                    sch.op("scalar", lambda e: e.activation(
                        ssq_tm[:], ssq_tm[:], AF.Ln, bias=RMS_EPS),
                        reads=["ssq_tm", "cst"], writes=["ssq_tm"])
                    sch.op("scalar", lambda e: e.activation(
                        ssq_tm[:], ssq_tm[:], AF.Exp, scale=-0.5),
                        reads=["ssq_tm", "cst"], writes=["ssq_tm"])

                    def rtr(e):
                        for q in range(NQ):
                            ins = e.transpose(PS[6][0:1, q * 128:(q + 1) * 128],
                                              ssq_tm[:, q:q + 1], I_f[:])
                        return ins
                    sch.op("tensor", rtr, reads=["ssq_tm", "masks"], writes=["ps6"])
                    sch.op("vector", lambda e: e.tensor_copy(
                        rstd_row[0:1, 0:TM], PS[6][0:1, 0:TM]),
                        reads=["ps6"], writes=["rstd_row"])
                    sch.op("tensor", lambda e: e.matmul(
                        PS[6][:, 0:TM], ones_row_f[:], rstd_row[0:1, 0:TM],
                        start=True, stop=True),
                        reads=["rstd_row", "masks"], writes=["ps6"])
                    sch.op("vector", lambda e: e.tensor_copy(rstd_bc[:], PS[6][:, 0:TM]),
                           reads=["ps6"], writes=["rstd_bc"])

                    for og in range(8):
                        par = og % 2
                        sch.op("sync", lambda e, og=og, par=par: e.dma_start(
                            wob[:, par * 8192:(par + 1) * 8192],
                            osc.ap()[:, og * 8192:(og + 1) * 8192]),
                            reads=[f"osc:{og}:0", f"osc:{og}:1"],
                            writes=[f"wob:{par}"], dma=True)

                        def omm(e, og=og, par=par):
                            for il in range(4):
                                for oc in range(16):
                                    ins = e.matmul(
                                        PS[oc // 2][0:128, (oc % 2) * 256:
                                                    (oc % 2) * 256 + 256],
                                        wob[:, par * 8192 + il * 2048 + oc * 128:
                                            par * 8192 + il * 2048 + oc * 128 + 128],
                                        cx[:, (og * 4 + il) * TM:(og * 4 + il + 1) * TM],
                                        start=(og == 0 and il == 0 and oc % 2 == 0),
                                        stop=(og == 7 and il == 3 and oc % 2 == 1),
                                        skip_group_check=True)
                            return ins
                        sch.op("tensor", omm,
                               reads=[f"wob:{par}"] +
                                     [f"cx:{og * 4 + i}" for i in range(4)],
                               writes=psk)
                    for oc in range(16):
                        sch.op("vector", lambda e, oc=oc: e.tensor_mul(
                            oacc[:], PS[oc // 2][:, (oc % 2) * 256:(oc % 2) * 256 + 256],
                            rstd_bc[:]),
                            reads=[psk[oc // 2], "rstd_bc"], writes=["oacc"])
                        sch.op("vector", lambda e, oc=oc: e.tensor_add(
                            resid[:, oc * TM:(oc + 1) * TM],
                            resid[:, oc * TM:(oc + 1) * TM], oacc[:]),
                            reads=["oacc"], writes=["resid"])

                    sch.op("scalar", lambda e, t0=t0: e.dma_start(
                        dst.ap()[:, t0:t0 + TM].rearrange("(c p) t -> p c t", p=128),
                        resid.ap().rearrange("p (c t) -> p c t", t=TM)),
                        reads=["resid"], dma=True)

        if "ffn1" in phases:
            emit_ffn(xT, X1, do_conversion=True)
            sch.barrier()
        if "mamba" in phases:
            emit_mamba(X1, X2)
            sch.barrier()
        if "ffn2" in phases:
            emit_ffn(X2, outT, do_conversion="ffn1" not in phases)

        counts, ctr, dctr = sch.emit()
        print("op counts:", counts, "incs:", ctr, "dma:", dctr)
    return nc


_NC_CACHE = {}


def kernel(**inputs):
    x = np.ascontiguousarray(inputs["L0_attn_hidden_states"], dtype=np.float32)
    key = "full"
    if key not in _NC_CACHE:
        _NC_CACHE[key] = build_nc()
    nc = _NC_CACHE[key]
    base = {
        "wgT": np.ascontiguousarray(np.asarray(inputs["L0_ffn_w_gate"], np.float32).T),
        "wuT": np.ascontiguousarray(np.asarray(inputs["L0_ffn_w_up"], np.float32).T),
        "wdT": np.ascontiguousarray(np.asarray(inputs["L0_ffn_w_down"], np.float32).T),
        "winT": np.ascontiguousarray(np.asarray(inputs["L0_attn_in_proj_w"], np.float32).T),
        "woT": np.ascontiguousarray(np.asarray(inputs["L0_attn_out_proj_w"], np.float32).T),
        "cw": np.ascontiguousarray(inputs["L0_attn_conv_w"], np.float32),
        "cb": np.ascontiguousarray(inputs["L0_attn_conv_b"], np.float32),
        "dtb": np.ascontiguousarray(inputs["L0_attn_dt_bias"], np.float32),
        "alog": np.ascontiguousarray(inputs["L0_attn_A_log"], np.float32),
        "dvec": np.ascontiguousarray(inputs["L0_attn_D"], np.float32),
        "nw": np.ascontiguousarray(inputs["L0_attn_norm_weight"], np.float32),
    }
    in_maps = []
    for c in range(NCORES):
        m = dict(base)
        m["xT"] = np.ascontiguousarray(x[c].T)
        in_maps.append(m)
    res = run_bass_kernel_spmd(nc, in_maps, core_ids=list(range(NCORES)))
    out = np.stack([np.ascontiguousarray(np.asarray(r["outT"]).T)
                    for r in res.results], axis=0)
    return np.ascontiguousarray(out.astype(np.float32, copy=False))


if __name__ == "__main__":
    import time
    t = time.time()
    nc = build_nc()
    print("built in %.1fs" % (time.time() - t))


# version 12 (final)
# speedup vs baseline: 1.2587x; 1.0948x over previous
import contextlib
import numpy as np

import concourse.bass as bass
import concourse.mybir as mybir
from concourse.bass_utils import run_bass_kernel_spmd

F32 = mybir.dt.float32
BF = mybir.dt.bfloat16
AF = mybir.ActivationFunctionType
ALU = mybir.AluOpType

NCORES = 8
DM = 2048
LTOK = 2048
FFNH = 8192
INTER = 4096
HEADS = 64
PD = 64
PROJ = 8384
CONVD = 4224
LN_EPS = 1e-5
RMS_EPS = 1e-6

TFF = 512
NFB = LTOK // TFF
GH = 256
NGH = FFNH // GH
TM = 256
NMT = LTOK // TM
QS = 128
NQ = TM // QS

ENGS = ("sync", "scalar", "tensor", "vector", "gpsimd")


NDMASEM = 6


class _Op:
    __slots__ = ("eng", "fn", "dma", "deps", "sem_val", "need_inc", "idx", "slot")

    def __init__(self, eng, fn, dma):
        self.eng = eng
        self.fn = fn
        self.dma = dma
        self.deps = []
        self.sem_val = None
        self.need_inc = dma
        self.idx = -1
        self.slot = 0


class Sched:

    def __init__(self, nc, stack, tag=""):
        self.nc = nc
        self.ops = []
        self.last_writer = {}
        self.readers = {}
        self.guards = []
        S = stack.enter_context
        self.eng_sem = {e: S(nc.semaphore(f"{tag}e_{e}")) for e in ENGS}
        self.dma_sem = {e: [S(nc.semaphore(f"{tag}d_{e}{i}")) for i in range(NDMASEM)]
                        for e in ("sync", "scalar", "gpsimd")}
        self.dma_rr = {e: 0 for e in self.dma_sem}

    def op(self, eng, fn, reads=(), writes=(), dma=False, after=()):
        o = _Op(eng, fn, dma)
        o.idx = len(self.ops)
        if dma:
            o.slot = self.dma_rr[eng] % NDMASEM
            self.dma_rr[eng] += 1
        deps = set(self.guards)
        for r in reads:
            w = self.last_writer.get(r)
            if w is not None:
                deps.add(w)
        for w in writes:
            lw = self.last_writer.get(w)
            if lw is not None:
                deps.add(lw)
            deps.update(self.readers.get(w, ()))
        for d in after:
            if d is not None:
                deps.add(d)
        deps.discard(o)
        o.deps = sorted(deps, key=lambda d: d.idx)
        self.ops.append(o)
        for r in reads:
            self.readers.setdefault(r, []).append(o)
        for w in writes:
            self.last_writer[w] = o
            self.readers[w] = []
        return o

    def barrier(self):
        last = {}
        for o in self.ops:
            last[(o.eng, o.dma)] = o
        g = self.op("sync", lambda e: e.nop(), after=list(last.values()))
        self.guards = [g]
        return g

    def emit(self, final_engine="sync"):
        for o in self.ops:
            for d in o.deps:
                if d.dma or o.dma or d.eng != o.eng:
                    d.need_inc = True
        last_per_eng = {}
        for o in self.ops:
            last_per_eng[o.eng] = o
        for o in last_per_eng.values():
            o.need_inc = True
        ctr = {e: 0 for e in ENGS}
        dctr = {(e, i): 0 for e in self.dma_sem for i in range(NDMASEM)}
        for o in self.ops:
            if o.dma:
                dctr[(o.eng, o.slot)] += 16
                o.sem_val = dctr[(o.eng, o.slot)]
            elif o.need_inc:
                ctr[o.eng] += 1
                o.sem_val = ctr[o.eng]
        for e, c in ctr.items():
            if c >= 60000:
                raise RuntimeError(f"semaphore range risk on {e}: {c}")
        for k, c in dctr.items():
            if c >= 60000:
                raise RuntimeError(f"semaphore range risk on {k}: {c}")
        finals = []
        for (e, i), c in dctr.items():
            if c:
                finals.append((self.dma_sem[e][i], c))
        for e, c in ctr.items():
            if c and e != final_engine:
                finals.append((self.eng_sem[e], c))
        nc = self.nc
        with nc.Block() as block:
            for eng_name in ENGS:
                items = [o for o in self.ops if o.eng == eng_name]
                if not items and eng_name != final_engine:
                    continue

                def make(items, eng_name=eng_name):
                    def f(eng):
                        hw = {}
                        for o in items:
                            for d in o.deps:
                                if d.dma:
                                    sem = self.dma_sem[d.eng][d.slot]
                                elif d.eng != o.eng or o.dma:
                                    sem = self.eng_sem[d.eng]
                                else:
                                    continue
                                key = id(sem)
                                if hw.get(key, 0) < d.sem_val:
                                    eng.wait_ge(sem, d.sem_val)
                                    hw[key] = d.sem_val
                            if o.dma and o.sem_val > 16:
                                sem = self.dma_sem[o.eng][o.slot]
                                key = id(sem)
                                if hw.get(key, 0) < o.sem_val - 16:
                                    eng.wait_ge(sem, o.sem_val - 16)
                                    hw[key] = o.sem_val - 16
                            ins = o.fn(eng)
                            if o.need_inc:
                                if o.dma:
                                    ins.then_inc(self.dma_sem[o.eng][o.slot], 16)
                                else:
                                    ins.then_inc(self.eng_sem[o.eng], 1)
                        if eng_name == final_engine:
                            for sem, v in finals:
                                eng.wait_ge(sem, v)
                    return f

                getattr(block, eng_name)(make(items))
        return ({e: sum(1 for o in self.ops if o.eng == e) for e in ENGS},
                ctr, dctr)


_uid = [0]


def _nm(b):
    _uid[0] += 1
    return f"{b}_{_uid[0]}"


def build_nc(debug_mid=False, phases=("ffn1", "mamba", "ffn2")):
    nc = bass.Bass("TRN2", target_bir_lowering=False)

    xT = nc.dram_tensor("xT", [DM, LTOK], F32, kind="ExternalInput")
    wgT = nc.dram_tensor("wgT", [DM, FFNH], F32, kind="ExternalInput")
    wuT = nc.dram_tensor("wuT", [DM, FFNH], F32, kind="ExternalInput")
    wdT = nc.dram_tensor("wdT", [FFNH, DM], F32, kind="ExternalInput")
    winT = nc.dram_tensor("winT", [DM, PROJ], F32, kind="ExternalInput")
    woT = nc.dram_tensor("woT", [INTER, DM], F32, kind="ExternalInput")
    cw = nc.dram_tensor("cw", [CONVD, 4], F32, kind="ExternalInput")
    cb = nc.dram_tensor("cb", [CONVD], F32, kind="ExternalInput")
    dtb = nc.dram_tensor("dtb", [HEADS], F32, kind="ExternalInput")
    alog = nc.dram_tensor("alog", [HEADS], F32, kind="ExternalInput")
    dvec = nc.dram_tensor("dvec", [HEADS], F32, kind="ExternalInput")
    nw = nc.dram_tensor("nw", [INTER], F32, kind="ExternalInput")
    outT = nc.dram_tensor("outT", [DM, LTOK], F32, kind="ExternalOutput")

    mid = "ExternalOutput" if debug_mid else "Internal"
    gsc = nc.dram_tensor("gsc", [128, NGH * 16 * GH], BF, kind="Internal")
    usc = nc.dram_tensor("usc", [128, NGH * 16 * GH], BF, kind="Internal")
    dsc = nc.dram_tensor("dsc", [128, 16 * 8192], BF, kind="Internal")
    isc = nc.dram_tensor("isc", [128, 32 * 4096 + 3072], BF, kind="Internal")
    osc = nc.dram_tensor("osc", [128, 8 * 8192], BF, kind="Internal")
    X1 = nc.dram_tensor("X1", [DM, LTOK], F32, kind=mid)
    X2 = nc.dram_tensor("X2", [DM, LTOK], F32, kind=mid)

    CONSTS = [LN_EPS, RMS_EPS, -0.5, -1.0, 1.0, 0.5, 1.0 / DM, 1.0 / INTER]

    es = contextlib.ExitStack()
    with es:
        S = es.enter_context
        sch = Sched(nc, es)

        cst = S(nc.sbuf_tensor("cst", [128, len(CONSTS)], F32))
        for i, cv in enumerate(CONSTS):
            nc.const_aps.register(F32, float(cv), cst[:, i:i + 1])

        ones_col_f = S(nc.sbuf_tensor("ones_col_f", [128, 1], F32))
        ones_col_b = S(nc.sbuf_tensor("ones_col_b", [128, 1], BF))
        ones_row_f = S(nc.sbuf_tensor("ones_row_f", [1, 128], F32))
        I_f = S(nc.sbuf_tensor("I_f", [128, 128], F32))
        I_b = S(nc.sbuf_tensor("I_b", [128, 128], BF))
        LT = S(nc.sbuf_tensor("LT", [128, 128], F32))
        US = S(nc.sbuf_tensor("US", [128, 128], F32))
        US_b = S(nc.sbuf_tensor("US_b", [128, 128], BF))
        NEGM = S(nc.sbuf_tensor("NEGM", [128, 128], F32))
        p_cw = S(nc.sbuf_tensor("p_cw", [128, 33 * 4], F32))
        p_cb = S(nc.sbuf_tensor("p_cb", [128, 33], F32))
        p_nw = S(nc.sbuf_tensor("p_nw", [128, 32], F32))
        p_Abc = S(nc.sbuf_tensor("p_Abc", [128, 64], F32))
        p_dtbbc = S(nc.sbuf_tensor("p_dtbbc", [128, 64], F32))
        p_Dbc = S(nc.sbuf_tensor("p_Dbc", [128, 64], F32))
        prow = S(nc.sbuf_tensor("prow", [1, 192], F32))
        arow = S(nc.sbuf_tensor("arow", [1, 192], F32))
        pstage = S(nc.sbuf_tensor("pstage", [33, 128], F32))
        nstage = S(nc.sbuf_tensor("nstage", [32, 128], F32))

        PS = [S(nc.psum_tensor(f"ps{k}", [128, 512], F32)) for k in range(8)]
        PSB = [p.ap().bitcast(BF) for p in PS]
        psk = [f"ps{k}" for k in range(8)]

        def set_consts(e):
            ins = None
            for i, cv in enumerate(CONSTS):
                ins = e.memset(cst[:, i:i + 1], float(cv))
            return ins
        sch.op("gpsimd", set_consts, writes=["cst"])

        def g_masks(e):
            e.memset(ones_col_f[:], 1.0)
            e.memset(ones_col_b[:], 1.0)
            e.memset(ones_row_f[:], 1.0)
            e.memset(I_f[:], 1.0)
            e.affine_select(out=I_f[:], in_=I_f[:], compare_op=ALU.is_equal,
                            fill=0.0, base=0, pattern=[[1, 128]], channel_multiplier=-1)
            e.memset(I_b[:], 1.0)
            e.affine_select(out=I_b[:], in_=I_b[:], compare_op=ALU.is_equal,
                            fill=0.0, base=0, pattern=[[1, 128]], channel_multiplier=-1)
            e.memset(LT[:], 1.0)
            e.affine_select(out=LT[:], in_=LT[:], compare_op=ALU.is_ge,
                            fill=0.0, base=0, pattern=[[1, 128]], channel_multiplier=-1)
            e.memset(US[:], 1.0)
            e.affine_select(out=US[:], in_=US[:], compare_op=ALU.is_ge,
                            fill=0.0, base=-1, pattern=[[-1, 128]], channel_multiplier=1)
            e.memset(US_b[:], 1.0)
            e.affine_select(out=US_b[:], in_=US_b[:], compare_op=ALU.is_ge,
                            fill=0.0, base=-1, pattern=[[-1, 128]], channel_multiplier=1)
            e.memset(NEGM[:], 0.0)
            return e.affine_select(out=NEGM[:], in_=NEGM[:], compare_op=ALU.is_ge,
                                   fill=-1.0e30, base=0, pattern=[[1, 128]],
                                   channel_multiplier=-1)
        sch.op("gpsimd", g_masks, writes=["masks"])

        sch.op("sync", lambda e: e.dma_start(
            prow[:, 0:64], alog.ap().rearrange("(o h) -> o h", o=1)),
            writes=["prow"], dma=True)
        sch.op("sync", lambda e: e.dma_start(
            prow[:, 64:128], dtb.ap().rearrange("(o h) -> o h", o=1)),
            writes=["prow"], dma=True)
        sch.op("sync", lambda e: e.dma_start(
            prow[:, 128:192], dvec.ap().rearrange("(o h) -> o h", o=1)),
            writes=["prow"], dma=True)
        sch.op("sync", lambda e: e.dma_start(
            pstage[:], cb.ap().rearrange("(c q) -> c q", q=128)),
            writes=["pstage"], dma=True)
        sch.op("sync", lambda e: e.dma_start(
            nstage[:], nw.ap().rearrange("(c q) -> c q", q=128)),
            writes=["nstage"], dma=True)
        sch.op("sync", lambda e: e.dma_start(
            p_cw.ap().rearrange("p (c k) -> p c k", k=4),
            cw.ap().rearrange("(c p) k -> p c k", p=128)),
            writes=["p_cw"], dma=True)

        sch.op("scalar", lambda e: e.activation(arow[:, 0:64], prow[:, 0:64], AF.Exp),
               reads=["prow"], writes=["arow"])
        sch.op("vector", lambda e: e.tensor_scalar_mul(arow[:, 0:64], arow[:, 0:64], -1.0),
               reads=["cst"], writes=["arow"])
        sch.op("vector", lambda e: e.tensor_copy(arow[:, 64:192], prow[:, 64:192]),
               reads=["prow"], writes=["arow"])

        def bc3(e):
            e.matmul(PS[0][:, 0:64], ones_row_f[:], arow[:, 0:64], start=True, stop=True)
            e.matmul(PS[0][:, 64:128], ones_row_f[:], arow[:, 64:128], start=True, stop=True)
            return e.matmul(PS[0][:, 128:192], ones_row_f[:], arow[:, 128:192],
                            start=True, stop=True)
        sch.op("tensor", bc3, reads=["arow", "masks"], writes=["ps0"])
        sch.op("vector", lambda e: e.tensor_copy(p_Abc[:], PS[0][:, 0:64]),
               reads=["ps0"], writes=["p_Abc"])
        sch.op("vector", lambda e: e.tensor_copy(p_dtbbc[:], PS[0][:, 64:128]),
               reads=["ps0"], writes=["p_dtbbc"])
        sch.op("vector", lambda e: e.tensor_copy(p_Dbc[:], PS[0][:, 128:192]),
               reads=["ps0"], writes=["p_Dbc"])

        def tr2(e):
            e.transpose(PS[1][:, 0:33], pstage[:], I_f[0:33, 0:33])
            return e.transpose(PS[1][:, 64:96], nstage[:], I_f[0:32, 0:32])
        sch.op("tensor", tr2, reads=["pstage", "nstage", "masks"], writes=["ps1"])
        sch.op("vector", lambda e: e.tensor_copy(p_cb[:], PS[1][:, 0:33]),
               reads=["ps1"], writes=["p_cb"])
        sch.op("vector", lambda e: e.tensor_copy(p_nw[:], PS[1][:, 64:96]),
               reads=["ps1"], writes=["p_nw"])

        conv_items = []

        def add_wgu(g):
            conv_items.append((f"gsc:{g}",
                lambda g=g: wgT.ap()[:, g * GH:(g + 1) * GH]
                    .rearrange("(c p) h -> p c h", p=128),
                lambda g=g: gsc.ap()[:, g * 4096:(g + 1) * 4096]
                    .rearrange("p (c h) -> p c h", h=GH)))
            conv_items.append((f"usc:{g}",
                lambda g=g: wuT.ap()[:, g * GH:(g + 1) * GH]
                    .rearrange("(c p) h -> p c h", p=128),
                lambda g=g: usc.ap()[:, g * 4096:(g + 1) * 4096]
                    .rearrange("p (c h) -> p c h", h=GH)))

        for g in range(NGH):
            add_wgu(g)
        for gg in range(16):
            for hh in range(2):
                conv_items.append((f"dsc:{gg}:{hh}",
                    lambda gg=gg, hh=hh: wdT.ap()[gg * 512 + hh * 256:
                                                  gg * 512 + hh * 256 + 256, :]
                        .rearrange("(hl p) d -> p hl d", p=128),
                    lambda gg=gg, hh=hh: dsc.ap()[:, gg * 8192 + hh * 4096:
                                                  gg * 8192 + hh * 4096 + 4096]
                        .rearrange("p (hl d) -> p hl d", d=2048)))
        for G in range(32):
            conv_items.append((f"isc:{G}",
                lambda G=G: winT.ap()[:, G * 256:(G + 1) * 256]
                    .rearrange("(c p) h -> p c h", p=128),
                lambda G=G: isc.ap()[:, G * 4096:(G + 1) * 4096]
                    .rearrange("p (c h) -> p c h", h=256)))
        conv_items.append(("isc:32",
            lambda: winT.ap()[:, 8192:8384].rearrange("(c p) h -> p c h", p=128),
            lambda: isc.ap()[:, 131072:131072 + 3072]
                .rearrange("p (c h) -> p c h", h=192)))
        for og in range(8):
            for ih in range(2):
                conv_items.append((f"osc:{og}:{ih}",
                    lambda og=og, ih=ih: woT.ap()[og * 512 + ih * 256:
                                                  og * 512 + ih * 256 + 256, :]
                        .rearrange("(il p) d -> p il d", p=128),
                    lambda og=og, ih=ih: osc.ap()[:, og * 8192 + ih * 4096:
                                                  og * 8192 + ih * 4096 + 4096]
                        .rearrange("p (il d) -> p il d", d=2048)))

        def emit_conversion():
            for key, src_fn, dst_fn in conv_items:
                sch.op("gpsimd", lambda e, src_fn=src_fn, dst_fn=dst_fn:
                       e.dma_start(dst_fn(), src_fn()),
                       writes=[key], dma=True)

        def emit_ln(T, resid, xln, sqbuf_fn, lnsc, rows, sq_keys):
            for dc in range(16):
                sch.op("vector", lambda e, dc=dc: e.tensor_copy(
                    xln[:, dc * T:(dc + 1) * T], resid[:, dc * T:(dc + 1) * T]),
                    reads=["resid"], writes=["xln"])
            for dc in range(16):
                sch.op("scalar", lambda e, dc=dc: e.activation(
                    sqbuf_fn(dc), resid[:, dc * T:(dc + 1) * T], AF.Square),
                    reads=["resid"], writes=[sq_keys[dc]])

            def s1(e):
                for dc in range(16):
                    ins = e.matmul(PS[0][0:1, 0:T], ones_col_b[:],
                                   xln[:, dc * T:(dc + 1) * T],
                                   start=(dc == 0), stop=(dc == 15))
                return ins
            sch.op("tensor", s1, reads=["xln", "masks"], writes=["ps0"])

            def s2(e):
                for dc in range(16):
                    ins = e.matmul(PS[1][0:1, 0:T], ones_col_b[:], sqbuf_fn(dc),
                                   start=(dc == 0), stop=(dc == 15))
                return ins
            sch.op("tensor", s2, reads=list(sq_keys) + ["masks"], writes=["ps1"])

            m_r = rows[:, 0:T]
            t_r = rows[:, 512:512 + T]
            sch.op("scalar", lambda e: e.mul(m_r, PS[0][0:1, 0:T], 1.0 / DM),
                   reads=["ps0", "cst"], writes=["rows_m"])
            sch.op("scalar", lambda e: e.activation(t_r, m_r, AF.Square),
                   reads=["rows_m"], writes=["rows_t"])
            sch.op("vector", lambda e: e.scalar_tensor_tensor(
                t_r, PS[1][0:1, 0:T], 1.0 / DM, t_r, op0=ALU.mult, op1=ALU.subtract),
                reads=["ps1", "rows_t", "cst"], writes=["rows_t"])
            sch.op("scalar", lambda e: e.activation(t_r, t_r, AF.Ln, bias=LN_EPS),
                   reads=["rows_t", "cst"], writes=["rows_t"])
            sch.op("scalar", lambda e: e.activation(t_r, t_r, AF.Exp, scale=-0.5),
                   reads=["rows_t", "cst"], writes=["rows_t"])
            sch.op("vector", lambda e: e.tensor_mul(m_r, m_r, t_r),
                   reads=["rows_m", "rows_t"], writes=["rows_m"])
            sch.op("tensor", lambda e: e.matmul(PS[2][:, 0:T], ones_row_f[:], t_r,
                                                start=True, stop=True),
                   reads=["rows_t", "masks"], writes=["ps2"])
            sch.op("tensor", lambda e: e.matmul(PS[3][:, 0:T], ones_row_f[:], m_r,
                                                start=True, stop=True),
                   reads=["rows_m", "masks"], writes=["ps3"])
            for dc in range(16):
                sch.op("vector", lambda e, dc=dc: e.tensor_mul(
                    lnsc[:, 0:T], resid[:, dc * T:(dc + 1) * T], PS[2][:, 0:T]),
                    reads=["resid", "ps2"], writes=["lnsc"])
                sch.op("vector", lambda e, dc=dc: e.tensor_sub(
                    xln[:, dc * T:(dc + 1) * T], lnsc[:, 0:T], PS[3][:, 0:T]),
                    reads=["lnsc", "ps3"], writes=["xln"])

        def emit_ffn(src, dst, do_conversion):
            with contextlib.ExitStack() as fs:
                FS = fs.enter_context
                resid = FS(nc.sbuf_tensor(_nm("f_resid"), [128, 16 * TFF], F32))
                xln = FS(nc.sbuf_tensor(_nm("f_xln"), [128, 16 * TFF], BF))
                actb = FS(nc.sbuf_tensor(_nm("f_act"), [128, 64 * TFF], BF))
                wgb = FS(nc.sbuf_tensor(_nm("f_wg"), [128, 2 * 4096], BF))
                wub = FS(nc.sbuf_tensor(_nm("f_wu"), [128, 2 * 4096], BF))
                wdb = FS(nc.sbuf_tensor(_nm("f_wd"), [128, 2 * 4096], BF))
                lnsc = FS(nc.sbuf_tensor(_nm("f_lnsc"), [128, TFF], F32))
                rows = FS(nc.sbuf_tensor(_nm("f_rows"), [1, 1024], F32))

                if do_conversion:
                    emit_conversion()

                for blk in range(NFB):
                    t0 = blk * TFF
                    sch.op("scalar", lambda e, t0=t0: e.dma_start(
                        resid.ap().rearrange("p (c t) -> p c t", t=TFF),
                        src.ap()[:, t0:t0 + TFF].rearrange("(c p) t -> p c t", p=128)),
                        writes=["resid"], dma=True)

                    emit_ln(TFF, resid, xln,
                            lambda dc: actb[:, dc * TFF:(dc + 1) * TFF], lnsc, rows,
                            [f"act:{dc}" for dc in range(16)])

                    for g in range(NGH):
                        par = g % 2
                        sch.op("sync", lambda e, g=g, par=par: e.dma_start(
                            wgb[:, par * 4096:(par + 1) * 4096],
                            gsc.ap()[:, g * 4096:(g + 1) * 4096]),
                            reads=[f"gsc:{g}"], writes=[f"wgb:{par}"], dma=True)
                        sch.op("sync", lambda e, g=g, par=par: e.dma_start(
                            wub[:, par * 4096:(par + 1) * 4096],
                            usc.ap()[:, g * 4096:(g + 1) * 4096]),
                            reads=[f"usc:{g}"], writes=[f"wub:{par}"], dma=True)
                        for hl in range(2):
                            ht = g * 2 + hl
                            pp = 4 + ht % 2

                            def gmm(e, par=par, hl=hl, pp=pp):
                                for dc in range(16):
                                    ins = e.matmul(
                                        PS[pp][:],
                                        wgb[:, par * 4096 + dc * GH + hl * 128:
                                            par * 4096 + dc * GH + hl * 128 + 128],
                                        xln[:, dc * TFF:(dc + 1) * TFF],
                                        start=(dc == 0), stop=(dc == 15))
                                return ins
                            sch.op("tensor", gmm, reads=[f"wgb:{par}", "xln"],
                                   writes=[psk[pp]])
                            sch.op("scalar", lambda e, ht=ht, pp=pp: e.activation(
                                actb[:, ht * TFF:(ht + 1) * TFF], PS[pp][:], AF.Silu),
                                reads=[psk[pp]], writes=[f"act:{ht}"])

                            pq = 6 + ht % 2

                            def umm(e, par=par, hl=hl, pq=pq):
                                for dc in range(16):
                                    ins = e.matmul(
                                        PS[pq][:],
                                        wub[:, par * 4096 + dc * GH + hl * 128:
                                            par * 4096 + dc * GH + hl * 128 + 128],
                                        xln[:, dc * TFF:(dc + 1) * TFF],
                                        start=(dc == 0), stop=(dc == 15))
                                return ins
                            sch.op("tensor", umm, reads=[f"wub:{par}", "xln"],
                                   writes=[psk[pq]])
                            sch.op("vector", lambda e, ht=ht, pq=pq: e.scalar_tensor_tensor(
                                actb[:, ht * TFF:(ht + 1) * TFF],
                                actb[:, ht * TFF:(ht + 1) * TFF], 0.5, PS[pq][:],
                                op0=ALU.mult, op1=ALU.mult),
                                reads=[f"act:{ht}", psk[pq], "cst"], writes=[f"act:{ht}"])

                    for hh in range(2):
                        for gg in range(16):
                            par = gg % 2
                            sch.op("sync", lambda e, gg=gg, hh=hh, par=par: e.dma_start(
                                wdb.ap()[:, par * 4096:(par + 1) * 4096]
                                .rearrange("p (hl d) -> p hl d", d=1024),
                                dsc.ap()[:, gg * 8192:(gg + 1) * 8192]
                                .rearrange("p (hl d) -> p hl d", d=2048)
                                [:, :, hh * 1024:(hh + 1) * 1024]),
                                reads=[f"dsc:{gg}:0", f"dsc:{gg}:1"],
                                writes=[f"wdb:{par}"], dma=True)
                            for oc8 in range(8):
                                def dmm(e, gg=gg, par=par, oc8=oc8):
                                    for hl in range(4):
                                        ins = e.matmul(
                                            PS[oc8][:],
                                            wdb[:, par * 4096 + hl * 1024 + oc8 * 128:
                                                par * 4096 + hl * 1024 + oc8 * 128 + 128],
                                            actb[:, (gg * 4 + hl) * TFF:
                                                 (gg * 4 + hl + 1) * TFF],
                                            start=(gg == 0 and hl == 0),
                                            stop=(gg == 15 and hl == 3))
                                    return ins
                                sch.op("tensor", dmm,
                                       reads=[f"wdb:{par}"] +
                                             [f"act:{gg * 4 + i}" for i in range(4)],
                                       writes=[psk[oc8]])
                        for oc8 in range(8):
                            oc = hh * 8 + oc8
                            sch.op("vector", lambda e, oc=oc, oc8=oc8: e.tensor_add(
                                resid[:, oc * TFF:(oc + 1) * TFF],
                                resid[:, oc * TFF:(oc + 1) * TFF], PS[oc8][:]),
                                reads=[psk[oc8]], writes=["resid"])

                    sch.op("scalar", lambda e, t0=t0: e.dma_start(
                        dst.ap()[:, t0:t0 + TFF].rearrange("(c p) t -> p c t", p=128),
                        resid.ap().rearrange("p (c t) -> p c t", t=TFF)),
                        reads=["resid"], dma=True)

        def emit_mamba(src, dst):
            with contextlib.ExitStack() as ms:
                MS = ms.enter_context
                resid = MS(nc.sbuf_tensor(_nm("m_resid"), [128, 16 * TM], F32))
                xln = MS(nc.sbuf_tensor(_nm("m_xln"), [128, 16 * TM], BF))
                wib = MS(nc.sbuf_tensor(_nm("m_wi"), [128, 2 * 8192], BF))
                wob = MS(nc.sbuf_tensor(_nm("m_wo"), [128, 2 * 8192], BF))
                sg = MS(nc.sbuf_tensor(_nm("m_sg"), [128, 32 * TM], BF))
                cx = MS(nc.sbuf_tensor(_nm("m_cx"), [128, 32 * TM], BF))
                cB = MS(nc.sbuf_tensor(_nm("m_cB"), [64, TM], BF))
                cC = MS(nc.sbuf_tensor(_nm("m_cC"), [64, TM], BF))
                dtf = MS(nc.sbuf_tensor(_nm("m_dtf"), [64, TM], F32))
                cin = MS(nc.sbuf_tensor(_nm("m_cin"), [128, 2 * (TM + 3)], F32))
                cacc = MS(nc.sbuf_tensor(_nm("m_cacc"), [128, 2 * TM], F32))
                halo = MS(nc.sbuf_tensor(_nm("m_halo"), [128, 99], F32))
                Sf = MS(nc.sbuf_tensor(_nm("m_Sf"), [64, HEADS * PD], F32))
                Sb = MS(nc.sbuf_tensor(_nm("m_Sb"), [64, HEADS * PD], BF))
                xst = MS(nc.sbuf_tensor(_nm("m_xst"), [128, 32 * 128], BF))
                sp1 = MS(nc.sbuf_tensor(_nm("m_sp1"), [128, 64], F32))
                sp2 = MS(nc.sbuf_tensor(_nm("m_sp2"), [128, 64], F32))
                dtp = MS(nc.sbuf_tensor(_nm("m_dtp"), [128, 64], F32))
                a_tm = MS(nc.sbuf_tensor(_nm("m_a"), [128, 64], F32))
                c_sb = MS(nc.sbuf_tensor(_nm("m_c"), [128, 64], F32))
                cq_sb = MS(nc.sbuf_tensor(_nm("m_cq"), [1, 64], F32))
                wdec = MS(nc.sbuf_tensor(_nm("m_wdec"), [128, 64], F32))
                wdt = MS(nc.sbuf_tensor(_nm("m_wdt"), [128, 64], F32))
                ec_sb = MS(nc.sbuf_tensor(_nm("m_ec"), [128, 64], F32))
                Ebc = MS(nc.sbuf_tensor(_nm("m_Ebc"), [64, 64], F32))
                Gt_sb = MS(nc.sbuf_tensor(_nm("m_Gt"), [128, 128], F32))
                BCt = MS(nc.sbuf_tensor(_nm("m_BCt"), [128, 64], BF))
                ATq = MS(nc.sbuf_tensor(_nm("m_ATq"), [128, 2 * 512], BF))
                expmq = MS(nc.sbuf_tensor(_nm("m_expq"), [128, 2 * 512], BF))
                GmTq = MS(nc.sbuf_tensor(_nm("m_GmTq"), [128, 2 * 512], BF))
                Bdq = MS(nc.sbuf_tensor(_nm("m_Bdq"), [128, 2 * 256], BF))
                y2b = MS(nc.sbuf_tensor(_nm("m_y2"), [128, 2 * 256], F32))
                tmpS = MS(nc.sbuf_tensor(_nm("m_tmpS"), [64, 256], F32))
                ysqq = MS(nc.sbuf_tensor(_nm("m_ysqq"), [128, 256], F32))
                ssqp = MS(nc.sbuf_tensor(_nm("m_ssqp"), [128, 16], F32))
                ssq_tm = MS(nc.sbuf_tensor(_nm("m_ssqtm"), [128, NQ], F32))
                rstd_row = MS(nc.sbuf_tensor(_nm("m_rsr"), [1, TM], F32))
                rstd_bc = MS(nc.sbuf_tensor(_nm("m_rsb"), [128, TM], F32))
                oacc = MS(nc.sbuf_tensor(_nm("m_oacc"), [128, TM], F32))
                lnsc = MS(nc.sbuf_tensor(_nm("m_lnsc"), [128, TM], F32))
                rows = MS(nc.sbuf_tensor(_nm("m_rows"), [1, 1024], F32))
                sqb = MS(nc.sbuf_tensor(_nm("m_sq"), [128, 16 * TM], BF))

                def g_init(e):
                    e.memset(halo[:], 0.0)
                    e.memset(Sf[:], 0.0)
                    return e.memset(Sb[:], 0.0)
                sch.op("gpsimd", g_init,
                       writes=[f"halo:{c}" for c in range(33)] +
                              [f"Sf:{hq}" for hq in range(16)] +
                              [f"Sb:{hq}" for hq in range(16)])

                def rep4(ap2d, nfree):
                    return ap2d.rearrange("p (o n) -> p o n", o=1) \
                               .broadcast_to([ap2d.shape[0], 4, nfree])

                def col4(ap2d, nfree):
                    return ap2d.rearrange("p (h o) -> p h o", o=1) \
                               .broadcast_to([ap2d.shape[0], 4, nfree])

                pin_ctr = [0]
                for mt in range(NMT):
                    t0 = mt * TM
                    sch.op("scalar", lambda e, t0=t0: e.dma_start(
                        resid.ap().rearrange("p (c t) -> p c t", t=TM),
                        src.ap()[:, t0:t0 + TM].rearrange("(c p) t -> p c t", p=128)),
                        writes=["resid"], dma=True)

                    emit_ln(TM, resid, xln,
                            lambda dc: sqb[:, dc * TM:(dc + 1) * TM], lnsc, rows,
                            [f"sq:{dc}" for dc in range(16)])

                    for Gp in range(17):
                        par = Gp % 2
                        if Gp < 16:
                            nw_ = 8192
                            fcs = [4 * Gp, 4 * Gp + 1, 4 * Gp + 2, 4 * Gp + 3]
                        else:
                            nw_ = 3072
                            fcs = [64, 65]
                        sch.op("sync", lambda e, Gp=Gp, par=par, nw_=nw_: e.dma_start(
                            wib[:, par * 8192:par * 8192 + nw_],
                            isc.ap()[:, Gp * 8192:Gp * 8192 + nw_]),
                            reads=[f"isc:{2 * Gp}", f"isc:{2 * Gp + 1}"]
                            if Gp < 16 else ["isc:32"],
                            writes=[f"wib:{par}"], dma=True)
                        for j, fc in enumerate(fcs):
                            pp = 4 + pin_ctr[0] % 2
                            pin_ctr[0] += 1
                            if Gp < 16:
                                W = 256
                                base = par * 8192 + (j // 2) * 4096
                                coff = (j % 2) * 128
                                mrows = 128
                            else:
                                W = 192
                                base = par * 8192
                                coff = j * 128
                                mrows = 64 if fc == 65 else 128

                            def imm(e, base=base, W=W, coff=coff, mrows=mrows, pp=pp):
                                for dc in range(16):
                                    ins = e.matmul(
                                        PS[pp][0:mrows, 0:TM],
                                        wib[:, base + dc * W + coff:
                                            base + dc * W + coff + mrows],
                                        xln[:, dc * TM:(dc + 1) * TM],
                                        start=(dc == 0), stop=(dc == 15))
                                return ins
                            sch.op("tensor", imm, reads=[f"wib:{par}", "xln"],
                                   writes=[psk[pp]])

                            if fc < 32:
                                sch.op("scalar", lambda e, fc=fc, pp=pp: e.activation(
                                    sg[:, fc * TM:(fc + 1) * TM], PS[pp][:, 0:TM],
                                    AF.Silu),
                                    reads=[psk[pp]], writes=[f"sg:{fc}"])
                            elif fc < 65:
                                cc = fc - 32
                                cpar = cc % 2
                                tape = "vector"
                                cb0 = cpar * (TM + 3)
                                sch.op("gpsimd", lambda e, cc=cc, cb0=cb0: e.tensor_copy(
                                    cin[:, cb0:cb0 + 3], halo[:, cc * 3:cc * 3 + 3]),
                                    reads=[f"halo:{cc}"], writes=[f"cin:{cpar}"])
                                sch.op("scalar", lambda e, pp=pp, cb0=cb0: e.copy(
                                    cin[:, cb0 + 3:cb0 + 3 + TM], PS[pp][:, 0:TM]),
                                    reads=[psk[pp]], writes=[f"cin:{cpar}"])
                                sch.op("gpsimd", lambda e, cc=cc, cb0=cb0: e.tensor_copy(
                                    halo[:, cc * 3:cc * 3 + 3],
                                    cin[:, cb0 + TM:cb0 + TM + 3]),
                                    reads=[f"cin:{cpar}"], writes=[f"halo:{cc}"])
                                sch.op(tape, lambda e, cc=cc, cpar=cpar, cb0=cb0:
                                       e.tensor_scalar(
                                    cacc[:, cpar * TM:(cpar + 1) * TM],
                                    cin[:, cb0:cb0 + TM],
                                    p_cw[:, cc * 4:cc * 4 + 1], None, op0=ALU.mult),
                                    reads=[f"cin:{cpar}", "p_cw"], writes=[f"cacc:{cpar}"])
                                for k in range(1, 4):
                                    sch.op(tape, lambda e, cc=cc, cpar=cpar, k=k, cb0=cb0:
                                           e.scalar_tensor_tensor(
                                        cacc[:, cpar * TM:(cpar + 1) * TM],
                                        cin[:, cb0 + k:cb0 + k + TM],
                                        p_cw[:, cc * 4 + k:cc * 4 + k + 1],
                                        cacc[:, cpar * TM:(cpar + 1) * TM],
                                        op0=ALU.mult, op1=ALU.add),
                                        reads=[f"cin:{cpar}", "p_cw", f"cacc:{cpar}"],
                                        writes=[f"cacc:{cpar}"])
                                if cc < 32:
                                    sch.op("scalar", lambda e, cc=cc, cpar=cpar:
                                           e.activation(
                                        cx[:, cc * TM:(cc + 1) * TM],
                                        cacc[:, cpar * TM:(cpar + 1) * TM], AF.Silu,
                                        bias=p_cb[:, cc:cc + 1]),
                                        reads=[f"cacc:{cpar}", "p_cb"],
                                        writes=[f"cx:{cc}"])
                                else:
                                    sch.op("scalar", lambda e, cpar=cpar: e.activation(
                                        cB[:, 0:TM],
                                        cacc[0:64, cpar * TM:cpar * TM + TM], AF.Silu,
                                        bias=p_cb[0:64, 32:33]),
                                        reads=[f"cacc:{cpar}", "p_cb"], writes=["cB"])
                                    sch.op("scalar", lambda e, cpar=cpar: e.activation(
                                        cC[:, 0:TM],
                                        cacc[64:128, cpar * TM:cpar * TM + TM], AF.Silu,
                                        bias=p_cb[64:128, 32:33]),
                                        reads=[f"cacc:{cpar}", "p_cb"], writes=["cC"])
                            else:
                                sch.op("vector", lambda e, pp=pp: e.tensor_copy(
                                    dtf[:, 0:TM], PS[pp][0:64, 0:TM]),
                                    reads=[psk[pp]], writes=["dtf"])

                    for q in range(NQ):
                        qa = q * QS
                        sch.op("tensor", lambda e, qa=qa: e.transpose(
                            PS[6][:, 0:64], dtf[:, qa:qa + QS], I_f[0:64, 0:64]),
                            reads=["dtf", "masks"], writes=["ps6"])
                        sch.op("vector", lambda e: e.tensor_copy(sp1[:], PS[6][:, 0:64]),
                               reads=["ps6"], writes=["sp1"])
                        sch.op("gpsimd", lambda e: e.tensor_add(sp1[:], sp1[:], p_dtbbc[:]),
                               reads=["p_dtbbc", "sp1"], writes=["sp1"])
                        sch.op("scalar", lambda e: e.activation(sp2[:], sp1[:], AF.Abs),
                               reads=["sp1"], writes=["sp2"])
                        sch.op("scalar", lambda e: e.activation(sp2[:], sp2[:], AF.Exp,
                                                                scale=-1.0),
                               reads=["sp2", "cst"], writes=["sp2"])
                        sch.op("scalar", lambda e: e.activation(sp2[:], sp2[:], AF.Ln,
                                                                bias=1.0),
                               reads=["sp2", "cst"], writes=["sp2"])
                        sch.op("gpsimd", lambda e: e.tensor_scalar_max(dtp[:], sp1[:], 0.0),
                               reads=["sp1"], writes=["dtp"])
                        sch.op("gpsimd", lambda e: e.tensor_add(dtp[:], dtp[:], sp2[:]),
                               reads=["sp2", "dtp"], writes=["dtp"])
                        sch.op("gpsimd", lambda e: e.tensor_mul(a_tm[:], dtp[:], p_Abc[:]),
                               reads=["dtp", "p_Abc"], writes=["a_tm"])
                        sch.op("tensor", lambda e: e.matmul(
                            PS[0][:, 0:64], LT[:], a_tm[:], start=True, stop=True),
                            reads=["a_tm", "masks"], writes=["ps0"])
                        sch.op("tensor", lambda e: e.matmul(
                            PS[1][0:1, 0:64], ones_col_f[:], a_tm[:], start=True, stop=True),
                            reads=["a_tm", "masks"], writes=["ps1"])
                        sch.op("vector", lambda e: e.tensor_copy(c_sb[:], PS[0][:, 0:64]),
                               reads=["ps0"], writes=["c_sb"])
                        sch.op("vector", lambda e: e.tensor_copy(cq_sb[:], PS[1][0:1, 0:64]),
                               reads=["ps1"], writes=["cq_sb"])
                        sch.op("tensor", lambda e: e.matmul(
                            PS[2][:, 0:64], ones_row_f[:], cq_sb[:], start=True, stop=True),
                            reads=["cq_sb", "masks"], writes=["ps2"])
                        sch.op("tensor", lambda e: e.matmul(
                            PS[3][0:64, 0:64], ones_row_f[:, 0:64], cq_sb[:],
                            start=True, stop=True),
                            reads=["cq_sb", "masks"], writes=["ps3"])
                        sch.op("vector", lambda e: e.tensor_sub(
                            wdec[:], PS[2][:, 0:64], c_sb[:]),
                            reads=["ps2", "c_sb"], writes=["wdec"])
                        sch.op("scalar", lambda e: e.activation(wdec[:], wdec[:], AF.Exp),
                               reads=["wdec"], writes=["wdec"])
                        sch.op("scalar", lambda e: e.activation(ec_sb[:], c_sb[:], AF.Exp),
                               reads=["c_sb"], writes=["ec_sb"])
                        sch.op("scalar", lambda e: e.activation(
                            Ebc[:], PS[3][0:64, 0:64], AF.Exp),
                            reads=["ps3"], writes=["Ebc"])
                        sch.op("tensor", lambda e, qa=qa: e.matmul(
                            PS[4][:, 0:128], cB[:, qa:qa + QS], cC[:, qa:qa + QS],
                            start=True, stop=True),
                            reads=["cB", "cC"], writes=["ps4"])
                        sch.op("vector", lambda e: e.tensor_mul(
                            Gt_sb[:], PS[4][:, 0:128], LT[:]),
                            reads=["ps4", "masks"], writes=["Gt_sb"])
                        sch.op("vector", lambda e: e.tensor_mul(
                            wdt[:], wdec[:], dtp[:]),
                            reads=["wdec", "dtp"], writes=["wdt"])
                        sch.op("tensor", lambda e, qa=qa: e.transpose(
                            PSB[7][:, 0:64], cB[:, qa:qa + QS], I_b[0:64, 0:64]),
                            reads=["cB", "masks"], writes=["ps7"])
                        sch.op("vector", lambda e: e.tensor_copy(BCt[:], PSB[7][:, 0:64]),
                               reads=["ps7"], writes=["BCt"])
                        for rnd in range(4):
                            def xtr(e, rnd=rnd, qa=qa):
                                for j in range(8):
                                    fc = rnd * 8 + j
                                    ins = e.transpose(
                                        PSB[7][:, j * 128:(j + 1) * 128],
                                        cx[:, fc * TM + qa:fc * TM + qa + QS], I_b[:])
                                return ins
                            sch.op("tensor", xtr,
                                   reads=[f"cx:{rnd * 8 + j}" for j in range(8)] + ["masks"],
                                   writes=["ps7"])
                            sch.op("vector", lambda e, rnd=rnd: e.tensor_copy(
                                xst[:, rnd * 1024:(rnd + 1) * 1024], PSB[7][:]),
                                reads=["ps7"], writes=["xst"])

                        for hq in range(16):
                            h0 = hq * 4
                            pq = hq % 2
                            aq = pq * 512
                            bq = pq * 256

                            sch.op("gpsimd", lambda e, h0=h0, aq=aq: e.tensor_tensor(
                                out=ATq[:, aq:aq + 512]
                                    .rearrange("p (h t) -> p h t", h=4),
                                in0=rep4(LT[:], 128),
                                in1=col4(a_tm[:, h0:h0 + 4], 128), op=ALU.mult),
                                reads=["a_tm", "masks"], writes=[f"ATq:{pq}"])

                            def margmm(e, aq=aq, pq=pq):
                                for k in range(4):
                                    ins = e.matmul(
                                        PS[pq][:, k * 128:(k + 1) * 128], US_b[:],
                                        ATq[:, aq + k * 128:aq + (k + 1) * 128],
                                        start=True, stop=True)
                                return ins
                            sch.op("tensor", margmm, reads=[f"ATq:{pq}", "masks"],
                                   writes=[psk[pq]])
                            sch.op("scalar", lambda e, aq=aq, pq=pq: e.activation(
                                expmq[:, aq:aq + 512], PS[pq][:, 0:512], AF.Exp),
                                reads=[psk[pq]], writes=[f"expm:{pq}"])
                            for k in range(4):
                                sch.op("vector", lambda e, aq=aq, k=k, h0=h0:
                                       e.scalar_tensor_tensor(
                                    GmTq[:, aq + k * 128:aq + (k + 1) * 128],
                                    expmq[:, aq + k * 128:aq + (k + 1) * 128],
                                    dtp[:, h0 + k:h0 + k + 1], Gt_sb[:],
                                    op0=ALU.mult, op1=ALU.mult),
                                    reads=[f"expm:{pq}", "dtp", "Gt_sb"],
                                    writes=[f"GmT:{pq}"])
                            sch.op("gpsimd", lambda e, h0=h0, bq=bq: e.tensor_tensor(
                                out=Bdq[:, bq:bq + 256]
                                    .rearrange("p (h n) -> p h n", h=4),
                                in0=rep4(BCt[:, 0:64], 64),
                                in1=col4(wdt[:, h0:h0 + 4], 64), op=ALU.mult),
                                reads=["BCt", "wdt"], writes=[f"Bdq:{pq}"])

                            def y1mm(e, aq=aq, pq=pq, h0=h0):
                                for k in range(4):
                                    ins = e.matmul(
                                        PS[2 + pq][:, k * 64:(k + 1) * 64],
                                        GmTq[:, aq + k * 128:aq + (k + 1) * 128],
                                        xst[:, (h0 + k) * 64:(h0 + k + 1) * 64],
                                        start=True, stop=True)
                                return ins
                            sch.op("tensor", y1mm, reads=[f"GmT:{pq}", "xst"],
                                   writes=[psk[2 + pq]])

                            def dsmm(e, bq=bq, h0=h0):
                                for k in range(4):
                                    ins = e.matmul(
                                        PS[5][0:64, k * 64:(k + 1) * 64],
                                        Bdq[:, bq + k * 64:bq + (k + 1) * 64],
                                        xst[:, (h0 + k) * 64:(h0 + k + 1) * 64],
                                        start=True, stop=True)
                                return ins
                            sch.op("tensor", dsmm, reads=[f"Bdq:{pq}", "xst"],
                                   writes=["ps5"])

                            for k2 in range(2):
                                bank = 4 if k2 == 0 else 7

                                def y2mm(e, k2=k2, bank=bank, h0=h0, qa=qa):
                                    for j in range(2):
                                        h = h0 + 2 * k2 + j
                                        ins = e.matmul(
                                            PS[bank][:, j * 64:(j + 1) * 64],
                                            cC[:, qa:qa + QS],
                                            Sb[:, h * 64:(h + 1) * 64],
                                            start=True, stop=True)
                                    return ins
                                sch.op("tensor", y2mm, reads=["cC", f"Sb:{hq}"],
                                       writes=[psk[bank]])

                            sch.op("vector", lambda e, bq=bq, h0=h0: e.tensor_tensor(
                                out=y2b[:, bq:bq + 256]
                                    .rearrange("p (h d) -> p h d", h=4),
                                in0=xst[:, h0 * 64:(h0 + 4) * 64]
                                    .rearrange("p (h d) -> p h d", h=4),
                                in1=col4(p_Dbc[:, h0:h0 + 4], 64), op=ALU.mult),
                                reads=["xst", "p_Dbc"], writes=[f"y2b:{pq}"])
                            sch.op("vector", lambda e, bq=bq, pq=pq: e.tensor_add(
                                y2b[:, bq:bq + 256], y2b[:, bq:bq + 256],
                                PS[2 + pq][:, 0:256]),
                                reads=[psk[2 + pq], f"y2b:{pq}"], writes=[f"y2b:{pq}"])
                            for k in range(4):
                                bank = 4 if k < 2 else 7
                                col = (k % 2) * 64
                                sch.op("vector", lambda e, bq=bq, k=k, h0=h0,
                                       bank=bank, col=col: e.scalar_tensor_tensor(
                                    y2b[:, bq + k * 64:bq + (k + 1) * 64],
                                    PS[bank][:, col:col + 64],
                                    ec_sb[:, h0 + k:h0 + k + 1],
                                    y2b[:, bq + k * 64:bq + (k + 1) * 64],
                                    op0=ALU.mult, op1=ALU.add),
                                    reads=[psk[bank], "ec_sb", f"y2b:{pq}"],
                                    writes=[f"y2b:{pq}"])
                            sch.op("vector", lambda e, h0=h0: e.tensor_tensor(
                                out=tmpS[:].rearrange("p (h d) -> p h d", h=4),
                                in0=Sf[:, h0 * 64:(h0 + 4) * 64]
                                    .rearrange("p (h d) -> p h d", h=4),
                                in1=col4(Ebc[:, h0:h0 + 4], 64), op=ALU.mult),
                                reads=[f"Sf:{hq}", "Ebc"], writes=["tmpS"])
                            sch.op("vector", lambda e, h0=h0: e.tensor_add(
                                Sf[:, h0 * 64:(h0 + 4) * 64], tmpS[:],
                                PS[5][0:64, 0:256]),
                                reads=["tmpS", "ps5"], writes=[f"Sf:{hq}"])
                            sch.op("vector", lambda e, h0=h0: e.tensor_copy(
                                Sb[:, h0 * 64:(h0 + 4) * 64],
                                Sf[:, h0 * 64:(h0 + 4) * 64]),
                                reads=[f"Sf:{hq}"], writes=[f"Sb:{hq}"])
                            sch.op("gpsimd", lambda e, bq=bq: e.tensor_mul(
                                ysqq[:], y2b[:, bq:bq + 256], y2b[:, bq:bq + 256]),
                                reads=[f"y2b:{pq}"], writes=["ysqq"])
                            sch.op("vector", lambda e, hq=hq: e.tensor_reduce(
                                ssqp[:, hq:hq + 1], ysqq[:],
                                mybir.AxisListType.X, ALU.add),
                                reads=["ysqq"], writes=["ssqp"])

                            def prtr(e, bq=bq):
                                for j2 in range(2):
                                    ins = e.transpose(
                                        PS[6][:, j2 * 128:(j2 + 1) * 128],
                                        y2b[:, bq + j2 * 128:bq + (j2 + 1) * 128],
                                        I_f[:])
                                return ins
                            sch.op("tensor", prtr, reads=[f"y2b:{pq}", "masks"],
                                   writes=["ps6"])
                            for j2 in range(2):
                                pr = hq * 2 + j2
                                sch.op("vector", lambda e, pr=pr, j2=j2, qa=qa:
                                       e.scalar_tensor_tensor(
                                    cx[:, pr * TM + qa:pr * TM + qa + QS],
                                    PS[6][:, j2 * 128:(j2 + 1) * 128],
                                    p_nw[:, pr:pr + 1],
                                    sg[:, pr * TM + qa:pr * TM + qa + QS],
                                    op0=ALU.mult, op1=ALU.mult),
                                    reads=["ps6", "p_nw", f"sg:{pr}"],
                                    writes=[f"cx:{pr}"])
                        sch.op("vector", lambda e, q=q: e.tensor_reduce(
                            ssq_tm[:, q:q + 1], ssqp[:], mybir.AxisListType.X, ALU.add),
                            reads=["ssqp"], writes=["ssq_tm"])

                    sch.op("gpsimd", lambda e: e.tensor_scalar(
                        ssq_tm[:], ssq_tm[:], 1.0 / INTER, None, op0=ALU.mult),
                        reads=["cst", "ssq_tm"], writes=["ssq_tm"])
                    sch.op("scalar", lambda e: e.activation(
                        ssq_tm[:], ssq_tm[:], AF.Ln, bias=RMS_EPS),
                        reads=["ssq_tm", "cst"], writes=["ssq_tm"])
                    sch.op("scalar", lambda e: e.activation(
                        ssq_tm[:], ssq_tm[:], AF.Exp, scale=-0.5),
                        reads=["ssq_tm", "cst"], writes=["ssq_tm"])

                    def rtr(e):
                        for q in range(NQ):
                            ins = e.transpose(PS[6][0:1, q * 128:(q + 1) * 128],
                                              ssq_tm[:, q:q + 1], I_f[:])
                        return ins
                    sch.op("tensor", rtr, reads=["ssq_tm", "masks"], writes=["ps6"])
                    sch.op("vector", lambda e: e.tensor_copy(
                        rstd_row[0:1, 0:TM], PS[6][0:1, 0:TM]),
                        reads=["ps6"], writes=["rstd_row"])
                    sch.op("tensor", lambda e: e.matmul(
                        PS[6][:, 0:TM], ones_row_f[:], rstd_row[0:1, 0:TM],
                        start=True, stop=True),
                        reads=["rstd_row", "masks"], writes=["ps6"])
                    sch.op("vector", lambda e: e.tensor_copy(rstd_bc[:], PS[6][:, 0:TM]),
                           reads=["ps6"], writes=["rstd_bc"])

                    for og in range(8):
                        par = og % 2
                        sch.op("sync", lambda e, og=og, par=par: e.dma_start(
                            wob[:, par * 8192:(par + 1) * 8192],
                            osc.ap()[:, og * 8192:(og + 1) * 8192]),
                            reads=[f"osc:{og}:0", f"osc:{og}:1"],
                            writes=[f"wob:{par}"], dma=True)

                        def omm(e, og=og, par=par):
                            for il in range(4):
                                for oc in range(16):
                                    ins = e.matmul(
                                        PS[oc // 2][0:128, (oc % 2) * 256:
                                                    (oc % 2) * 256 + 256],
                                        wob[:, par * 8192 + il * 2048 + oc * 128:
                                            par * 8192 + il * 2048 + oc * 128 + 128],
                                        cx[:, (og * 4 + il) * TM:(og * 4 + il + 1) * TM],
                                        start=(og == 0 and il == 0 and oc % 2 == 0),
                                        stop=(og == 7 and il == 3 and oc % 2 == 1),
                                        skip_group_check=True)
                            return ins
                        sch.op("tensor", omm,
                               reads=[f"wob:{par}"] +
                                     [f"cx:{og * 4 + i}" for i in range(4)],
                               writes=psk)
                    for oc in range(16):
                        sch.op("vector", lambda e, oc=oc: e.tensor_mul(
                            oacc[:], PS[oc // 2][:, (oc % 2) * 256:(oc % 2) * 256 + 256],
                            rstd_bc[:]),
                            reads=[psk[oc // 2], "rstd_bc"], writes=["oacc"])
                        sch.op("vector", lambda e, oc=oc: e.tensor_add(
                            resid[:, oc * TM:(oc + 1) * TM],
                            resid[:, oc * TM:(oc + 1) * TM], oacc[:]),
                            reads=["oacc"], writes=["resid"])

                    sch.op("scalar", lambda e, t0=t0: e.dma_start(
                        dst.ap()[:, t0:t0 + TM].rearrange("(c p) t -> p c t", p=128),
                        resid.ap().rearrange("p (c t) -> p c t", t=TM)),
                        reads=["resid"], dma=True)

        if "ffn1" in phases:
            emit_ffn(xT, X1, do_conversion=True)
            sch.barrier()
        if "mamba" in phases:
            emit_mamba(X1, X2)
            sch.barrier()
        if "ffn2" in phases:
            emit_ffn(X2, outT, do_conversion="ffn1" not in phases)

        counts, ctr, dctr = sch.emit()
        print("op counts:", counts, "incs:", ctr, "dma:", dctr)
    return nc


_NC_CACHE = {}


def kernel(**inputs):
    x = np.ascontiguousarray(inputs["L0_attn_hidden_states"], dtype=np.float32)
    key = "full"
    if key not in _NC_CACHE:
        _NC_CACHE[key] = build_nc()
    nc = _NC_CACHE[key]
    base = {
        "wgT": np.ascontiguousarray(np.asarray(inputs["L0_ffn_w_gate"], np.float32).T),
        "wuT": np.ascontiguousarray(np.asarray(inputs["L0_ffn_w_up"], np.float32).T),
        "wdT": np.ascontiguousarray(np.asarray(inputs["L0_ffn_w_down"], np.float32).T),
        "winT": np.ascontiguousarray(np.asarray(inputs["L0_attn_in_proj_w"], np.float32).T),
        "woT": np.ascontiguousarray(np.asarray(inputs["L0_attn_out_proj_w"], np.float32).T),
        "cw": np.ascontiguousarray(inputs["L0_attn_conv_w"], np.float32),
        "cb": np.ascontiguousarray(inputs["L0_attn_conv_b"], np.float32),
        "dtb": np.ascontiguousarray(inputs["L0_attn_dt_bias"], np.float32),
        "alog": np.ascontiguousarray(inputs["L0_attn_A_log"], np.float32),
        "dvec": np.ascontiguousarray(inputs["L0_attn_D"], np.float32),
        "nw": np.ascontiguousarray(inputs["L0_attn_norm_weight"], np.float32),
    }
    in_maps = []
    for c in range(NCORES):
        m = dict(base)
        m["xT"] = np.ascontiguousarray(x[c].T)
        in_maps.append(m)
    res = run_bass_kernel_spmd(nc, in_maps, core_ids=list(range(NCORES)))
    out = np.stack([np.ascontiguousarray(np.asarray(r["outT"]).T)
                    for r in res.results], axis=0)
    return np.ascontiguousarray(out.astype(np.float32, copy=False))


if __name__ == "__main__":
    import time
    t = time.time()
    nc = build_nc()
    print("built in %.1fs" % (time.time() - t))
